# version 05
# baseline (speedup 1.0000x reference)
import numpy as np
import ml_dtypes
import concourse.bass as bass
import concourse.mybir as mybir
from concourse.bass_utils import run_bass_kernel_spmd
from concourse.alu_op_type import AluOpType as OP
from contextlib import ExitStack

F32 = mybir.dt.float32
BF16 = mybir.dt.bfloat16
I32 = mybir.dt.int32
AF = mybir.ActivationFunctionType
AX = mybir.AxisListType
BF = ml_dtypes.bfloat16

NC = 8
S = 8192
D = 2048
TC = S // NC
NCH = D // 128

LN_EPS = 1e-5
L0_HEADS = 16
L0_DH = 128
L0_SCALE = 1.0 / np.sqrt(L0_DH)
ALIBI_MAX = 8.0
L1_SCALE = float(np.asarray(D, np.float32) ** (-0.5))
B0 = 24.0

ENGINES = ("sync", "scalar", "vector", "tensor", "gpsimd")


class KB:

    def __init__(self, name="k"):
        self.nc = bass.Bass("TRN2", target_bir_lowering=False)
        self.st = ExitStack()
        self.streams = {e: [] for e in ENGINES}
        self._sems = {}
        self.semv = {}

    def sem(self, name):
        if name not in self._sems:
            self._sems[name] = self.st.enter_context(self.nc.semaphore(name))
            self.semv[name] = 0
        return self._sems[name]

    def sb(self, name, shape, dt):
        name = getattr(self, "_prefix", "") + name
        st = getattr(self, "_phase_st", None) or self.st
        return st.enter_context(self.nc.sbuf_tensor(name, shape, dt))

    def ps(self, name, shape, dt=F32):
        if not hasattr(self, "_ps_cache"):
            self._ps_cache = {}
        if name not in self._ps_cache:
            self._ps_cache[name] = self.st.enter_context(self.nc.psum_tensor(name, shape, dt))
        return self._ps_cache[name]

    def input(self, name, shape, dt):
        io = getattr(self, "io", None)
        if io is not None and name in io:
            return io[name]
        return self.nc.dram_tensor(name, shape, dt, kind="ExternalInput").ap()

    def output(self, name, shape, dt):
        io = getattr(self, "io", None)
        if io is not None and name in io:
            return io[name]
        return self.nc.dram_tensor(name, shape, dt, kind="ExternalOutput").ap()

    def begin_phase(self, prefix, io):
        self._prefix = prefix
        self.io = io
        self._phase_st = ExitStack()
        self._arena_mark = len(getattr(self, "_arenas", []))

    def end_phase(self):
        self.full_barrier()
        for a in reversed(self._arenas[self._arena_mark:] if hasattr(self, "_arenas") else []):
            a.free()
        if hasattr(self, "_arenas"):
            del self._arenas[self._arena_mark:]
        self._phase_st.close()
        self._phase_st = None
        self._prefix = ""
        self.io = None

    def full_barrier(self):
        for eng in ENGINES:
            for s, v in list(self.semv.items()):
                if v > 0:
                    self.wait(eng, s, v)

    def dram(self, name, shape, dt, kind="Internal"):
        return self.nc.dram_tensor(name, shape, dt, kind=kind).ap()

    def arena(self):
        from concourse.bass import AutoArena
        a = AutoArena(self.nc)
        if not hasattr(self, "_arenas"):
            self._arenas = []
        self._arenas.append(a)
        return a

    def barrier(self, sems=("pe", "dv", "ac", "po")):
        for eng in ENGINES:
            for s in sems:
                self.wait(eng, s, self.val(s))

    def wait(self, eng, name, val):
        if val is None or val <= 0:
            return
        s = self.sem(name)
        self.streams[eng].append(lambda e, s=s, v=val: e.wait_ge(s, v))

    DEFSEM = {"vector": "dv", "scalar": "ac", "gpsimd": "po"}

    def op(self, eng, fn, sem=None, amt=1):
        d = self.DEFSEM.get(eng)
        if d is not None:
            if self.semv.get(d, 0) > 0:
                self.wait(eng, d, self.semv[d])
            if sem is None:
                sem, amt = d, 1
        if sem is None:
            self.streams[eng].append(fn)
            return None
        s = self.sem(sem)
        self.semv[sem] += amt
        v = self.semv[sem]

        def run(e, f=fn, s=s, a=amt):
            f(e).then_inc(s, a)

        self.streams[eng].append(run)
        return v

    def dma(self, q, out, in_, sem="ld", **kw):
        return self.op(q, lambda e, o=out, i=in_, k=kw: e.dma_start(out=o, in_=i, **k),
                       sem, 16)

    def val(self, name):
        return self.semv.get(name, 0)

    def finalize(self):
        block = self.st.enter_context(self.nc.Block())
        for eng in ENGINES:
            stream = self.streams[eng]
            if not stream:
                continue
            dec = getattr(block, eng)

            def body(e, st_=stream):
                for f in st_:
                    f(e)

            dec(body)
        for a in reversed(getattr(self, "_arenas", [])):
            a.free()
        self.st.close()
        return self.nc



def emit_const_tiles(kb):
    ones_row = kb.sb("c_ones_row", [2, 512], BF16)
    ones_col = kb.sb("c_ones_col", [128, 1], BF16)
    kb.op("vector", lambda e: e.memset(ones_row[:], 1.0), "dv")
    kb.op("vector", lambda e: e.memset(ones_col[:], 1.0), "dv")
    return ones_row, ones_col


def emit_ln_fm(kb, x_slices, t_w, name, sc, out_cb=None):
    nhalf = (t_w + 511) // 512
    pe_done = None
    for ch in range(16):
        xs = x_slices[ch]
        v = kb.op("vector", lambda e, ch=ch, xs=xs: e.tensor_copy(sc["xbf"][:], xs()), "dv")
        a = kb.op("scalar", lambda e, ch=ch: e.square(sc["xsq"][:], sc["xbf"][:]), "ac")
        kb.wait("tensor", "dv", v)
        kb.wait("tensor", "ac", a)
        for hf in range(nhalf):
            w = min(512, t_w - hf * 512)
            st = (ch == 0)
            sp = (ch == 15)
            kb.op("tensor", lambda e, ch=ch, hf=hf, w=w, st=st, sp=sp: e.matmul(
                sc["ps_sum"][0:1, hf * 512:hf * 512 + w],
                sc["onescol_bf"][:], sc["xbf"][:, hf * 512:hf * 512 + w],
                start=st, stop=sp), "pe")
            pe_done = kb.op("tensor", lambda e, ch=ch, hf=hf, w=w, st=st, sp=sp: e.matmul(
                sc["ps_sq"][0:1, hf * 512:hf * 512 + w],
                sc["onescol_bf"][:], sc["xsq"][:, hf * 512:hf * 512 + w],
                start=st, stop=sp), "pe")
    kb.wait("vector", "pe", pe_done)
    kb.op("vector", lambda e: e.tensor_scalar(sc["mean"][0:1, 0:t_w], sc["ps_sum"][0:1, 0:t_w],
                                              1.0 / D, None, op0=OP.mult), "dv")
    kb.op("vector", lambda e: e.tensor_scalar(sc["msq"][0:1, 0:t_w], sc["ps_sq"][0:1, 0:t_w],
                                              1.0 / D, None, op0=OP.mult), "dv")
    kb.op("vector", lambda e: e.tensor_tensor(sc["tmp"][0:1, 0:t_w], sc["mean"][0:1, 0:t_w],
                                              sc["mean"][0:1, 0:t_w], op=OP.mult), "dv")
    dvv = kb.op("vector", lambda e: e.tensor_tensor(sc["var"][0:1, 0:t_w], sc["msq"][0:1, 0:t_w],
                                                    sc["tmp"][0:1, 0:t_w], op=OP.subtract), "dv")
    kb.wait("scalar", "dv", dvv)
    acv = kb.op("scalar", lambda e: e.activation(sc["std"][0:1, 0:t_w], sc["var"][0:1, 0:t_w],
                                                 AF.Sqrt, bias=LN_EPS, scale=1.0), "ac")
    kb.wait("vector", "ac", acv)
    kb.op("vector", lambda e: e.reciprocal(sc["a_row"][0:1, 0:t_w], sc["std"][0:1, 0:t_w]), "dv")
    dvv = kb.op("vector", lambda e: e.tensor_tensor(sc["b_row"][0:1, 0:t_w], sc["mean"][0:1, 0:t_w],
                                                    sc["a_row"][0:1, 0:t_w], op=OP.mult), "dv")
    dvv = kb.op("vector", lambda e: e.tensor_scalar(sc["b_row"][0:1, 0:t_w], sc["b_row"][0:1, 0:t_w],
                                                    -1.0, None, op0=OP.mult), "dv")
    kb.op("vector", lambda e: e.tensor_copy(sc["a_row_bf"][0:1, 0:t_w], sc["a_row"][0:1, 0:t_w]), "dv")
    dvv = kb.op("vector", lambda e: e.tensor_copy(sc["b_row_bf"][0:1, 0:t_w], sc["b_row"][0:1, 0:t_w]), "dv")
    kb.wait("tensor", "dv", dvv)
    pev = None
    for hf in range(nhalf):
        w = min(512, t_w - hf * 512)
        kb.op("tensor", lambda e, hf=hf, w=w: e.matmul(
            sc["ps_a"][:, hf * 512:hf * 512 + w], sc["onesrow_bf"][0:1, 0:128],
            sc["a_row_bf"][0:1, hf * 512:hf * 512 + w], start=True, stop=True), "pe")
        pev = kb.op("tensor", lambda e, hf=hf, w=w: e.matmul(
            sc["ps_b"][:, hf * 512:hf * 512 + w], sc["onesrow_bf"][0:1, 0:128],
            sc["b_row_bf"][0:1, hf * 512:hf * 512 + w], start=True, stop=True), "pe")
    kb.wait("vector", "pe", pev)
    last = None
    for ch in range(16):
        xs = x_slices[ch]
        kb.op("vector", lambda e, ch=ch, xs=xs: e.tensor_tensor(
            sc["tmp128"][:, 0:t_w], xs(), sc["ps_a"][:, 0:t_w], op=OP.mult), "dv")
        if out_cb is None:
            last = kb.op("vector", lambda e, ch=ch, xs=xs: e.tensor_tensor(
                xs(), sc["tmp128"][:, 0:t_w], sc["ps_b"][:, 0:t_w], op=OP.add), "dv")
        else:
            last = out_cb(ch)
    return last


def emit_ln_scratch(kb, t_w):
    sc = {}
    sc["xbf"] = kb.sb("ln_xbf", [128, t_w], BF16)
    sc["xsq"] = kb.sb("ln_xsq", [128, t_w], BF16)
    sc["onescol_bf"] = kb.sb("ln_ones_col", [128, 1], BF16)
    sc["onesrow_bf"] = kb.sb("ln_ones_row", [1, 128], BF16)
    kb.op("vector", lambda e: e.memset(sc["onescol_bf"][:], 1.0), "dv")
    kb.op("vector", lambda e: e.memset(sc["onesrow_bf"][:], 1.0), "dv")
    for nm in ("mean", "msq", "tmp", "var", "std", "a_row", "b_row"):
        sc[nm] = kb.sb("ln_" + nm, [1, t_w], F32)
    sc["a_row_bf"] = kb.sb("ln_a_bf", [1, t_w], BF16)
    sc["b_row_bf"] = kb.sb("ln_b_bf", [1, t_w], BF16)
    sc["tmp128"] = kb.sb("ln_t128", [128, t_w], F32)
    sc["ps_sum"] = kb.ps("ln_ps_sum", [1, t_w], F32)
    sc["ps_sq"] = kb.ps("ln_ps_sq", [1, t_w], F32)
    sc["ps_a"] = kb.ps("ln_ps_a", [128, t_w], F32)
    sc["ps_b"] = kb.ps("ln_ps_b", [128, t_w], F32)
    return sc



def build_l1_qkv(kb=None):
    standalone = kb is None
    if standalone:
        kb = KB()
    nc = kb.nc
    x = kb.input("x", [D, TC], F32)
    Wq = kb.input("Wq", [D, D], F32)
    Wk = kb.input("Wk", [D, D], F32)
    Wv = kb.input("Wv", [D, D], F32)
    bqT = kb.input("bqT", [128, 16], F32)
    bkT = kb.input("bkT", [128, 16], F32)
    bv = kb.input("bv", [1, D], F32)
    q1 = kb.output("q1", [D, TC], BF16)
    k1 = kb.output("k1", [D, TC], BF16)
    v1p = kb.output("v1p", [TC, D + 1], BF16)

    x_bf = kb.sb("x_bf", [128, 16 * TC], BF16)
    wtile = [kb.sb(f"wt{i}", [128, 16 * 512], BF16) for i in range(2)]
    bq_sb = kb.sb("bq_sb", [128, 16], F32)
    bk_sb = kb.sb("bk_sb", [128, 16], F32)
    bv_bf = kb.sb("bv_bf", [1, D], BF16)
    bv_f32 = kb.sb("bv_f32", [1, D], F32)
    onesrow = kb.sb("onesrow", [1, 128], BF16)
    q_all = kb.sb("q_all", [128, 16 * TC], BF16)
    k_all = kb.sb("k_all", [128, 16 * TC], BF16)
    v_all = kb.sb("v_all", [128, 8 * (D + 1)], BF16)
    ps = [kb.ps(f"pool{i}", [128, 512], F32) for i in range(4)]

    xg = kb.dma("gpsimd", x_bf.ap().rearrange("p (c t) -> p c t", c=16),
                x.rearrange("(c p) t -> p c t", p=128), sem="wgx")
    kb.dma("sync", bq_sb[:], bqT, sem="ld")
    kb.dma("sync", bk_sb[:], bkT, sem="ld")
    kb.dma("sync", bv_f32[:], bv, sem="ld")
    ldv = kb.val("ld")
    kb.wait("vector", "ld", ldv)
    kb.wait("vector", "wgx", xg)
    kb.wait("tensor", "wgx", xg)
    kb.op("vector", lambda e: e.memset(onesrow[:], 1.0), "dv")
    kb.op("vector", lambda e: e.tensor_copy(bv_bf[:], bv_f32[:]), "dv")
    vav = kb.op("vector", lambda e: e.memset(
        v_all.ap().rearrange("p (t d) -> p t d", t=8)[:, :, D:D + 1], 1.0), "dv")
    dvv = kb.op("vector", lambda e: e.tensor_scalar(bq_sb[:], bq_sb[:], L1_SCALE, None,
                                                    op0=OP.mult), "dv")
    kb.wait("scalar", "dv", dvv)
    kb.wait("tensor", "dv", dvv)

    wfree = {0: 0, 1: 0}
    psfree = [0, 0, 0, 0]

    def qk_proj(Wap, bias_sb, out_all, scale):
        for blk in range(4):
            bi = blk % 2
            kb.wait("gpsimd", "pe", wfree[bi])
            wld = kb.dma("gpsimd",
                         wtile[bi].ap().rearrange("p (c d) -> p c d", c=16),
                         Wap[:, blk * 512:(blk + 1) * 512].rearrange("(c p) d -> p c d", p=128),
                         sem=f"wg{bi}")
            kb.wait("tensor", f"wg{bi}", wld)
            pv = None
            for sub in range(4):
                ch = blk * 4 + sub
                for hf in range(2):
                    pi = (sub * 2 + hf) % 4
                    kb.wait("tensor", "ac", psfree[pi])
                    for kc in range(16):
                        pv = kb.op("tensor", lambda e, bi=bi, sub=sub, hf=hf, kc=kc, pi=pi: e.matmul(
                            ps[pi][:, 0:512],
                            wtile[bi][:, kc * 512 + sub * 128: kc * 512 + sub * 128 + 128],
                            x_bf[:, kc * TC + hf * 512: kc * TC + hf * 512 + 512],
                            start=(kc == 0), stop=(kc == 15)), "pe")
                    kb.wait("scalar", "pe", pv)
                    av = kb.op("scalar", lambda e, ch=ch, hf=hf, pi=pi, bias_sb=bias_sb,
                               scale=scale, out_all=out_all: e.activation(
                        out_all[:, ch * TC + hf * 512: ch * TC + hf * 512 + 512],
                        ps[pi][:, 0:512], AF.Identity,
                        bias=bias_sb[:, ch:ch + 1], scale=scale), "ac")
                    psfree[pi] = av
            wfree[bi] = pv

    qk_proj(Wk, bk_sb, k_all, 1.0)
    kv = kb.val("ac")
    kb.wait("sync", "ac", kv)
    kb.dma("sync", k1.rearrange("(c p) t -> p c t", p=128),
           k_all.ap().rearrange("p (c t) -> p c t", c=16), sem="stk")

    for blk in range(4):
        bi = blk % 2
        kb.wait("gpsimd", "pe", wfree[bi])
        wld = kb.dma("gpsimd",
                     wtile[bi].ap().rearrange("p (c d) -> p c d", c=16),
                     Wv[:, blk * 512:(blk + 1) * 512].rearrange("(c p) d -> p c d", p=128),
                     sem=f"wg{bi}")
        kb.wait("tensor", f"wg{bi}", wld)
        pv = None
        for tt in range(8):
            pi = tt % 4
            kb.wait("tensor", "ac", psfree[pi])
            for kc in range(16):
                kb.op("tensor", lambda e, bi=bi, tt=tt, kc=kc, pi=pi: e.matmul(
                    ps[pi][:, 0:512],
                    x_bf[:, kc * TC + tt * 128: kc * TC + tt * 128 + 128],
                    wtile[bi][:, kc * 512: kc * 512 + 512],
                    start=(kc == 0), stop=False), "pe")
            pv = kb.op("tensor", lambda e, blk=blk, pi=pi: e.matmul(
                ps[pi][:, 0:512], onesrow[0:1, 0:128],
                bv_bf[0:1, blk * 512:(blk + 1) * 512], start=False, stop=True), "pe")
            kb.wait("scalar", "pe", pv)
            av = kb.op("scalar", lambda e, blk=blk, tt=tt, pi=pi: e.activation(
                v_all[:, tt * (D + 1) + blk * 512: tt * (D + 1) + blk * 512 + 512],
                ps[pi][:, 0:512], AF.Identity, bias=0.0, scale=1.0), "ac")
            psfree[pi] = av
        wfree[bi] = pv

    acv = kb.val("ac")
    kb.wait("sync", "ac", acv)
    kb.wait("sync", "dv", vav)
    kb.dma("sync", v1p.rearrange("(t p) d -> p t d", p=128),
           v_all.ap().rearrange("p (t d) -> p t d", t=8), sem="stv")
    qk_proj(Wq, bq_sb, q_all, L1_SCALE)
    acv = kb.val("ac")
    kb.wait("sync", "ac", acv)
    kb.dma("sync", q1.rearrange("(c p) t -> p c t", p=128),
           q_all.ap().rearrange("p (c t) -> p c t", c=16), sem="stq")
    kb.wait("sync", "stk", kb.val("stk"))
    kb.wait("sync", "stv", kb.val("stv"))
    kb.wait("sync", "stq", kb.val("stq"))
    if standalone:
        return kb.finalize()



def build_l1_attn(kb=None):
    standalone = kb is None
    if standalone:
        kb = KB()
    nc = kb.nc
    x = kb.input("x", [D, TC], F32)
    q1 = kb.input("q1", [D, TC], BF16)
    Kg = kb.input("Kg", [NC * D, TC], BF16)
    Vg = kb.input("Vg", [S, D + 1], BF16)
    qbase = kb.input("qbase", [1, 1], F32)
    x3 = kb.output("x3", [D, TC], F32)
    z_dram = kb.dram("z_dram", [D, TC], F32)

    EXPB = -20.0

    q_sb = kb.sb("q_sb", [128, 16 * TC], BF16)
    acc = kb.sb("acc", [128, 8 * (D + 1)], F32)
    iota_row = kb.sb("iota_row", [1, TC], F32)
    qb_sb = kb.sb("qb_sb", [1, 1], F32)
    qpos_row = kb.sb("qpos_row", [1, TC], F32)
    qpos_bf = kb.sb("qpos_bf", [1, TC], BF16)
    qpos_bc = kb.sb("qpos_bc", [128, TC], F32)
    kcols = kb.sb("kcols", [128, 64], F32)
    onesrow = kb.sb("onesrow", [1, 128], BF16)
    rcol = kb.sb("rcol", [128, 1], F32)
    cb = kb.sb("cb", [128, 4], F32)
    attn_bf = kb.sb("attn_bf", [128, D], BF16)
    ln_xbf = kb.sb("ln_xbf", [128, 128], BF16)
    ln_xsq = kb.sb("ln_xsq", [128, 128], BF16)
    ln_onescol = kb.sb("ln_onescol", [128, 1], BF16)
    ln_rows = kb.sb("ln_rows", [1, 2 * TC], F32)
    ln_b = kb.sb("ln_b", [1, TC], F32)
    ln_a_bf = kb.sb("ln_a_bf", [1, TC], BF16)
    ln_b_bf = kb.sb("ln_b_bf", [1, TC], BF16)
    arA = kb.arena()
    kblk = [arA.sbuf_tensor(f"kblk{i}", [128, 16 * 512], BF16) for i in range(2)]
    vblk = [arA.sbuf_tensor(f"vblk{i}", [128, 4 * (D + 1)], BF16) for i in range(2)]
    ptil = arA.sbuf_tensor("ptil", [128, 512], BF16)
    ptil2 = arA.sbuf_tensor("ptil2", [128, 512], BF16)
    msk = arA.sbuf_tensor("msk", [128, 512], BF16)
    arA.allocate()

    pool = [kb.ps(f"pool{i}", [128, 512], F32) for i in range(6)]
    pt = kb.ps("pt", [128, 1024], BF16)
    pe1 = kb.ps("pe1", [128, 16], F32)

    qv = kb.dma("sync", q_sb.ap().rearrange("p (c t) -> p c t", c=16),
                q1.rearrange("(c p) t -> p c t", p=128), sem="ld")
    kb.dma("sync", qb_sb[:], qbase, sem="ld")
    ldv = kb.val("ld")
    kb.op("gpsimd", lambda e: e.iota(iota_row[:], pattern=[[1, TC]], base=0,
                                     channel_multiplier=0,
                                     allow_small_or_imprecise_dtypes=True), "po")
    pov = kb.op("gpsimd", lambda e: e.iota(kcols[:], pattern=[[128, 64]], base=0,
                                           channel_multiplier=1,
                                           allow_small_or_imprecise_dtypes=True), "po")
    kb.wait("vector", "ld", ldv)
    kb.wait("vector", "po", pov)
    kb.op("vector", lambda e: e.memset(onesrow[:], 1.0), "dv")
    kb.op("vector", lambda e: e.memset(ln_onescol[:], 1.0), "dv")
    kb.op("vector", lambda e: e.memset(cb[:, 0:1], EXPB), "dv")
    cbv = kb.op("vector", lambda e: e.memset(cb[:, 1:2], LN_EPS), "dv")
    kb.wait("scalar", "dv", cbv)
    kb.op("vector", lambda e: e.tensor_scalar(qpos_row[:], iota_row[:], qb_sb[0:1, 0:1],
                                              None, op0=OP.add), "dv")
    dvv = kb.op("vector", lambda e: e.tensor_copy(qpos_bf[:], qpos_row[:]), "dv")
    kb.wait("tensor", "dv", dvv)
    pev = None
    for hf in range(2):
        pev = kb.op("tensor", lambda e, hf=hf: e.matmul(
            pool[hf][:, 0:512], onesrow[0:1, 0:128],
            qpos_bf[0:1, hf * 512:hf * 512 + 512], start=True, stop=True), "pe")
    kb.wait("vector", "pe", pev)
    dvv = None
    for hf in range(2):
        dvv = kb.op("vector", lambda e, hf=hf: e.tensor_copy(
            qpos_bc[:, hf * 512:hf * 512 + 512], pool[hf][:, 0:512]), "dv")
    qpos_ready = dvv

    kfree = [0, 0]
    vfree = [0, 0]
    psS_free = [0, 0]
    psO_free = 0
    ptil_free = 0
    msk_free = 0
    kb.wait("vector", "dv", qpos_ready)

    for ks in range(16):
        bi = ks % 2
        r, half = ks // 2, ks % 2
        kb.wait("sync", "pe", kfree[bi])
        kv = kb.dma("sync", kblk[bi].ap().rearrange("p (c t) -> p c t", c=16),
                    Kg[r * D:(r + 1) * D, half * 512:(half + 1) * 512]
                    .rearrange("(c p) t -> p c t", p=128), sem=f"kld{bi}")
        kb.wait("sync", "pe", vfree[bi])
        vv = kb.dma("sync", vblk[bi].ap().rearrange("p (c d) -> p c d", c=4),
                    Vg[ks * 512:(ks + 1) * 512, :].rearrange("(c p) d -> p c d", p=128),
                    sem=f"vld{bi}")
        kb.wait("tensor", f"kld{bi}", kv)
        kb.wait("tensor", f"vld{bi}", vv)
        for qt in range(8):
            si = qt % 2
            psS = pool[4 + si]
            kb.wait("tensor", "ac", psS_free[si])
            spev = None
            for j in range(4):
                for kc in range(16):
                    spev = kb.op("tensor", lambda e, bi=bi, j=j, kc=kc, qt=qt, psS=psS: e.matmul(
                        psS[:, j * 128:j * 128 + 128],
                        kblk[bi][:, kc * 512 + j * 128: kc * 512 + j * 128 + 128],
                        q_sb[:, kc * TC + qt * 128: kc * TC + qt * 128 + 128],
                        start=(kc == 0), stop=(kc == 15)), "pe")
            kb.wait("scalar", "pe", spev)
            kb.wait("scalar", "pe", ptil_free)
            av = kb.op("scalar", lambda e, psS=psS: e.activation(
                ptil[:], psS[:, 0:512], AF.Exp, bias=cb[:, 0:1], scale=1.0), "ac")
            psS_free[si] = av
            kb.wait("vector", "pe", msk_free)
            mv = None
            for j in range(4):
                kc = ks * 4 + j
                mv = kb.op("vector", lambda e, j=j, kc=kc, qt=qt: e.tensor_scalar(
                    msk[:, j * 128:j * 128 + 128],
                    qpos_bc[:, qt * 128:qt * 128 + 128],
                    kcols[:, kc:kc + 1], None, op0=OP.is_ge), "dv")
            kb.wait("vector", "ac", av)
            mav = kb.op("vector", lambda e: e.tensor_tensor(ptil2[:], ptil[:], msk[:],
                                                            op=OP.mult), "dv")
            kb.wait("tensor", "dv", mav)
            kb.wait("tensor", "dv", psO_free)
            ppv = None
            for j in range(4):
                for nb in range(4):
                    ppv = kb.op("tensor", lambda e, bi=bi, j=j, nb=nb: e.matmul(
                        pool[nb][:, 0:512],
                        ptil2[:, j * 128:j * 128 + 128],
                        vblk[bi][:, j * (D + 1) + nb * 512: j * (D + 1) + nb * 512 + 512],
                        start=(j == 0), stop=(j == 3)), "pe")
                ppv = kb.op("tensor", lambda e, bi=bi, j=j: e.matmul(
                    pe1[:, 0:1],
                    ptil2[:, j * 128:j * 128 + 128],
                    vblk[bi][:, j * (D + 1) + D: j * (D + 1) + D + 1],
                    start=(j == 0), stop=(j == 3)), "pe")
            ptil_free = ppv
            msk_free = ppv
            if qt == 7:
                kfree[bi] = ppv
                vfree[bi] = ppv
            kb.wait("vector", "pe", ppv)
            dvv = None
            top = OP.add if ks > 0 else None
            for nb in range(4):
                c0 = qt * (D + 1) + nb * 512
                if ks == 0:
                    dvv = kb.op("vector", lambda e, nb=nb, c0=c0: e.tensor_copy(
                        acc[:, c0:c0 + 512], pool[nb][:, 0:512]), "dv")
                else:
                    dvv = kb.op("vector", lambda e, nb=nb, c0=c0: e.tensor_tensor(
                        acc[:, c0:c0 + 512], pool[nb][:, 0:512],
                        acc[:, c0:c0 + 512], op=OP.add), "dv")
            c1 = qt * (D + 1) + D
            if ks == 0:
                dvv = kb.op("vector", lambda e, c1=c1: e.tensor_copy(
                    acc[:, c1:c1 + 1], pe1[:, 0:1]), "dv")
            else:
                dvv = kb.op("vector", lambda e, c1=c1: e.tensor_tensor(
                    acc[:, c1:c1 + 1], pe1[:, 0:1], acc[:, c1:c1 + 1], op=OP.add), "dv")
            psO_free = dvv

    kb.barrier()
    arA.free()
    arB = kb.arena()
    xq_sb = [arB.sbuf_tensor(f"xq_sb{i}", [128, 16 * 128], F32) for i in range(2)]
    z_stage = [arB.sbuf_tensor(f"z_stage{i}", [128, 16 * 128], F32) for i in range(2)]
    zc = [arB.sbuf_tensor(f"zc{i}", [128, TC], F32) for i in range(2)]
    xo = [arB.sbuf_tensor(f"xo{i}", [128, TC], F32) for i in range(2)]
    idn = arB.sbuf_tensor("idn", [128, 128], BF16)
    onesq = arB.sbuf_tensor("onesq", [128, 128], BF16)
    arB.allocate()
    kb.op("vector", lambda e: e.memset(onesq[:], 1.0), "dv")
    kb.wait("gpsimd", "dv", kb.val("dv"))
    idv = kb.op("gpsimd", lambda e: e.affine_select(idn[:], onesq[:], pattern=[[1, 128]],
                                                    compare_op=OP.is_equal, fill=0.0,
                                                    base=0, channel_multiplier=-1), "po")
    kb.wait("tensor", "po", idv)
    accv = kb.val("dv")
    xq_cons = [0, 0]
    abf_cons = 0
    lnx_cons = 0
    pt_free = 0
    zst_free = [0, 0]
    statv = None
    for qt in range(8):
        bi = qt % 2
        kb.wait("vector", "dv", accv)
        kb.wait("sync", "dv", xq_cons[bi])
        xv = kb.dma("sync", xq_sb[bi].ap().rearrange("p (c q) -> p c q", c=16),
                    x[:, qt * 128:(qt + 1) * 128].rearrange("(c p) q -> p c q", p=128),
                    sem=f"xq{bi}")
        kb.wait("vector", "pe", abf_cons)
        kb.op("vector", lambda e, qt=qt: e.reciprocal(
            rcol[:], acc[:, qt * (D + 1) + D: qt * (D + 1) + D + 1]), "dv")
        dvv = None
        for nb in range(4):
            dvv = kb.op("vector", lambda e, qt=qt, nb=nb: e.tensor_scalar(
                attn_bf[:, nb * 512:nb * 512 + 512],
                acc[:, qt * (D + 1) + nb * 512: qt * (D + 1) + nb * 512 + 512],
                rcol[:], None, op0=OP.mult), "dv")
        kb.wait("tensor", "dv", dvv)
        kb.wait("vector", f"xq{bi}", xv)
        for rd in range(2):
            kb.wait("tensor", "dv", pt_free)
            pev = None
            for c8 in range(8):
                ch = rd * 8 + c8
                pev = kb.op("tensor", lambda e, ch=ch, c8=c8: e.transpose(
                    pt[:, c8 * 128: c8 * 128 + 128],
                    attn_bf[:, ch * 128:ch * 128 + 128], idn[:]), "pe")
            kb.wait("vector", "pe", pev)
            kb.wait("vector", f"zst{bi}", zst_free[bi])
            pt_free = kb.op("vector", lambda e, rd=rd, bi=bi: e.tensor_tensor(
                z_stage[bi][:, rd * 1024:(rd + 1) * 1024], pt[:, 0:1024],
                xq_sb[bi][:, rd * 1024:(rd + 1) * 1024], op=OP.add), "dv")
            if rd == 1:
                xq_cons[bi] = pt_free
                abf_cons = pev
        kb.wait("sync", "dv", pt_free)
        zv = kb.dma("sync", z_dram[:, qt * 128:(qt + 1) * 128]
                    .rearrange("(c p) q -> p c q", p=128),
                    z_stage[bi].ap().rearrange("p (c q) -> p c q", c=16), sem=f"zst{bi}")
        zst_free[bi] = zv
        pse_s = pool[0] if qt < 4 else pool[1]
        pse_q = pool[2] if qt < 4 else pool[3]
        col = (qt % 4) * 128
        for ch in range(16):
            kb.wait("vector", "pe", lnx_cons)
            dvv = kb.op("vector", lambda e, ch=ch, bi=bi: e.tensor_copy(
                ln_xbf[:], z_stage[bi][:, ch * 128:ch * 128 + 128]), "dv")
            kb.wait("scalar", "dv", dvv)
            acv = kb.op("scalar", lambda e: e.square(ln_xsq[:], ln_xbf[:]), "ac")
            kb.wait("tensor", "dv", dvv)
            kb.wait("tensor", "ac", acv)
            kb.op("tensor", lambda e, ch=ch, col=col, pse_s=pse_s: e.matmul(
                pse_s[0:1, col:col + 128], ln_onescol[:], ln_xbf[:],
                start=(ch == 0), stop=(ch == 15)), "pe")
            statv = kb.op("tensor", lambda e, ch=ch, col=col, pse_q=pse_q: e.matmul(
                pse_q[0:1, col:col + 128], ln_onescol[:], ln_xsq[:],
                start=(ch == 0), stop=(ch == 15)), "pe")
            lnx_cons = statv

    mean = lambda: ln_rows[0:1, 0:TC]
    msq = lambda: ln_rows[0:1, TC:2 * TC]
    tmp = lambda: ln_b[0:1, 0:TC]
    var = lambda: ln_rows[0:1, TC:2 * TC]
    std = lambda: ln_rows[0:1, TC:2 * TC]
    arow = lambda: ln_rows[0:1, TC:2 * TC]
    kb.wait("vector", "pe", statv)
    for hf in range(2):
        kb.op("vector", lambda e, hf=hf: e.tensor_scalar(
            mean()[0:1, hf * 512:hf * 512 + 512], pool[hf][0:1, 0:512],
            1.0 / D, None, op0=OP.mult), "dv")
        kb.op("vector", lambda e, hf=hf: e.tensor_scalar(
            msq()[0:1, hf * 512:hf * 512 + 512], pool[2 + hf][0:1, 0:512],
            1.0 / D, None, op0=OP.mult), "dv")
    kb.op("vector", lambda e: e.tensor_tensor(tmp(), mean(), mean(), op=OP.mult), "dv")
    dvv = kb.op("vector", lambda e: e.tensor_tensor(var(), msq(), tmp(), op=OP.subtract), "dv")
    kb.wait("scalar", "dv", dvv)
    acv = kb.op("scalar", lambda e: e.activation(std(), var(), AF.Sqrt,
                                                 bias=cb[0:1, 1:2], scale=1.0), "ac")
    kb.wait("vector", "ac", acv)
    kb.op("vector", lambda e: e.reciprocal(arow(), std()), "dv")
    kb.op("vector", lambda e: e.tensor_tensor(ln_b[0:1, :], mean(), arow(), op=OP.mult), "dv")
    kb.op("vector", lambda e: e.tensor_scalar(ln_b[0:1, :], ln_b[0:1, :], -1.0, None,
                                              op0=OP.mult), "dv")
    kb.op("vector", lambda e: e.tensor_copy(ln_a_bf[:], arow()), "dv")
    dvv = kb.op("vector", lambda e: e.tensor_copy(ln_b_bf[:], ln_b[0:1, :]), "dv")
    kb.wait("tensor", "dv", dvv)
    pev = None
    for hf in range(2):
        kb.op("tensor", lambda e, hf=hf: e.matmul(
            pool[hf][:, 0:512], onesrow[0:1, 0:128],
            ln_a_bf[0:1, hf * 512:hf * 512 + 512], start=True, stop=True), "pe")
        pev = kb.op("tensor", lambda e, hf=hf: e.matmul(
            pool[2 + hf][:, 0:512], onesrow[0:1, 0:128],
            ln_b_bf[0:1, hf * 512:hf * 512 + 512], start=True, stop=True), "pe")
    kb.wait("vector", "pe", pev)
    zld = [0, 0]
    xov = [0, 0]
    for ch in range(16):
        bi = ch % 2
        kb.wait("sync", "dv", zld[bi])
        kb.wait("sync", f"zst{0}", zst_free[0])
        kb.wait("sync", f"zst{1}", zst_free[1])
        zv = kb.dma("sync", zc[bi][:], z_dram[ch * 128:(ch + 1) * 128, :], sem=f"zld{bi}")
        kb.wait("vector", f"zld{bi}", zv)
        kb.wait("vector", f"xo{bi}", xov[bi])
        dvv = None
        for hf in range(2):
            kb.op("vector", lambda e, bi=bi, hf=hf: e.tensor_tensor(
                xo[bi][:, hf * 512:hf * 512 + 512], zc[bi][:, hf * 512:hf * 512 + 512],
                pool[hf][:, 0:512], op=OP.mult), "dv")
            dvv = kb.op("vector", lambda e, bi=bi, hf=hf: e.tensor_tensor(
                xo[bi][:, hf * 512:hf * 512 + 512], xo[bi][:, hf * 512:hf * 512 + 512],
                pool[2 + hf][:, 0:512], op=OP.add), "dv")
        zld[bi] = dvv
        kb.wait("sync", "dv", dvv)
        xov[bi] = kb.dma("sync", x3[ch * 128:(ch + 1) * 128, :], xo[bi][:], sem=f"xo{bi}")
    kb.wait("sync", f"xo{0}", xov[0])
    kb.wait("sync", f"xo{1}", xov[1])
    if standalone:
        return kb.finalize()



def build_l1_ffn(kb=None):
    standalone = kb is None
    if standalone:
        kb = KB()
    nc = kb.nc
    H = 8192
    x = kb.input("x", [D, TC], F32)
    upq = kb.input("upq", [D, H], I32)
    ups = kb.input("ups", [D, H], F32)
    b_up = kb.input("b_up", [1, H], F32)
    upA = kb.input("upA", [D, 16], F32)
    upB = kb.input("upB", [16, H], F32)
    dnq = kb.input("dnq", [H, D], I32)
    dns = kb.input("dns", [H, D], F32)
    b_dn = kb.input("b_dn", [1, D], F32)
    dnA = kb.input("dnA", [H, 16], F32)
    dnB = kb.input("dnB", [16, D], F32)
    x4 = kb.output("x4", [D, TC], F32)
    z_dram = kb.dram("zf_dram", [D, TC], F32)

    x_bf = kb.sb("x_bf", [128, 16 * 512], BF16)
    x2_bf = kb.sb("x2_bf", [128, 64 * 512], BF16)
    upA_sb = kb.sb("upA_sb", [128, 16 * 16], BF16)
    dnA_sb = kb.sb("dnA_sb", [128, 64 * 16], BF16)
    upB_sb = kb.sb("upB_sb", [16, H], BF16)
    dnB_sb = kb.sb("dnB_sb", [16, D], BF16)
    bup_bf = kb.sb("bup_bf", [1, H], BF16)
    bdn_bf = kb.sb("bdn_bf", [1, D], BF16)
    u1_bf = kb.sb("u1_bf", [16, 512], BF16)
    u2_bf = kb.sb("u2_bf", [16, 512], BF16)
    onesrow = kb.sb("onesrow", [1, 512], BF16)
    xr = [kb.sb(f"xr{i}", [128, 512], F32) for i in range(2)]
    zs = [kb.sb(f"zs{i}", [128, 512], F32) for i in range(2)]
    zbf = kb.sb("zbf", [128, 512], BF16)
    zsq = kb.sb("zsq", [128, 512], BF16)
    onescol = kb.sb("onescol", [128, 1], BF16)
    cb = kb.sb("cb", [128, 2], F32)
    ln_rows = kb.sb("ln_rows", [1, 2 * TC], F32)
    ln_b = kb.sb("ln_b", [1, TC], F32)
    ln_a_bf = kb.sb("ln_a_bf", [1, TC], BF16)
    ln_b_bf = kb.sb("ln_b_bf", [1, TC], BF16)
    arW = kb.arena()
    wq = [arW.sbuf_tensor(f"wq{i}", [128, 4096], BF16) for i in range(2)]
    ws = [arW.sbuf_tensor(f"ws{i}", [128, 4096], BF16) for i in range(2)]
    arW.allocate()

    pool = [kb.ps(f"pool{i}", [128, 512], F32) for i in range(6)]

    kb.dma("gpsimd", upA_sb.ap().rearrange("p (c r) -> p c r", c=16),
           upA.rearrange("(c p) r -> p c r", p=128), sem="wgx")
    kb.dma("gpsimd", dnA_sb.ap().rearrange("p (c r) -> p c r", c=64),
           dnA.rearrange("(c p) r -> p c r", p=128), sem="wgx")
    kb.dma("gpsimd", upB_sb[:], upB, sem="wgx")
    kb.dma("gpsimd", dnB_sb[:], dnB, sem="wgx")
    kb.dma("gpsimd", bup_bf[:], b_up, sem="wgx")
    xgv = kb.dma("gpsimd", bdn_bf[:], b_dn, sem="wgx")
    kb.op("vector", lambda e: e.memset(onesrow[:], 1.0))
    kb.op("vector", lambda e: e.memset(onescol[:], 1.0))
    dvv = kb.op("vector", lambda e: e.memset(cb[:, 0:1], LN_EPS), "dv")
    kb.wait("tensor", "dv", dvv)
    kb.wait("tensor", "wgx", xgv)
    kb.wait("scalar", "wgx", xgv)

    wfree = {0: 0, 1: 0}
    psfree = {4: 0, 5: 0}
    statv = None
    zsfree = [0, 0]
    xrfree = [0, 0]
    zstv = [0, 0]
    lnx_cons = 0
    xbf_cons = 0
    x2_cons = 0
    for th in range(2):
        kb.wait("gpsimd", "pe", xbf_cons)
        xv = kb.dma("gpsimd", x_bf.ap().rearrange("p (c t) -> p c t", c=16),
                    x[:, th * 512:(th + 1) * 512].rearrange("(c p) t -> p c t", p=128),
                    sem="xbf")
        kb.wait("tensor", "xbf", xv)
        kb.wait("tensor", "ac", psfree[5])
        pv = None
        for kc in range(16):
            pv = kb.op("tensor", lambda e, kc=kc: e.matmul(
                pool[5][0:16, 0:512], upA_sb[:, kc * 16:kc * 16 + 16],
                x_bf[:, kc * 512:(kc + 1) * 512],
                start=(kc == 0), stop=(kc == 15)), "pe")
        kb.wait("scalar", "pe", pv)
        u1v = kb.op("scalar", lambda e: e.activation(
            u1_bf[0:16, 0:512], pool[5][0:16, 0:512],
            AF.Identity, bias=0.0, scale=1.0), "ac")
        psfree[5] = u1v
        kb.wait("tensor", "ac", u1v)
        for hb in range(32):
            bi = hb % 2
            kb.wait("gpsimd", "pe", wfree[bi])
            kb.dma("gpsimd", wq[bi].ap().rearrange("p (c d) -> p c d", c=16),
                   upq[:, hb * 256:(hb + 1) * 256].rearrange("(c p) d -> p c d", p=128),
                   sem=f"wq{bi}")
            wv1 = kb.val(f"wq{bi}")
            kb.dma("gpsimd", ws[bi].ap().rearrange("p (c d) -> p c d", c=16),
                   ups[:, hb * 256:(hb + 1) * 256].rearrange("(c p) d -> p c d", p=128),
                   sem=f"ws{bi}")
            wv2 = kb.val(f"ws{bi}")
            kb.wait("vector", f"wq{bi}", wv1)
            kb.wait("vector", f"ws{bi}", wv2)
            dqv = kb.op("vector", lambda e, bi=bi: e.scalar_tensor_tensor(
                wq[bi][:], wq[bi][:], -7.5, ws[bi][:], op0=OP.add, op1=OP.mult), "dv")
            kb.wait("tensor", "dv", dqv)
            for sub in range(2):
                hchunk = hb * 2 + sub
                pi = 4 + (hchunk % 2)
                kb.wait("tensor", "ac", psfree[pi])
                for kc in range(16):
                    kb.op("tensor", lambda e, bi=bi, sub=sub, kc=kc, pi=pi: e.matmul(
                        pool[pi][:, 0:512],
                        wq[bi][:, kc * 256 + sub * 128: kc * 256 + sub * 128 + 128],
                        x_bf[:, kc * 512:(kc + 1) * 512],
                        start=(kc == 0), stop=False), "pe")
                kb.op("tensor", lambda e, hchunk=hchunk, pi=pi: e.matmul(
                    pool[pi][:, 0:512],
                    upB_sb[0:16, hchunk * 128:(hchunk + 1) * 128],
                    u1_bf[0:16, 0:512], start=False, stop=False), "pe")
                pv = kb.op("tensor", lambda e, hchunk=hchunk, pi=pi: e.matmul(
                    pool[pi][:, 0:512],
                    bup_bf[0:1, hchunk * 128:(hchunk + 1) * 128],
                    onesrow[0:1, 0:512], start=False, stop=True), "pe")
                if sub == 1:
                    wfree[bi] = pv
                kb.wait("scalar", "pe", pv)
                psfree[pi] = kb.op("scalar", lambda e, hchunk=hchunk, pi=pi: e.activation(
                    x2_bf[:, hchunk * 512:(hchunk + 1) * 512],
                    pool[pi][:, 0:512], AF.Relu, bias=0.0, scale=1.0), "ac")
        x2v = kb.val("ac")
        xbf_cons = pv
        kb.wait("tensor", "ac", x2v)
        for hc in range(64):
            pv = kb.op("tensor", lambda e, hc=hc: e.matmul(
                pool[5][0:16, 0:512], dnA_sb[:, hc * 16:hc * 16 + 16],
                x2_bf[:, hc * 512:(hc + 1) * 512],
                start=(hc == 0), stop=(hc == 63)), "pe")
        kb.wait("scalar", "pe", pv)
        u2v = kb.op("scalar", lambda e: e.activation(
            u2_bf[0:16, 0:512], pool[5][0:16, 0:512],
            AF.Identity, bias=0.0, scale=1.0), "ac")
        psfree[5] = u2v
        kb.wait("tensor", "ac", u2v)
        for ch in range(16):
            pi = 4 + (ch % 2)
            kb.wait("tensor", "dv", zsfree[pi - 4])
            for hh in range(2):
                bi = (ch * 2 + hh) % 2
                kb.wait("gpsimd", "pe", wfree[bi])
                kb.dma("gpsimd", wq[bi].ap().rearrange("p (c d) -> p c d", c=32),
                       dnq[hh * 4096:(hh + 1) * 4096, ch * 128:(ch + 1) * 128]
                       .rearrange("(c p) d -> p c d", p=128), sem=f"wq{bi}")
                wv1 = kb.val(f"wq{bi}")
                kb.dma("gpsimd", ws[bi].ap().rearrange("p (c d) -> p c d", c=32),
                       dns[hh * 4096:(hh + 1) * 4096, ch * 128:(ch + 1) * 128]
                       .rearrange("(c p) d -> p c d", p=128), sem=f"ws{bi}")
                wv2 = kb.val(f"ws{bi}")
                kb.wait("vector", f"wq{bi}", wv1)
                kb.wait("vector", f"ws{bi}", wv2)
                dqv = kb.op("vector", lambda e, bi=bi: e.scalar_tensor_tensor(
                    wq[bi][:], wq[bi][:], -7.5, ws[bi][:], op0=OP.add, op1=OP.mult), "dv")
                kb.wait("tensor", "dv", dqv)
                for hc in range(32):
                    pv = kb.op("tensor", lambda e, bi=bi, hh=hh, hc=hc, pi=pi: e.matmul(
                        pool[pi][:, 0:512],
                        wq[bi][:, hc * 128:(hc + 1) * 128],
                        x2_bf[:, (hh * 32 + hc) * 512:(hh * 32 + hc) * 512 + 512],
                        start=(hh == 0 and hc == 0), stop=False), "pe")
                wfree[bi] = pv
            kb.op("tensor", lambda e, ch=ch, pi=pi: e.matmul(
                pool[pi][:, 0:512], dnB_sb[0:16, ch * 128:(ch + 1) * 128],
                u2_bf[0:16, 0:512], start=False, stop=False), "pe")
            pv = kb.op("tensor", lambda e, ch=ch, pi=pi: e.matmul(
                pool[pi][:, 0:512], bdn_bf[0:1, ch * 128:(ch + 1) * 128],
                onesrow[0:1, 0:512], start=False, stop=True), "pe")
            xbi = ch % 2
            kb.wait("sync", "dv", xrfree[xbi])
            xrv = kb.dma("sync", xr[xbi][:],
                         x[ch * 128:(ch + 1) * 128, th * 512:(th + 1) * 512], sem=f"xr{xbi}")
            kb.wait("vector", "pe", pv)
            kb.wait("vector", f"xr{xbi}", xrv)
            zi = ch % 2
            kb.wait("vector", f"zst{zi}", zstv[zi])
            kb.wait("vector", "pe", lnx_cons)
            zv = kb.op("vector", lambda e, zi=zi, xbi=xbi, pi=pi: e.tensor_tensor(
                zs[zi][:], pool[pi][:, 0:512], xr[xbi][:], op=OP.add), "dv")
            zsfree[pi - 4] = zv
            xrfree[xbi] = zv
            zbv = kb.op("vector", lambda e, zi=zi: e.tensor_copy(zbf[:], zs[zi][:]), "dv")
            kb.wait("scalar", "dv", zbv)
            sqv = kb.op("scalar", lambda e: e.square(zsq[:], zbf[:]), "ac")
            kb.wait("tensor", "dv", zbv)
            kb.wait("tensor", "ac", sqv)
            pse_s = pool[0] if th == 0 else pool[1]
            pse_q = pool[2] if th == 0 else pool[3]
            kb.op("tensor", lambda e, ch=ch, pse_s=pse_s: e.matmul(
                pse_s[0:1, 0:512], onescol[:], zbf[:],
                start=(ch == 0), stop=(ch == 15)), "pe")
            statv = kb.op("tensor", lambda e, ch=ch, pse_q=pse_q: e.matmul(
                pse_q[0:1, 0:512], onescol[:], zsq[:],
                start=(ch == 0), stop=(ch == 15)), "pe")
            lnx_cons = statv
            kb.wait("sync", "dv", zv)
            zstv[zi] = kb.dma("sync",
                              z_dram[ch * 128:(ch + 1) * 128, th * 512:(th + 1) * 512],
                              zs[zi][:], sem=f"zst{zi}")

    kb.barrier()
    arW.free()
    arZ = kb.arena()
    zc = [arZ.sbuf_tensor(f"zc{i}", [128, TC], F32) for i in range(2)]
    xo = [arZ.sbuf_tensor(f"xo{i}", [128, TC], F32) for i in range(2)]
    arZ.allocate()
    mean = lambda: ln_rows[0:1, 0:TC]
    s2 = lambda: ln_rows[0:1, TC:2 * TC]
    kb.wait("vector", "pe", statv)
    for th in range(2):
        kb.op("vector", lambda e, th=th: e.tensor_scalar(
            mean()[0:1, th * 512:th * 512 + 512], pool[th][0:1, 0:512],
            1.0 / D, None, op0=OP.mult), "dv")
        kb.op("vector", lambda e, th=th: e.tensor_scalar(
            s2()[0:1, th * 512:th * 512 + 512], pool[2 + th][0:1, 0:512],
            1.0 / D, None, op0=OP.mult), "dv")
    kb.op("vector", lambda e: e.tensor_tensor(ln_b[0:1, :], mean(), mean(), op=OP.mult), "dv")
    dvv = kb.op("vector", lambda e: e.tensor_tensor(s2(), s2(), ln_b[0:1, :],
                                                    op=OP.subtract), "dv")
    kb.wait("scalar", "dv", dvv)
    acv = kb.op("scalar", lambda e: e.activation(s2(), s2(), AF.Sqrt,
                                                 bias=cb[0:1, 0:1], scale=1.0), "ac")
    kb.wait("vector", "ac", acv)
    kb.op("vector", lambda e: e.reciprocal(s2(), s2()), "dv")
    kb.op("vector", lambda e: e.tensor_tensor(ln_b[0:1, :], mean(), s2(), op=OP.mult), "dv")
    kb.op("vector", lambda e: e.tensor_scalar(ln_b[0:1, :], ln_b[0:1, :], -1.0, None,
                                              op0=OP.mult), "dv")
    kb.op("vector", lambda e: e.tensor_copy(ln_a_bf[:], s2()), "dv")
    dvv = kb.op("vector", lambda e: e.tensor_copy(ln_b_bf[:], ln_b[0:1, :]), "dv")
    kb.wait("tensor", "dv", dvv)
    pev = None
    for hf in range(2):
        kb.op("tensor", lambda e, hf=hf: e.matmul(
            pool[hf][:, 0:512], onesrow[0:1, 0:128],
            ln_a_bf[0:1, hf * 512:hf * 512 + 512], start=True, stop=True), "pe")
        pev = kb.op("tensor", lambda e, hf=hf: e.matmul(
            pool[2 + hf][:, 0:512], onesrow[0:1, 0:128],
            ln_b_bf[0:1, hf * 512:hf * 512 + 512], start=True, stop=True), "pe")
    kb.wait("vector", "pe", pev)
    zld = [0, 0]
    xov = [0, 0]
    kb.wait("sync", "zst0", kb.val("zst0"))
    kb.wait("sync", "zst1", kb.val("zst1"))
    for ch in range(16):
        bi = ch % 2
        kb.wait("sync", "dv", zld[bi])
        zv = kb.dma("sync", zc[bi][:], z_dram[ch * 128:(ch + 1) * 128, :], sem=f"zld{bi}")
        kb.wait("vector", f"zld{bi}", zv)
        kb.wait("vector", f"xo{bi}", xov[bi])
        dvv = None
        for hf in range(2):
            kb.op("vector", lambda e, bi=bi, hf=hf: e.tensor_tensor(
                xo[bi][:, hf * 512:hf * 512 + 512], zc[bi][:, hf * 512:hf * 512 + 512],
                pool[hf][:, 0:512], op=OP.mult), "dv")
            dvv = kb.op("vector", lambda e, bi=bi, hf=hf: e.tensor_tensor(
                xo[bi][:, hf * 512:hf * 512 + 512], xo[bi][:, hf * 512:hf * 512 + 512],
                pool[2 + hf][:, 0:512], op=OP.add), "dv")
        zld[bi] = dvv
        kb.wait("sync", "dv", dvv)
        xov[bi] = kb.dma("sync", x4[ch * 128:(ch + 1) * 128, :], xo[bi][:], sem=f"xo{bi}")
    kb.wait("sync", "xo0", xov[0])
    kb.wait("sync", "xo1", xov[1])
    if standalone:
        return kb.finalize()



F16 = mybir.dt.float16

def build_l0_attn(kb=None, dbg=False):
    standalone = kb is None
    if standalone:
        kb = KB()
    nc = kb.nc
    x = kb.input("x", [D, S], F32)
    xr = kb.input("xr", [D, TC], F32)
    wqT = kb.input("wqT", [D, 256], F32)
    wkT = kb.input("wkT", [D, 256], F32)
    wvT = kb.input("wvT", [D, 256], F32)
    woT = kb.input("woT", [256, D], F32)
    hb2 = kb.input("hb2", [2, 1], F32)
    x1 = kb.output("x1", [D, TC], F32)
    rs_in = kb.dram("rs_in", [S, D], BF16)
    rs_out = kb.dram("rs_out", [TC, D], BF16)
    z_dram = kb.dram("za_dram", [D, TC], F32)
    sl_dram = kb.dram("sl_dram", [1, 2], F32)
    attn_dbg = nc.dram_tensor("attn_dbg", [S, 256], BF16,
                              kind="ExternalOutput").ap() if dbg else None

    LN2 = float(np.log(2.0))

    q_sb = kb.sb("q_sb", [128, 2 * S], BF16)
    k_sb = kb.sb("k_sb", [128, 2 * S], BF16)
    vp_sb = kb.sb("vp_sb", [128, 64 * 258], BF16)
    wo_sb = kb.sb("wo_sb", [128, 2 * D], BF16)
    ptil = kb.sb("ptil", [128, 512], BF16)
    ptil_b2 = kb.sb("ptil_b", [128, 512], BF16)
    ptmask = kb.sb("ptmask", [128, 128], BF16)
    abf = kb.sb("abf", [128, 256], BF16)
    afm_t = kb.sb("afm_t", [128, 256], BF16)
    rs_stage = [kb.sb(f"rs_stage{i}", [128, D], BF16) for i in range(2)]
    m_diag = kb.sb("m_diag", [128, 128], BF16)
    onesq = kb.sb("onesq", [128, 128], BF16)
    idn = kb.sb("idn", [128, 128], BF16)
    onesrow = kb.sb("onesrow", [1, 512], BF16)
    onescol = kb.sb("onescol", [128, 1], BF16)
    rcol = kb.sb("rcol", [128, 1], F32)
    cb = kb.sb("cb", [128, 2], F32)
    slopes = kb.sb("slopes", [2, 1], F32)
    hcol = kb.sb("hcol", [2, 1], F32)
    slope0 = [kb.sb("slope0a", [1, 1], F32), kb.sb("slope0b", [1, 1], F32)]
    ln_xbf = kb.sb("ln_xbf", [128, 128], BF16)
    ln_xsq = kb.sb("ln_xsq", [128, 128], BF16)
    ln_rows = kb.sb("ln_rows", [1, 2 * TC], F32)
    ln_b = kb.sb("ln_b", [1, TC], F32)
    ln_a_bf = kb.sb("ln_a_bf", [1, TC], BF16)
    ln_b_bf = kb.sb("ln_b_bf", [1, TC], BF16)

    arT = kb.arena()
    crhs = arT.sbuf_tensor("crhs", [3, 2 * S], F16)
    clhs = arT.sbuf_tensor("clhs", [3, 256], F16)
    arT.allocate()
    arS = kb.arena()
    scr = arS.sbuf_tensor("scr", [1, S], F32)
    arS.allocate()

    pool = [kb.ps(f"pool{i}", [128, 512], F32) for i in range(6)]
    pt = kb.ps("pt", [128, 1024], BF16)

    ldv = kb.dma("sync", slopes[:], hb2, sem="ld")
    kb.op("vector", lambda e: e.memset(onesrow[:], 1.0))
    kb.op("vector", lambda e: e.memset(onescol[:], 1.0))
    kb.op("vector", lambda e: e.memset(onesq[:], 1.0))
    kb.op("vector", lambda e: e.memset(cb[:, 0:1], LN_EPS))
    kb.op("vector", lambda e: e.memset(
        vp_sb.ap().rearrange("p (t d) -> p t d", t=64)[:, :, 128:129], 1.0))
    dvv = kb.op("vector", lambda e: e.memset(
        vp_sb.ap().rearrange("p (t d) -> p t d", t=64)[:, :, 257:258], 1.0), "dv")
    kb.wait("gpsimd", "dv", dvv)
    idv = kb.op("gpsimd", lambda e: e.affine_select(idn[:], onesq[:], pattern=[[1, 128]],
                                                    compare_op=OP.is_equal, fill=0.0,
                                                    base=0, channel_multiplier=-1), "po")
    idv = kb.op("gpsimd", lambda e: e.affine_select(m_diag[:], onesq[:], pattern=[[1, 128]],
                                                    compare_op=OP.is_ge, fill=0.0,
                                                    base=0, channel_multiplier=-1), "po")
    kb.wait("scalar", "ld", ldv)
    acv = kb.op("scalar", lambda e: e.activation(slopes[:], slopes[:], AF.Exp,
                                                 bias=0.0, scale=-LN2 * 0.5), "ac")
    kb.wait("sync", "ac", acv)
    s1 = kb.dma("sync", sl_dram.rearrange("a b -> b a"), slopes[:], sem="ld")
    kb.wait("sync", "ld", s1)
    kb.dma("sync", slope0[0][:], sl_dram[0:1, 0:1], sem="ld")
    s2v = kb.dma("sync", slope0[1][:], sl_dram[0:1, 1:2], sem="ld")
    kb.wait("vector", "ld", s2v)
    kb.wait("gpsimd", "ld", s2v)

    for h in range(2):
        kb.wait("gpsimd", "tb", kb.val("tb"))
        kb.wait("vector", "tb", kb.val("tb"))
        pov = kb.op("gpsimd", lambda e: e.iota(scr[:], pattern=[[1, 64], [0, 128]],
                                               base=0, channel_multiplier=0,
                                               allow_small_or_imprecise_dtypes=True), "po")
        kb.wait("vector", "po", pov)
        dvv = kb.op("vector", lambda e, h=h: e.tensor_scalar(
            scr[:], scr[:], slope0[h][0:1, 0:1], -128.0, op0=OP.mult, op1=OP.mult), "dv")
        kb.wait("gpsimd", "dv", dvv)
        g1 = kb.dma("gpsimd", crhs[2:3, h * S:(h + 1) * S], scr[:], sem="tb")
        kb.wait("gpsimd", "tb", g1)
        kb.wait("gpsimd", "tb", kb.val("tb"))
        pov = kb.op("gpsimd", lambda e: e.iota(scr[:], pattern=[[0, 64], [1, 128]],
                                               base=0, channel_multiplier=0,
                                               allow_small_or_imprecise_dtypes=True), "po")
        kb.wait("vector", "po", pov)
        kb.wait("vector", "tb", g1)
        dvv = kb.op("vector", lambda e, h=h: e.tensor_scalar(
            scr[:], scr[:], slope0[h][0:1, 0:1], -1.0, op0=OP.mult, op1=OP.mult), "dv")
        dvv = kb.op("vector", lambda e: e.tensor_scalar(
            scr[:], scr[:], B0, None, op0=OP.subtract), "dv")
        kb.wait("gpsimd", "dv", dvv)
        g2 = kb.dma("gpsimd", crhs[1:2, h * S:(h + 1) * S], scr[:], sem="tb")
        kb.wait("gpsimd", "tb", g2)
        kb.wait("vector", "tb", kb.val("tb"))
        dvv = kb.op("vector", lambda e: e.memset(scr[:], 1.0), "dv")
        kb.wait("gpsimd", "dv", dvv)
        g3 = kb.dma("gpsimd", crhs[0:1, h * S:(h + 1) * S], scr[:], sem="tb")
        kb.wait("gpsimd", "tb", g3)
        kb.wait("gpsimd", "tb", kb.val("tb"))
        pov = kb.op("gpsimd", lambda e: e.iota(scr[0:1, 0:128], pattern=[[1, 128]],
                                               base=0, channel_multiplier=0,
                                               allow_small_or_imprecise_dtypes=True), "po")
        kb.wait("vector", "po", pov)
        dvv = kb.op("vector", lambda e, h=h: e.tensor_scalar(
            scr[0:1, 0:128], scr[0:1, 0:128], slope0[h][0:1, 0:1], None,
            op0=OP.mult), "dv")
        kb.wait("gpsimd", "dv", dvv)
        g4 = kb.dma("gpsimd", clhs[0:1, h * 128:(h + 1) * 128], scr[0:1, 0:128], sem="tb")
        kb.wait("gpsimd", "tb", g4)
        kb.wait("vector", "tb", kb.val("tb"))
        dvv = kb.op("vector", lambda e: e.memset(scr[0:1, 0:256], 1.0), "dv")
        kb.wait("gpsimd", "dv", dvv)
        g5 = kb.dma("gpsimd", clhs[1:2, h * 128:(h + 1) * 128], scr[0:1, 0:128], sem="tb")
        g6 = kb.dma("gpsimd", clhs[2:3, h * 128:(h + 1) * 128], scr[0:1, 128:256], sem="tb")
        kb.wait("gpsimd", "tb", g6)
    tbv = kb.val("tb")
    kb.wait("tensor", "tb", tbv)

    kb.barrier()
    arS.free()
    arP = kb.arena()
    xb = [arP.sbuf_tensor(f"xb{i}", [128, 16 * 256], BF16) for i in range(2)]
    wq_sb = arP.sbuf_tensor("wq_sb", [128, 16 * 256], BF16)
    wk_sb = arP.sbuf_tensor("wk_sb", [128, 16 * 256], BF16)
    wv_sb = arP.sbuf_tensor("wv_sb", [128, 16 * 256], BF16)
    arP.allocate()

    wv1 = kb.dma("gpsimd", wq_sb.ap().rearrange("p (c d) -> p c d", c=16),
                 wqT.rearrange("(c p) d -> p c d", p=128), sem="wg0")
    wv2 = kb.dma("gpsimd", wk_sb.ap().rearrange("p (c d) -> p c d", c=16),
                 wkT.rearrange("(c p) d -> p c d", p=128), sem="wg1")
    wv3 = kb.dma("gpsimd", wv_sb.ap().rearrange("p (c d) -> p c d", c=16),
                 wvT.rearrange("(c p) d -> p c d", p=128), sem="wg2")
    wv4 = kb.dma("gpsimd", wo_sb.ap().rearrange("p (c d) -> p c d", c=2),
                 woT.rearrange("(c p) d -> p c d", p=128), sem="wg3")
    for i, s in enumerate(["wg0", "wg1", "wg2", "wg3"]):
        kb.wait("tensor", s, [wv1, wv2, wv3, wv4][i])

    xbv = [0, 0]
    psfree = [0, 0, 0, 0]
    for tb in range(32):
        bi = tb % 2
        kb.wait("gpsimd", "pe", xbv[bi])
        xv = kb.dma("gpsimd", xb[bi].ap().rearrange("p (c t) -> p c t", c=16),
                    x[:, tb * 256:(tb + 1) * 256].rearrange("(c p) t -> p c t", p=128),
                    sem=f"xb{bi}")
        kb.wait("tensor", f"xb{bi}", xv)
        pv = None
        for hc in range(2):
            for src, dst, scale in ((wq_sb, q_sb, L0_SCALE), (wk_sb, k_sb, 1.0)):
                pi = hc % 2
                kb.wait("tensor", "ac", psfree[pi])
                for kc in range(16):
                    pv = kb.op("tensor", lambda e, bi=bi, hc=hc, kc=kc, src=src, pi=pi: e.matmul(
                        pool[pi][:, 0:256],
                        src[:, kc * 256 + hc * 128: kc * 256 + hc * 128 + 128],
                        xb[bi][:, kc * 256:(kc + 1) * 256],
                        start=(kc == 0), stop=(kc == 15)), "pe")
                kb.wait("scalar", "pe", pv)
                psfree[pi] = kb.op("scalar", lambda e, tb=tb, hc=hc, dst=dst, scale=scale,
                                   pi=pi: e.activation(
                    dst[:, hc * S + tb * 256: hc * S + tb * 256 + 256],
                    pool[pi][:, 0:256], AF.Identity, bias=0.0, scale=scale), "ac")
        for ts in range(2):
            tt = tb * 2 + ts
            pi = 2 + ts % 2
            kb.wait("tensor", "ac", psfree[pi])
            for kc in range(16):
                pv = kb.op("tensor", lambda e, bi=bi, ts=ts, kc=kc, pi=pi: e.matmul(
                    pool[pi][:, 0:256],
                    xb[bi][:, kc * 256 + ts * 128: kc * 256 + ts * 128 + 128],
                    wv_sb[:, kc * 256:(kc + 1) * 256],
                    start=(kc == 0), stop=(kc == 15)), "pe")
            kb.wait("scalar", "pe", pv)
            kb.op("scalar", lambda e, tt=tt, pi=pi: e.activation(
                vp_sb[:, tt * 258: tt * 258 + 128], pool[pi][:, 0:128],
                AF.Identity, bias=0.0, scale=1.0), "ac")
            psfree[pi] = kb.op("scalar", lambda e, tt=tt, pi=pi: e.activation(
                vp_sb[:, tt * 258 + 129: tt * 258 + 257], pool[pi][:, 128:256],
                AF.Identity, bias=0.0, scale=1.0), "ac")
        xbv[bi] = pv

    kb.barrier()
    arP.free()
    projv = kb.val("ac")
    psS_free = [0, 0]
    pfree = [0, 0]
    psO_free = [("ac", projv), ("ac", projv)]
    ptil_free = 0
    abf_cons = 0
    pt_free = 0
    rs_free = [0, 0]
    opfree = [0, 0]
    ptil2b = [ptil, ptil_b2]
    for qt in range(64):
        for h in range(2):
            psO = pool[2 + h]
            kb.wait("tensor", psO_free[h][0], psO_free[h][1])
            ngr = qt // 4 + 1

            def emit_S(g, h=h, qt=qt, ngr=ngr):
                si = g % 2
                psS = pool[si]
                nsl = min(4, qt + 1 - g * 4)
                kb.wait("tensor", "ac", psS_free[si])
                spev = None
                for j in range(nsl):
                    kc = g * 4 + j
                    dd = qt - kc
                    kb.op("tensor", lambda e, h=h, kc=kc, qt=qt, j=j, psS=psS: e.matmul(
                        psS[:, j * 128:(j + 1) * 128],
                        k_sb[:, h * S + kc * 128: h * S + kc * 128 + 128],
                        q_sb[:, h * S + qt * 128: h * S + qt * 128 + 128],
                        start=True, stop=False), "pe")
                    spev = kb.op("tensor", lambda e, h=h, dd=dd, j=j, psS=psS: e.matmul(
                        psS[:, j * 128:(j + 1) * 128],
                        clhs[0:3, h * 128:(h + 1) * 128],
                        crhs[0:3, h * S + dd * 128: h * S + dd * 128 + 128],
                        start=False, stop=True), "pe")
                return spev

            def emit_P(g, spev, h=h, qt=qt, ngr=ngr):
                nonlocal psO_free, ptil_free, pfree
                si = g % 2
                pi = g % 2
                psS = pool[si]
                pt_b = ptil2b[pi]
                nsl = min(4, qt + 1 - g * 4)
                kb.wait("scalar", "pe", spev)
                kb.wait("scalar", "pe", pfree[pi])
                av = kb.op("scalar", lambda e, nsl=nsl, psS=psS, pt_b=pt_b: e.activation(
                    pt_b[:, 0:nsl * 128], psS[:, 0:nsl * 128], AF.Exp,
                    bias=0.0, scale=1.0), "ac")
                psS_free[si] = av
                mav = av
                if g == ngr - 1:
                    kb.wait("vector", "ac", av)
                    kb.wait("vector", "pe", ptil_free)
                    mav = kb.op("vector", lambda e, nsl=nsl, pt_b=pt_b: e.tensor_tensor(
                        ptmask[:], pt_b[:, (nsl - 1) * 128: nsl * 128], m_diag[:],
                        op=OP.mult), "dv")
                    kb.wait("tensor", "dv", mav)
                kb.wait("tensor", "ac", av)
                last = None
                for j in range(nsl):
                    kc = g * 4 + j
                    lhs = (lambda e, nsl=nsl: ptmask[:]) if (g == ngr - 1 and j == nsl - 1) \
                        else (lambda e, j=j, pt_b=pt_b: pt_b[:, j * 128:(j + 1) * 128])
                    last = kb.op("tensor", lambda e, h=h, kc=kc, g=g, j=j,
                                 lhs=lhs, psO=psO, ngr=ngr, nsl=nsl: e.matmul(
                        psO[:, 0:129], lhs(e),
                        vp_sb[:, kc * 258 + 129 * h: kc * 258 + 129 * h + 129],
                        start=(g == 0 and j == 0), stop=(g == ngr - 1 and j == nsl - 1)), "pe")
                pfree[pi] = last
                ptil_free = last

            pend = emit_S(0)
            for g in range(1, ngr):
                pend2 = emit_S(g)
                emit_P(g - 1, pend)
                pend = pend2
            emit_P(ngr - 1, pend)
            kb.wait("vector", "pe", ptil_free)
            kb.wait("vector", "pe", abf_cons)
            kb.op("vector", lambda e, psO=psO: e.reciprocal(rcol[:], psO[:, 128:129]), "dv")
            nv = kb.op("vector", lambda e, h=h, psO=psO: e.tensor_scalar(
                abf[:, h * 128:(h + 1) * 128], psO[:, 0:128], rcol[:], None,
                op0=OP.mult), "dv")
            psO_free[h] = ("dv", nv)
        if dbg:
            kb.wait("sync", "dv", nv)
            kb.dma("sync", nc.__dict__.setdefault("_attn_dbg", None) or attn_dbg[qt * 128:(qt + 1) * 128, :], abf[:], sem="dbg2")
        kb.wait("tensor", "dv", nv)
        kb.wait("tensor", "ac", pt_free)
        pev = None
        for hc in range(2):
            pev = kb.op("tensor", lambda e, hc=hc: e.transpose(
                pt[:, hc * 128:(hc + 1) * 128], abf[:, hc * 128:(hc + 1) * 128],
                idn[:]), "pe")
        kb.wait("scalar", "pe", pev)
        abf_cons = pev
        tev = kb.op("scalar", lambda e: e.activation(
            afm_t[:, 0:256], pt[:, 0:256], AF.Identity, bias=0.0, scale=1.0), "ac")
        pt_free = tev
        kb.wait("tensor", "ac", tev)
        ri = qt % 2
        kb.wait("tensor", f"st{ri}", rs_free[ri])
        opv = None
        for nb in range(4):
            pi = 4 + nb % 2
            kb.wait("tensor", "ac", opfree[pi - 4])
            for hc in range(2):
                opv = kb.op("tensor", lambda e, hc=hc, nb=nb, pi=pi: e.matmul(
                    pool[pi][:, 0:512],
                    afm_t[:, hc * 128:(hc + 1) * 128],
                    wo_sb[:, hc * D + nb * 512: hc * D + nb * 512 + 512],
                    start=(hc == 0), stop=(hc == 1)), "pe")
            kb.wait("scalar", "pe", opv)
            opfree[pi - 4] = kb.op("scalar", lambda e, nb=nb, pi=pi, ri=ri: e.activation(
                rs_stage[ri][:, nb * 512:(nb + 1) * 512], pool[pi][:, 0:512],
                AF.Identity, bias=0.0, scale=1.0), "ac")
        sv = kb.val("ac")
        kb.wait("sync", "ac", sv)
        rs_free[ri] = kb.dma("sync", rs_in[qt * 128:(qt + 1) * 128, :],
                             rs_stage[ri][:], sem=f"st{ri}")
    kb.wait("gpsimd", "st0", kb.val("st0"))
    kb.wait("gpsimd", "st1", kb.val("st1"))

    if dbg:
        kb.wait("sync", "dbg2", kb.val("dbg2"))
        cdbg = kb.output("cdbg", [3, 2 * S], F32)
        ldbg = kb.output("ldbg", [3, 256], F32)
        kb.dma("gpsimd", cdbg, crhs[0:3, :], sem="dbg3")
        db3 = kb.dma("gpsimd", ldbg, clhs[0:3, :], sem="dbg3")
        kb.wait("sync", "dbg3", db3)
        qdbg = kb.output("qdbg", [128, 2 * S], BF16)
        kdbg = kb.output("kdbg", [128, 2 * S], BF16)
        vdbg = kb.output("vdbg", [128, 64 * 258], BF16)
        rdbg = kb.output("rdbg", [S, D], BF16)
        kb.dma("sync", qdbg, q_sb[:], sem="dbg")
        kb.dma("sync", kdbg, k_sb[:], sem="dbg")
        kb.dma("sync", vdbg, vp_sb[:], sem="dbg")
        dbgv = kb.dma("sync", rdbg, rs_in, sem="dbg")
        kb.wait("sync", "dbg", dbgv)

    ccv = kb.op("gpsimd", lambda e: e.collective_compute(
        "ReduceScatter", mybir.AluOpType.add,
        replica_groups=[list(range(NC))],
        ins=[rs_in.opt()], outs=[rs_out.opt()]), "cc", amt=1)
    kb.wait("sync", "cc", ccv)
    kb.wait("vector", "cc", ccv)
    kb.wait("tensor", "cc", ccv)
    kb.wait("scalar", "cc", ccv)

    kb.barrier()
    arT.free()
    arZ = kb.arena()
    rst = [arZ.sbuf_tensor(f"rst{i}", [128, D], BF16) for i in range(2)]
    xq_sb = [arZ.sbuf_tensor(f"xq_sb{i}", [128, 16 * 128], F32) for i in range(2)]
    z_stage = [arZ.sbuf_tensor(f"z_stage{i}", [128, 16 * 128], F32) for i in range(2)]
    zc = [arZ.sbuf_tensor(f"zc{i}", [128, TC], F32) for i in range(2)]
    xo = [arZ.sbuf_tensor(f"xo{i}", [128, TC], F32) for i in range(2)]
    arZ.allocate()
    pt_free = 0
    zst_free = [0, 0]
    statv = None
    lnx_cons = 0
    rst_cons = [0, 0]
    xq_cons = [0, 0]
    for qt in range(8):
        bi = qt % 2
        kb.wait("sync", "pe", rst_cons[bi])
        rv = kb.dma("sync", rst[bi][:], rs_out[qt * 128:(qt + 1) * 128, :], sem=f"rst{bi}")
        kb.wait("sync", "dv", xq_cons[bi])
        xv = kb.dma("sync", xq_sb[bi].ap().rearrange("p (c q) -> p c q", c=16),
                    xr[:, qt * 128:(qt + 1) * 128].rearrange("(c p) q -> p c q", p=128),
                    sem=f"xq{bi}")
        kb.wait("tensor", f"rst{bi}", rv)
        kb.wait("vector", f"xq{bi}", xv)
        for rd in range(2):
            kb.wait("tensor", "dv", pt_free)
            pev = None
            for c8 in range(8):
                ch = rd * 8 + c8
                pev = kb.op("tensor", lambda e, ch=ch, c8=c8, bi=bi: e.transpose(
                    pt[:, c8 * 128: c8 * 128 + 128],
                    rst[bi][:, ch * 128:ch * 128 + 128], idn[:]), "pe")
            kb.wait("vector", "pe", pev)
            kb.wait("vector", f"zst{bi}", zst_free[bi])
            pt_free = kb.op("vector", lambda e, rd=rd, bi=bi: e.tensor_tensor(
                z_stage[bi][:, rd * 1024:(rd + 1) * 1024], pt[:, 0:1024],
                xq_sb[bi][:, rd * 1024:(rd + 1) * 1024], op=OP.add), "dv")
            if rd == 1:
                xq_cons[bi] = pt_free
                rst_cons[bi] = pev
        kb.wait("sync", "dv", pt_free)
        zv = kb.dma("sync", z_dram[:, qt * 128:(qt + 1) * 128]
                    .rearrange("(c p) q -> p c q", p=128),
                    z_stage[bi].ap().rearrange("p (c q) -> p c q", c=16), sem=f"zst{bi}")
        zst_free[bi] = zv
        pse_s = pool[0] if qt < 4 else pool[1]
        pse_q = pool[2] if qt < 4 else pool[3]
        col = (qt % 4) * 128
        for ch in range(16):
            kb.wait("vector", "pe", lnx_cons)
            dvv = kb.op("vector", lambda e, ch=ch, bi=bi: e.tensor_copy(
                ln_xbf[:], z_stage[bi][:, ch * 128:ch * 128 + 128]), "dv")
            kb.wait("scalar", "dv", dvv)
            acv = kb.op("scalar", lambda e: e.square(ln_xsq[:], ln_xbf[:]), "ac")
            kb.wait("tensor", "dv", dvv)
            kb.wait("tensor", "ac", acv)
            kb.op("tensor", lambda e, ch=ch, col=col, pse_s=pse_s: e.matmul(
                pse_s[0:1, col:col + 128], onescol[:], ln_xbf[:],
                start=(ch == 0), stop=(ch == 15)), "pe")
            statv = kb.op("tensor", lambda e, ch=ch, col=col, pse_q=pse_q: e.matmul(
                pse_q[0:1, col:col + 128], onescol[:], ln_xsq[:],
                start=(ch == 0), stop=(ch == 15)), "pe")
            lnx_cons = statv

    mean = lambda: ln_rows[0:1, 0:TC]
    s2r = lambda: ln_rows[0:1, TC:2 * TC]
    kb.wait("vector", "pe", statv)
    for hf2 in range(2):
        kb.op("vector", lambda e, hf2=hf2: e.tensor_scalar(
            mean()[0:1, hf2 * 512:hf2 * 512 + 512], pool[hf2][0:1, 0:512],
            1.0 / D, None, op0=OP.mult), "dv")
        kb.op("vector", lambda e, hf2=hf2: e.tensor_scalar(
            s2r()[0:1, hf2 * 512:hf2 * 512 + 512], pool[2 + hf2][0:1, 0:512],
            1.0 / D, None, op0=OP.mult), "dv")
    kb.op("vector", lambda e: e.tensor_tensor(ln_b[0:1, :], mean(), mean(), op=OP.mult), "dv")
    dvv = kb.op("vector", lambda e: e.tensor_tensor(s2r(), s2r(), ln_b[0:1, :],
                                                    op=OP.subtract), "dv")
    kb.wait("scalar", "dv", dvv)
    acv = kb.op("scalar", lambda e: e.activation(s2r(), s2r(), AF.Sqrt,
                                                 bias=cb[0:1, 0:1], scale=1.0), "ac")
    kb.wait("vector", "ac", acv)
    kb.op("vector", lambda e: e.reciprocal(s2r(), s2r()), "dv")
    kb.op("vector", lambda e: e.tensor_tensor(ln_b[0:1, :], mean(), s2r(), op=OP.mult), "dv")
    kb.op("vector", lambda e: e.tensor_scalar(ln_b[0:1, :], ln_b[0:1, :], -1.0, None,
                                              op0=OP.mult), "dv")
    kb.op("vector", lambda e: e.tensor_copy(ln_a_bf[:], s2r()), "dv")
    dvv = kb.op("vector", lambda e: e.tensor_copy(ln_b_bf[:], ln_b[0:1, :]), "dv")
    kb.wait("tensor", "dv", dvv)
    pev = None
    for hf2 in range(2):
        kb.op("tensor", lambda e, hf2=hf2: e.matmul(
            pool[hf2][:, 0:512], onesrow[0:1, 0:128],
            ln_a_bf[0:1, hf2 * 512:hf2 * 512 + 512], start=True, stop=True), "pe")
        pev = kb.op("tensor", lambda e, hf2=hf2: e.matmul(
            pool[2 + hf2][:, 0:512], onesrow[0:1, 0:128],
            ln_b_bf[0:1, hf2 * 512:hf2 * 512 + 512], start=True, stop=True), "pe")
    kb.wait("vector", "pe", pev)
    zld = [0, 0]
    xov = [0, 0]
    kb.wait("sync", "zst0", kb.val("zst0"))
    kb.wait("sync", "zst1", kb.val("zst1"))
    for ch in range(16):
        bi = ch % 2
        kb.wait("sync", "dv", zld[bi])
        zv = kb.dma("sync", zc[bi][:], z_dram[ch * 128:(ch + 1) * 128, :], sem=f"zld{bi}")
        kb.wait("vector", f"zld{bi}", zv)
        kb.wait("vector", f"xo{bi}", xov[bi])
        dvv = None
        for hf2 in range(2):
            kb.op("vector", lambda e, bi=bi, hf2=hf2: e.tensor_tensor(
                xo[bi][:, hf2 * 512:hf2 * 512 + 512], zc[bi][:, hf2 * 512:hf2 * 512 + 512],
                pool[hf2][:, 0:512], op=OP.mult), "dv")
            dvv = kb.op("vector", lambda e, bi=bi, hf2=hf2: e.tensor_tensor(
                xo[bi][:, hf2 * 512:hf2 * 512 + 512], xo[bi][:, hf2 * 512:hf2 * 512 + 512],
                pool[2 + hf2][:, 0:512], op=OP.add), "dv")
        zld[bi] = dvv
        kb.wait("sync", "dv", dvv)
        xov[bi] = kb.dma("sync", x1[ch * 128:(ch + 1) * 128, :], xo[bi][:], sem=f"xo{bi}")
    kb.wait("sync", "xo0", xov[0])
    kb.wait("sync", "xo1", xov[1])
    if standalone:
        return kb.finalize()



def build_l0_ffn(kb=None, dbg=False):
    standalone = kb is None
    if standalone:
        kb = KB()
    nc = kb.nc
    E, G, LAT, INTER, SHI = 32, 4, 1024, 1024, 4096
    x = kb.input("x", [D, TC], F32)
    gateT = kb.input("gateT", [D, E], F32)
    gbias = kb.input("gbias", [1, E], F32)
    fc1T = kb.input("fc1T", [D, LAT], F32)
    w1T = kb.input("w1T", [E, LAT, INTER], F32)
    w2T = kb.input("w2T", [E, INTER, LAT], F32)
    fc2T = kb.input("fc2T", [LAT, D], F32)
    shupT = kb.input("shupT", [D, SHI], F32)
    shdnT = kb.input("shdnT", [SHI, D], F32)
    x2 = kb.output("x2", [D, TC], F32)
    z_dram = kb.dram("zm_dram", [D, TC], F32)
    y0d = kb.output("y0d", [128, 8 * TC], F32) if dbg else None
    h0d = kb.output("h0d", [128, 8 * TC], BF16) if dbg else None
    cw0d = kb.output("cw0d", [128, 8 * E], F32) if dbg else None
    w2d = kb.output("w2d", [128, 8 * 1024], BF16) if dbg else None
    psd = kb.output("psd", [128, 512], F32) if dbg else None
    ysd1 = kb.output("ysd1", [128, 512], F32) if dbg else None
    yse0 = kb.output("yse0", [128, 8 * TC], F32) if dbg else None
    s1d = kb.output("s1d", [128, 32 * 512], BF16) if dbg else None
    zd0 = kb.output("zd0", [128, 512], F32) if dbg else None
    pad = kb.output("pad", [128, 512], F32) if dbg else None
    yse1 = kb.output("yse1", [128, 8 * TC], F32) if dbg else None

    x_bf = kb.sb("x_bf", [128, 16 * TC], BF16)
    x_lat = kb.sb("x_lat", [128, 8 * TC], BF16)
    y_sb = kb.sb("y_sb", [128, 8 * TC], F32)
    hsb = kb.sb("hsb", [128, 8 * TC], BF16)
    cw_all = kb.sb("cw_all", [128, 8 * E], F32)
    gate_sb = kb.sb("gate_sb", [128, 16 * E], BF16)
    glog = kb.sb("glog", [128, 8 * E], F32)
    glog_bf = kb.sb("glog_bf", [32, TC], F32)
    idn32 = kb.sb("idn32", [32, 32], F32)
    onesq32 = kb.sb("onesq32", [32, 32], F32)
    gb_bc = kb.sb("gb_bc", [128, E], F32)
    gb_bf = kb.sb("gb_bf", [1, E], BF16)
    onesrow = kb.sb("onesrow", [1, 512], BF16)
    onescol = kb.sb("onescol", [128, 1], BF16)
    idn = kb.sb("idn", [128, 128], BF16)
    onesq = kb.sb("onesq", [128, 128], BF16)
    cb = kb.sb("cb", [128, 2], F32)
    scr_s = kb.sb("scr_s", [128, E], F32)
    sbv = kb.sb("sbv", [128, E], F32)
    gs = kb.sb("gs", [128, G], F32)
    gwk = kb.sb("gwk", [128, 8], F32)
    gme = kb.sb("gme", [128, E], F32)
    mskd = kb.sb("mskd", [128, E], F32)
    sel = kb.sb("sel", [128, E], F32)
    eqt = kb.sb("eqt", [128, E], F32)
    mcol = kb.sb("mcol", [128, 1], F32)
    m2col = kb.sb("m2col", [128, 1], F32)
    rcol = kb.sb("rcol", [128, 1], F32)
    xr = [kb.sb(f"xr{i}", [128, 512], F32) for i in range(2)]
    zs = [kb.sb(f"zs{i}", [128, 512], F32) for i in range(2)]
    zbf = kb.sb("zbf", [128, 512], BF16)
    zsq = kb.sb("zsq", [128, 512], BF16)
    ln_rows = kb.sb("ln_rows", [1, 2 * TC], F32)
    ln_b = kb.sb("ln_b", [1, TC], F32)
    ln_a_bf = kb.sb("ln_a_bf", [1, TC], BF16)
    ln_b_bf = kb.sb("ln_b_bf", [1, TC], BF16)
    gfl = kb.sb("gfl", [128, 16 * 32], F32)
    glo = kb.sb("glo", [128, 16 * 32], BF16)
    xfc = [kb.sb("xfc0", [128, 512], F32)] * 2
    xlo = [kb.sb("xlo0", [128, 512], BF16)] * 2
    arW = kb.arena()
    wstrip = [arW.sbuf_tensor(f"wstrip{i}", [128, 8 * 1024], BF16) for i in range(2)]
    arW.allocate()

    pool = [kb.ps(f"pool{i}", [128, 512], F32) for i in range(6)]
    pt = kb.ps("pt", [128, 1024], BF16)

    xg = kb.dma("gpsimd", x_bf.ap().rearrange("p (c t) -> p c t", c=16),
                x.rearrange("(c p) t -> p c t", p=128), sem="xbf")
    gldv = kb.dma("gpsimd", gate_sb.ap().rearrange("p (c e) -> p c e", c=16),
                  gateT.rearrange("(c p) e -> p c e", p=128), sem="gld")
    gbv = kb.dma("gpsimd", gb_bf[:], gbias, sem="gbl")
    kb.op("vector", lambda e: e.memset(onesrow[:], 1.0))
    kb.op("vector", lambda e: e.memset(onescol[:], 1.0))
    kb.op("vector", lambda e: e.memset(onesq[:], 1.0))
    dvv = kb.op("vector", lambda e: e.memset(cb[:, 0:1], LN_EPS), "dv")
    for tt in range(8):
        dvv = kb.op("vector", lambda e, tt=tt: e.memset(y_sb[:, tt * LAT:(tt + 1) * LAT], 0.0), "dv")
    kb.wait("gpsimd", "dv", dvv)
    if dbg:
        kb.wait("sync", "dv", dvv)
        ymv = kb.dma("sync", y0d, y_sb[:], sem="dbgm")
        kb.wait("vector", "dbgm", ymv)
    idv = kb.op("gpsimd", lambda e: e.affine_select(idn[:], onesq[:], pattern=[[1, 128]],
                                                    compare_op=OP.is_equal, fill=0.0,
                                                    base=0, channel_multiplier=-1), "po")
    kb.op("vector", lambda e: e.memset(onesq32[:], 1.0), "dv")
    kb.wait("gpsimd", "dv", kb.val("dv"))
    idv = kb.op("gpsimd", lambda e: e.affine_select(idn32[:], onesq32[:], pattern=[[1, 32]],
                                                    compare_op=OP.is_equal, fill=0.0,
                                                    base=0, channel_multiplier=-1), "po")
    kb.wait("tensor", "po", idv)
    kb.wait("tensor", "xbf", xg)
    kb.wait("tensor", "gld", gldv)
    kb.wait("tensor", "gbl", gbv)

    gfv = kb.dma("sync", gfl.ap().rearrange("p (c e) -> p c e", c=16),
                 gateT.rearrange("(c p) e -> p c e", p=128), sem="gf")
    kb.wait("vector", "gf", gfv)
    kb.wait("vector", "gld", gldv)
    kb.wait("vector", "xbf", xg)
    dvv = kb.op("vector", lambda e: e.tensor_tensor(glo[:], gfl[:], gate_sb[:],
                                                    op=OP.subtract), "dv")
    kb.wait("tensor", "dv", dvv)
    pv = None
    xlov = [0, 0]
    xlo_pe = [0, 0]
    for th in range(2):
        pi = th % 2
        for kc in range(16):
            xbi = 0
            kb.wait("sync", "dv", xlov[xbi])
            kb.wait("sync", "pe", xlo_pe[xbi])
            xcv = kb.dma("sync", xfc[xbi][:],
                         x[kc * 128:(kc + 1) * 128, th * 512:(th + 1) * 512], sem=f"xf{xbi}")
            kb.wait("vector", f"xf{xbi}", xcv)
            kb.wait("vector", "pe", xlo_pe[xbi])
            xlov[xbi] = kb.op("vector", lambda e, kc=kc, th=th, xbi=xbi: e.tensor_tensor(
                xlo[xbi][:], xfc[xbi][:],
                x_bf[:, kc * TC + th * 512: kc * TC + th * 512 + 512], op=OP.subtract), "dv")
            kb.wait("tensor", "dv", xlov[xbi])
            kb.op("tensor", lambda e, th=th, kc=kc, pi=pi: e.matmul(
                pool[pi][0:32, 0:512], gate_sb[:, kc * E:(kc + 1) * E],
                x_bf[:, kc * TC + th * 512: kc * TC + th * 512 + 512],
                start=(kc == 0), stop=False), "pe")
            xlo_pe[xbi] = kb.op("tensor", lambda e, th=th, kc=kc, pi=pi, xbi=xbi: e.matmul(
                pool[pi][0:32, 0:512], gate_sb[:, kc * E:(kc + 1) * E],
                xlo[xbi][:], start=False, stop=False), "pe")
            pv = kb.op("tensor", lambda e, th=th, kc=kc, pi=pi: e.matmul(
                pool[pi][0:32, 0:512], glo[:, kc * E:(kc + 1) * E],
                x_bf[:, kc * TC + th * 512: kc * TC + th * 512 + 512],
                start=False, stop=(kc == 15)), "pe")
        kb.wait("scalar", "pe", pv)
        kb.op("scalar", lambda e, th=th, pi=pi: e.activation(
            glog_bf[0:32, th * 512:(th + 1) * 512], pool[pi][0:32, 0:512],
            AF.Identity, bias=0.0, scale=1.0), "ac")
    acv = kb.val("ac")
    kb.wait("tensor", "ac", acv)
    pv = kb.op("tensor", lambda e: e.matmul(
        pool[2][:, 0:E], onesrow[0:1, 0:128], gb_bf[0:1, 0:E], start=True, stop=True), "pe")
    kb.wait("vector", "pe", pv)
    dvv = kb.op("vector", lambda e: e.tensor_copy(gb_bc[:], pool[2][:, 0:E]), "dv")
    ptfree = 0
    for tt in range(8):
        kb.wait("tensor", "dv", ptfree)
        pv = kb.op("tensor", lambda e, tt=tt: e.transpose(
            pool[3][:, 0:32], glog_bf[0:32, tt * 128:(tt + 1) * 128], idn32[:]), "pe")
        kb.wait("vector", "pe", pv)
        ptfree = kb.op("vector", lambda e, tt=tt: e.tensor_copy(
            glog[:, tt * E:(tt + 1) * E], pool[3][:, 0:32]), "dv")

    NEG = -1.0e30
    kb.wait("scalar", "dv", kb.val("dv"))
    prev_dv = 0
    for tt in range(8):
        kb.wait("scalar", "dv", prev_dv)
        acv = kb.op("scalar", lambda e, tt=tt: e.activation(
            scr_s[:], glog[:, tt * E:(tt + 1) * E], AF.Sigmoid, bias=0.0, scale=1.0), "ac")
        kb.wait("vector", "ac", acv)
        kb.op("vector", lambda e: e.tensor_tensor(sbv[:], scr_s[:], gb_bc[:], op=OP.add), "dv")
        for g in range(G):
            kb.op("vector", lambda e, g=g: e.reduce_max(mcol[:], sbv[:, g * 8:(g + 1) * 8],
                                                        axis=AX.X), "dv")
            kb.op("vector", lambda e, g=g: e.tensor_scalar(gwk[:, 0:8], sbv[:, g * 8:(g + 1) * 8],
                                                           mcol[:], None, op0=OP.is_equal), "dv")
            kb.op("vector", lambda e, g=g: e.scalar_tensor_tensor(
                gwk[:, 0:8], gwk[:, 0:8], NEG, sbv[:, g * 8:(g + 1) * 8],
                op0=OP.mult, op1=OP.add), "dv")
            kb.op("vector", lambda e, g=g: e.reduce_max(m2col[:], gwk[:, 0:8], axis=AX.X), "dv")
            kb.op("vector", lambda e, g=g: e.tensor_tensor(gs[:, g:g + 1], mcol[:], m2col[:],
                                                           op=OP.add), "dv")
        kb.op("vector", lambda e: e.reduce_max(mcol[:], gs[:], axis=AX.X), "dv")
        kb.op("vector", lambda e: e.tensor_scalar(gwk[:, 0:G], gs[:], mcol[:], None,
                                                  op0=OP.is_equal), "dv")
        kb.op("vector", lambda e: e.scalar_tensor_tensor(gwk[:, 0:G], gwk[:, 0:G], NEG, gs[:],
                                                         op0=OP.mult, op1=OP.add), "dv")
        kb.op("vector", lambda e: e.reduce_max(m2col[:], gwk[:, 0:G], axis=AX.X), "dv")
        for g in range(G):
            kb.op("vector", lambda e, g=g: e.tensor_scalar(
                mcol[:], gs[:, g:g + 1], m2col[:], None, op0=OP.is_ge), "dv")
            kb.op("vector", lambda e, g=g: e.tensor_scalar(
                gme[:, g * 8:(g + 1) * 8], onesq[:, 0:8].bitcast(F32) if False else gme[:, g * 8:(g + 1) * 8],
                mcol[:], None, op0=OP.mult), "dv") if False else None
            kb.op("vector", lambda e, g=g: e.memset(gme[:, g * 8:(g + 1) * 8], 1.0), "dv")
            kb.op("vector", lambda e, g=g: e.tensor_scalar(
                gme[:, g * 8:(g + 1) * 8], gme[:, g * 8:(g + 1) * 8], mcol[:], None,
                op0=OP.mult), "dv")
        kb.op("vector", lambda e: e.tensor_scalar(gme[:], gme[:], 1.0, NEG,
                                                  op0=OP.subtract, op1=OP.mult), "dv")
        kb.op("vector", lambda e: e.tensor_tensor(mskd[:], sbv[:], gme[:], op=OP.subtract), "dv")
        kb.op("vector", lambda e: e.memset(sel[:], 0.0), "dv")
        for it in range(8):
            kb.op("vector", lambda e: e.reduce_max(mcol[:], mskd[:], axis=AX.X), "dv")
            kb.op("vector", lambda e: e.tensor_scalar(eqt[:], mskd[:], mcol[:], None,
                                                      op0=OP.is_equal), "dv")
            kb.op("vector", lambda e: e.tensor_tensor(sel[:], sel[:], eqt[:], op=OP.max), "dv")
            kb.op("vector", lambda e: e.scalar_tensor_tensor(mskd[:], eqt[:], NEG, mskd[:],
                                                             op0=OP.mult, op1=OP.add), "dv")
        kb.op("vector", lambda e: e.tensor_tensor(eqt[:], scr_s[:], sel[:], op=OP.mult), "dv")
        kb.op("vector", lambda e: e.reduce_sum(mcol[:], eqt[:], axis=AX.X), "dv")
        kb.op("vector", lambda e: e.reciprocal(rcol[:], mcol[:]), "dv")
        prev_dv = kb.op("vector", lambda e, tt=tt: e.tensor_scalar(
            cw_all[:, tt * E:(tt + 1) * E], eqt[:], rcol[:], 2.5,
            op0=OP.mult, op1=OP.mult), "dv")
    cw_done = prev_dv

    kb.wait("tensor", "dv", ptfree)
    wfree = [0, 0]
    psfree = [0, 0, 0, 0]
    for blk in range(2):
        bi = blk % 2
        kb.wait("gpsimd", "pe", wfree[bi])
        wld = kb.dma("gpsimd", wstrip[bi].ap().rearrange("p (c d) -> p c d", c=16)[:, :, 0:512],
                     fc1T[:, blk * 512:(blk + 1) * 512].rearrange("(c p) d -> p c d", p=128),
                     sem=f"w{bi}")
        kb.wait("tensor", f"w{bi}", wld)
        for sub in range(4):
            lc = blk * 4 + sub
            for th in range(2):
                pi = (sub * 2 + th) % 4
                kb.wait("tensor", "ac", psfree[pi])
                pv = None
                for kc in range(16):
                    pv = kb.op("tensor", lambda e, bi=bi, sub=sub, th=th, kc=kc, pi=pi: e.matmul(
                        pool[pi][:, 0:512],
                        wstrip[bi][:, kc * 512 + sub * 128: kc * 512 + sub * 128 + 128],
                        x_bf[:, kc * TC + th * 512: kc * TC + th * 512 + 512],
                        start=(kc == 0), stop=(kc == 15)), "pe")
                kb.wait("scalar", "pe", pv)
                psfree[pi] = kb.op("scalar", lambda e, lc=lc, th=th, pi=pi: e.activation(
                    x_lat[:, lc * TC + th * 512: lc * TC + th * 512 + 512],
                    pool[pi][:, 0:512], AF.Identity, bias=0.0, scale=1.0), "ac")
        wfree[bi] = pv
    xlat_done = kb.val("ac")

    kb.wait("tensor", "ac", xlat_done)
    kb.wait("vector", "dv", cw_done)
    hfree = 0
    yfree = [0, 0]
    for ee in range(E):
        bi = ee % 2
        kb.wait("gpsimd", "pe", wfree[bi])
        wld = kb.dma("gpsimd", wstrip[bi].ap().rearrange("p (c d) -> p c d", c=8),
                     w1T[ee].rearrange("(c p) d -> p c d", p=128), sem=f"w{bi}")
        kb.wait("tensor", f"w{bi}", wld)
        sqv = None
        sq_guard = 0
        for ic in range(8):
            for th in range(2):
                pi = th % 2
                kb.wait("tensor", "ac", psfree[pi])
                kb.wait("tensor", "dv", sq_guard)
                pv = None
                for lc in range(8):
                    pv = kb.op("tensor", lambda e, bi=bi, ic=ic, th=th, lc=lc, pi=pi: e.matmul(
                        pool[pi][:, 0:512],
                        wstrip[bi][:, lc * 1024 + ic * 128: lc * 1024 + ic * 128 + 128],
                        x_lat[:, lc * TC + th * 512: lc * TC + th * 512 + 512],
                        start=(lc == 0), stop=(lc == 7)), "pe")
                kb.wait("scalar", "pe", pv)
                psfree[pi] = kb.op("scalar", lambda e, ic=ic, th=th, pi=pi: e.activation(
                    hsb[:, ic * TC + th * 512: ic * TC + th * 512 + 512],
                    pool[pi][:, 0:512], AF.Relu, bias=0.0, scale=1.0), "ac")
            kb.wait("vector", "ac", psfree[1])
            sqv = kb.op("vector", lambda e, ic=ic: e.tensor_tensor(
                hsb[:, ic * TC:(ic + 1) * TC], hsb[:, ic * TC:(ic + 1) * TC],
                hsb[:, ic * TC:(ic + 1) * TC], op=OP.mult), "dv")
        bi2 = (ee + 1) % 2
        kb.wait("gpsimd", "pe", wfree[bi2])
        wld2 = kb.dma("gpsimd", wstrip[bi2].ap().rearrange("p (c d) -> p c d", c=8),
                      w2T[ee].rearrange("(c p) d -> p c d", p=128), sem=f"w{bi2}")
        kb.wait("tensor", f"w{bi2}", wld2)
        if dbg and ee == 31:
            kb.wait("sync", f"w{bi2}", wld2)
            w2v = kb.dma("sync", w2d, wstrip[bi2][:], sem="dbgw")
            kb.wait("gpsimd", "dbgw", w2v)
        kb.wait("tensor", "dv", sqv)
        pv = None
        for ts2 in range(8):
            for lh in range(2):
                pi = 4 + (ts2 * 2 + lh) % 2
                kb.wait("tensor", "dv", yfree[pi - 4])
                for ic in range(8):
                    pv = kb.op("tensor", lambda e, bi2=bi2, ts2=ts2, lh=lh, ic=ic, pi=pi: e.matmul(
                        pool[pi][:, 0:512],
                        hsb[:, ic * TC + ts2 * 128: ic * TC + ts2 * 128 + 128],
                        wstrip[bi2][:, ic * 1024 + lh * 512: ic * 1024 + lh * 512 + 512],
                        start=(ic == 0), stop=(ic == 7)), "pe")
                kb.wait("vector", "pe", pv)
                if dbg and ee == 0 and ts2 == 0 and lh == 0:
                    cpv = kb.op("vector", lambda e, pi=pi: e.tensor_copy(zs[0][:], pool[pi][:, 0:512]), "dv")
                    kb.wait("sync", "dv", cpv)
                    pdv = kb.dma("sync", psd, zs[0][:], sem="dbgp")
                    kb.wait("sync", "dbgp", pdv)
                    kb.wait("vector", "dbgp", pdv)
                yfree[pi - 4] = kb.op("vector", lambda e, ee=ee, ts2=ts2, lh=lh, pi=pi: e.scalar_tensor_tensor(
                    y_sb[:, ts2 * LAT + lh * 512: ts2 * LAT + lh * 512 + 512],
                    pool[pi][:, 0:512], cw_all[:, ts2 * E + ee: ts2 * E + ee + 1],
                    y_sb[:, ts2 * LAT + lh * 512: ts2 * LAT + lh * 512 + 512],
                    op0=OP.mult, op1=OP.add), "dv")
                if dbg and ee == 0 and ts2 == 0 and lh == 0:
                    kb.wait("sync", "dv", yfree[pi - 4])
                    y1v = kb.dma("sync", ysd1, y_sb[:, 0:512], sem="dbgy")
                    kb.wait("sync", "dbgy", y1v)
                    kb.wait("vector", "dbgy", y1v)
        wfree[bi2] = pv
        hfree = pv

    if dbg:
        cwd = kb.output("cwd", [128, 8 * E], F32)
        xld = kb.output("xld", [128, 8 * TC], BF16)
        ysd = kb.output("ysd", [128, 8 * TC], F32)
        gld2 = kb.output("gld2", [128, 8 * E], F32)
        kb.wait("sync", "dv", kb.val("dv"))
        kb.wait("sync", "ac", kb.val("ac"))
        hd = kb.output("hd", [128, 8 * TC], BF16)
        kb.dma("sync", hd, hsb[:], sem="dbg")
        kb.dma("sync", cwd, cw_all[:], sem="dbg")
        kb.dma("sync", xld, x_lat[:], sem="dbg")
        kb.dma("sync", ysd, y_sb[:], sem="dbg")
        dgv = kb.dma("sync", gld2, glog[:], sem="dbg")
        kb.wait("sync", "dbg", dgv)
        kb.wait("vector", "dbg", dgv)

    ydone = kb.val("dv")
    kb.wait("vector", "dv", ydone)
    kb.wait("vector", "pe", hfree)
    ptfree = 0
    ycast = 0
    for ts2 in range(8):
        dvv = kb.op("vector", lambda e, ts2=ts2: e.tensor_copy(
            x_lat[:, ts2 * 1024: ts2 * 1024 + 1024],
            y_sb[:, ts2 * LAT:(ts2 + 1) * LAT]), "dv")
        kb.wait("tensor", "dv", dvv)
        kb.wait("tensor", "ac", ptfree)
        pv = None
        for lc8 in range(8):
            pv = kb.op("tensor", lambda e, ts2=ts2, lc8=lc8: e.transpose(
                pt[:, lc8 * 128:(lc8 + 1) * 128],
                x_lat[:, ts2 * 1024 + lc8 * 128: ts2 * 1024 + lc8 * 128 + 128], idn[:]), "pe")
        kb.wait("scalar", "pe", pv)
        ptfree = kb.op("scalar", lambda e, ts2=ts2: e.activation(
            hsb.ap().rearrange("p (c t) -> p c t", c=8)[:, :, ts2 * 128:(ts2 + 1) * 128],
            pt.ap().rearrange("p (c t) -> p c t", c=8), AF.Identity,
            bias=0.0, scale=1.0), "ac")
    yfm_done = ptfree

    arS1 = kb.arena()
    s1 = arS1.sbuf_tensor("s1", [128, 32 * 512], BF16)
    fcs = [arS1.sbuf_tensor(f"fcs{i}", [128, 1024], BF16) for i in range(2)]
    arS1.allocate()
    kb.wait("tensor", "ac", yfm_done)
    statv = None
    zstv = [0, 0]
    xrfree = [0, 0]
    zsfree = [0, 0]
    lnx_cons = 0
    s1_cons = 0
    for th in range(2):
        kb.wait("vector", "ac", s1_cons)
        sq_last = None
        for blk8 in range(8):
            bi = blk8 % 2
            kb.wait("gpsimd", "pe", wfree[bi])
            wld = kb.dma("gpsimd", wstrip[bi].ap().rearrange("p (c d) -> p c d", c=16)[:, :, 0:512],
                         shupT[:, blk8 * 512:(blk8 + 1) * 512].rearrange("(c p) d -> p c d", p=128),
                         sem=f"w{bi}")
            kb.wait("tensor", f"w{bi}", wld)
            for sub in range(4):
                sc = blk8 * 4 + sub
                pi = sub % 2
                kb.wait("tensor", "ac", psfree[pi])
                pv = None
                for kc in range(16):
                    pv = kb.op("tensor", lambda e, bi=bi, sub=sub, th=th, kc=kc, pi=pi: e.matmul(
                        pool[pi][:, 0:512],
                        wstrip[bi][:, kc * 512 + sub * 128: kc * 512 + sub * 128 + 128],
                        x_bf[:, kc * TC + th * 512: kc * TC + th * 512 + 512],
                        start=(kc == 0), stop=(kc == 15)), "pe")
                kb.wait("scalar", "pe", pv)
                psfree[pi] = kb.op("scalar", lambda e, sc=sc, pi=pi: e.activation(
                    s1[:, sc * 512:(sc + 1) * 512], pool[pi][:, 0:512],
                    AF.Relu, bias=0.0, scale=1.0), "ac")
            wfree[bi] = pv
        rlv = kb.val("ac")
        kb.wait("vector", "ac", rlv)
        for sc in range(32):
            sq_last = kb.op("vector", lambda e, sc=sc: e.tensor_tensor(
                s1[:, sc * 512:(sc + 1) * 512], s1[:, sc * 512:(sc + 1) * 512],
                s1[:, sc * 512:(sc + 1) * 512], op=OP.mult), "dv")
        kb.wait("tensor", "dv", sq_last)
        if dbg and th == 0:
            kb.wait("sync", "dv", kb.val("dv"))
            sdv = kb.dma("sync", s1d, s1[:], sem="dbgs")
            kb.wait("sync", "dbgs", sdv)
            kb.wait("vector", "dbgs", sdv)
            kb.wait("scalar", "dbgs", sdv)
        for dc in range(16):
            bi = dc % 2
            kb.wait("gpsimd", "pe", wfree[bi])
            wld = kb.dma("gpsimd",
                         wstrip[bi][:, 0:4096].rearrange("p (c d) -> p c d", c=32),
                         shdnT[:, dc * 128:(dc + 1) * 128].rearrange("(c p) d -> p c d", p=128),
                         sem=f"w{bi}")
            kb.wait("gpsimd", "ac", kb.val("ac") if th == 0 and dc < 2 else 0)
            fv = kb.dma("gpsimd", fcs[bi].ap().rearrange("p (c d) -> p c d", c=8),
                        fc2T[:, dc * 128:(dc + 1) * 128].rearrange("(c p) d -> p c d", p=128),
                        sem=f"fc{bi}")
            kb.wait("tensor", f"w{bi}", wld)
            kb.wait("tensor", f"fc{bi}", fv)
            kb.wait("tensor", "dv", zsfree[0 if dc % 2 == 0 else 1])
            pvA = None
            for lc in range(8):
                pvA = kb.op("tensor", lambda e, bi=bi, lc=lc, th=th: e.matmul(
                    pool[4][:, 0:512], fcs[bi][:, lc * 128:(lc + 1) * 128],
                    hsb[:, lc * TC + th * 512: lc * TC + th * 512 + 512],
                    start=(lc == 0), stop=(lc == 7)), "pe")
            pvB = None
            for sc in range(32):
                pvB = kb.op("tensor", lambda e, bi=bi, sc=sc, th=th: e.matmul(
                    pool[5][:, 0:512], wstrip[bi][:, sc * 128:(sc + 1) * 128],
                    s1[:, sc * 512:(sc + 1) * 512],
                    start=(sc == 0), stop=(sc == 31)), "pe")
            wfree[bi] = pvB
            if th == 1 and dc >= 14:
                s1_cons = 0
            xbi = dc % 2
            kb.wait("sync", "dv", xrfree[xbi])
            xrv = kb.dma("sync", xr[xbi][:],
                         x[dc * 128:(dc + 1) * 128, th * 512:(th + 1) * 512], sem=f"xr{xbi}")
            kb.wait("vector", "pe", pvB)
            kb.wait("vector", f"xr{xbi}", xrv)
            zi = dc % 2
            kb.wait("vector", f"zst{zi}", zstv[zi])
            kb.wait("vector", "pe", lnx_cons)
            kb.op("vector", lambda e, zi=zi, xbi=xbi: e.scalar_tensor_tensor(
                zs[zi][:], pool[4][:, 0:512], 1.0, xr[xbi][:], op0=OP.mult, op1=OP.add), "dv")
            zv = kb.op("vector", lambda e, zi=zi: e.tensor_tensor(
                zs[zi][:], zs[zi][:], pool[5][:, 0:512], op=OP.add), "dv")
            zsfree[zi] = zv
            xrfree[xbi] = zv
            if dbg and th == 0 and dc == 0:
                cpv = kb.op("vector", lambda e: e.tensor_copy(zbf[:], pool[4][:, 0:512]), "dv")
                kb.wait("sync", "dv", cpv)
                kb.dma("sync", pad, zbf.ap().bitcast(F32) if False else zs[1][:], sem="dbgt") if False else None
                cpv2 = kb.op("vector", lambda e: e.tensor_copy(zs[1][:], pool[4][:, 0:512]), "dv")
                kb.wait("sync", "dv", cpv2)
                pdv2 = kb.dma("sync", pad, zs[1][:], sem="dbgt")
                kb.wait("sync", "dbgt", pdv2)
                kb.wait("vector", "dbgt", pdv2)
                kb.wait("sync", "dv", zv)
                zdv = kb.dma("sync", zd0, zs[zi][:], sem="dbgt")
                kb.wait("sync", "dbgt", zdv)
                kb.wait("vector", "dbgt", zdv)
            zbv = kb.op("vector", lambda e, zi=zi: e.tensor_copy(zbf[:], zs[zi][:]), "dv")
            kb.wait("scalar", "dv", zbv)
            sqv = kb.op("scalar", lambda e: e.square(zsq[:], zbf[:]), "ac")
            kb.wait("tensor", "dv", zbv)
            kb.wait("tensor", "ac", sqv)
            pse_s = pool[0] if th == 0 else pool[1]
            pse_q = pool[2] if th == 0 else pool[3]
            kb.op("tensor", lambda e, dc=dc, pse_s=pse_s: e.matmul(
                pse_s[0:1, 0:512], onescol[:], zbf[:],
                start=(dc == 0), stop=(dc == 15)), "pe")
            statv = kb.op("tensor", lambda e, dc=dc, pse_q=pse_q: e.matmul(
                pse_q[0:1, 0:512], onescol[:], zsq[:],
                start=(dc == 0), stop=(dc == 15)), "pe")
            lnx_cons = statv
            kb.wait("sync", "dv", zv)
            zstv[zi] = kb.dma("sync",
                              z_dram[dc * 128:(dc + 1) * 128, th * 512:(th + 1) * 512],
                              zs[zi][:], sem=f"zst{zi}")

    kb.barrier()
    arS1.free()
    arZ = kb.arena()
    zc = [arZ.sbuf_tensor("zc0", [128, TC], F32)] * 2
    xo = [arZ.sbuf_tensor("xo0", [128, TC], F32)] * 2
    arZ.allocate()
    mean = lambda: ln_rows[0:1, 0:TC]
    s2 = lambda: ln_rows[0:1, TC:2 * TC]
    kb.wait("vector", "pe", statv)
    for th in range(2):
        kb.op("vector", lambda e, th=th: e.tensor_scalar(
            mean()[0:1, th * 512:th * 512 + 512], pool[th][0:1, 0:512],
            1.0 / D, None, op0=OP.mult), "dv")
        kb.op("vector", lambda e, th=th: e.tensor_scalar(
            s2()[0:1, th * 512:th * 512 + 512], pool[2 + th][0:1, 0:512],
            1.0 / D, None, op0=OP.mult), "dv")
    kb.op("vector", lambda e: e.tensor_tensor(ln_b[0:1, :], mean(), mean(), op=OP.mult), "dv")
    dvv = kb.op("vector", lambda e: e.tensor_tensor(s2(), s2(), ln_b[0:1, :],
                                                    op=OP.subtract), "dv")
    kb.wait("scalar", "dv", dvv)
    acv = kb.op("scalar", lambda e: e.activation(s2(), s2(), AF.Sqrt,
                                                 bias=cb[0:1, 0:1], scale=1.0), "ac")
    kb.wait("vector", "ac", acv)
    kb.op("vector", lambda e: e.reciprocal(s2(), s2()), "dv")
    kb.op("vector", lambda e: e.tensor_tensor(ln_b[0:1, :], mean(), s2(), op=OP.mult), "dv")
    kb.op("vector", lambda e: e.tensor_scalar(ln_b[0:1, :], ln_b[0:1, :], -1.0, None,
                                              op0=OP.mult), "dv")
    kb.op("vector", lambda e: e.tensor_copy(ln_a_bf[:], s2()), "dv")
    dvv = kb.op("vector", lambda e: e.tensor_copy(ln_b_bf[:], ln_b[0:1, :]), "dv")
    kb.wait("tensor", "dv", dvv)
    pev = None
    for hf in range(2):
        kb.op("tensor", lambda e, hf=hf: e.matmul(
            pool[hf][:, 0:512], onesrow[0:1, 0:128],
            ln_a_bf[0:1, hf * 512:hf * 512 + 512], start=True, stop=True), "pe")
        pev = kb.op("tensor", lambda e, hf=hf: e.matmul(
            pool[2 + hf][:, 0:512], onesrow[0:1, 0:128],
            ln_b_bf[0:1, hf * 512:hf * 512 + 512], start=True, stop=True), "pe")
    kb.wait("vector", "pe", pev)
    zld = [0, 0]
    xov = [0, 0]
    kb.wait("sync", "zst0", kb.val("zst0"))
    kb.wait("sync", "zst1", kb.val("zst1"))
    for ch in range(16):
        bi = 0
        kb.wait("sync", "dv", zld[bi])
        zv = kb.dma("sync", zc[bi][:], z_dram[ch * 128:(ch + 1) * 128, :], sem=f"zld{bi}")
        kb.wait("vector", f"zld{bi}", zv)
        kb.wait("vector", f"xo{bi}", xov[bi])
        dvv = None
        for hf in range(2):
            kb.op("vector", lambda e, bi=bi, hf=hf: e.tensor_tensor(
                xo[bi][:, hf * 512:hf * 512 + 512], zc[bi][:, hf * 512:hf * 512 + 512],
                pool[hf][:, 0:512], op=OP.mult), "dv")
            dvv = kb.op("vector", lambda e, bi=bi, hf=hf: e.tensor_tensor(
                xo[bi][:, hf * 512:hf * 512 + 512], xo[bi][:, hf * 512:hf * 512 + 512],
                pool[2 + hf][:, 0:512], op=OP.add), "dv")
        zld[bi] = dvv
        kb.wait("sync", "dv", dvv)
        xov[bi] = kb.dma("sync", x2[ch * 128:(ch + 1) * 128, :], xo[bi][:], sem=f"xo{bi}")
    kb.wait("sync", "xo0", xov[0])
    if standalone:
        return kb.finalize()



def _bf(a):
    a = np.asarray(a)
    return a.view(BF) if a.dtype == np.uint16 else a


def kernel_unfused(L0_attn_x, L0_attn_w_qkv, L0_attn_w_out, L0_ffn_gate_w, L0_ffn_gate_bias,
           L0_ffn_fc1_w, L0_ffn_fc2_w, L0_ffn_w1, L0_ffn_w2, L0_ffn_shared_up_w,
           L0_ffn_shared_down_w, L1_attn_Wq, L1_attn_bq, L1_attn_Wk, L1_attn_bk,
           L1_attn_Wv, L1_attn_bv, L1_ffn_w_up_q, L1_ffn_w_up_scale, L1_ffn_b_up,
           L1_ffn_w_up_lora_a, L1_ffn_w_up_lora_b, L1_ffn_w_down_q, L1_ffn_w_down_scale,
           L1_ffn_b_down, L1_ffn_w_down_lora_a, L1_ffn_w_down_lora_b):
    f32 = np.float32
    X = np.asarray(L0_attn_x, f32).reshape(S, D)
    Xfm = np.ascontiguousarray(X.T)
    w_qkv = np.asarray(L0_attn_w_qkv, f32)
    w_out = np.asarray(L0_attn_w_out, f32)
    cores = list(range(NC))

    maps = []
    for c in cores:
        maps.append({
            "x": Xfm,
            "xr": np.ascontiguousarray(Xfm[:, c * TC:(c + 1) * TC]),
            "wqT": np.ascontiguousarray(w_qkv[256 * c:256 * c + 256, :].T),
            "wkT": np.ascontiguousarray(w_qkv[D + 256 * c:D + 256 * c + 256, :].T),
            "wvT": np.ascontiguousarray(w_qkv[2 * D + 256 * c:2 * D + 256 * c + 256, :].T),
            "woT": np.ascontiguousarray(w_out[:, 256 * c:256 * c + 256].T),
            "hb2": np.array([[2.0 * c + 1], [2.0 * c + 2]], f32),
        })
    r = run_bass_kernel_spmd(build_l0_attn(), maps, core_ids=cores)
    x1 = [np.asarray(r.results[c]["x1"], f32) for c in cores]

    common = {
        "gateT": np.ascontiguousarray(np.asarray(L0_ffn_gate_w, f32).T),
        "gbias": np.asarray(L0_ffn_gate_bias, f32).reshape(1, 32),
        "fc1T": np.ascontiguousarray(np.asarray(L0_ffn_fc1_w, f32).T),
        "w1T": np.ascontiguousarray(np.asarray(L0_ffn_w1, f32).transpose(0, 2, 1)),
        "w2T": np.ascontiguousarray(np.asarray(L0_ffn_w2, f32).transpose(0, 2, 1)),
        "fc2T": np.ascontiguousarray(np.asarray(L0_ffn_fc2_w, f32).T),
        "shupT": np.ascontiguousarray(np.asarray(L0_ffn_shared_up_w, f32).T),
        "shdnT": np.ascontiguousarray(np.asarray(L0_ffn_shared_down_w, f32).T),
    }
    maps = [dict(common, x=x1[c]) for c in cores]
    r = run_bass_kernel_spmd(build_l0_ffn(), maps, core_ids=cores)
    x2 = [np.asarray(r.results[c]["x2"], f32) for c in cores]

    common = {
        "Wq": np.asarray(L1_attn_Wq, f32), "Wk": np.asarray(L1_attn_Wk, f32),
        "Wv": np.asarray(L1_attn_Wv, f32),
        "bqT": np.ascontiguousarray(np.asarray(L1_attn_bq, f32).reshape(16, 128).T),
        "bkT": np.ascontiguousarray(np.asarray(L1_attn_bk, f32).reshape(16, 128).T),
        "bv": np.asarray(L1_attn_bv, f32).reshape(1, D),
    }
    maps = [dict(common, x=x2[c]) for c in cores]
    r = run_bass_kernel_spmd(build_l1_qkv(), maps, core_ids=cores)
    q1 = [_bf(r.results[c]["q1"]) for c in cores]
    Kg = np.concatenate([_bf(r.results[c]["k1"]) for c in cores], 0)
    Vg = np.concatenate([_bf(r.results[c]["v1p"]) for c in cores], 0)

    maps = []
    for c in cores:
        maps.append({
            "x": x2[c], "q1": np.ascontiguousarray(q1[c]), "Kg": Kg, "Vg": Vg,
            "qbase": np.array([[TC * c]], f32),
        })
    r = run_bass_kernel_spmd(build_l1_attn(), maps, core_ids=cores)
    x3 = [np.asarray(r.results[c]["x3"], f32) for c in cores]

    common = {
        "upq": np.ascontiguousarray(np.asarray(L1_ffn_w_up_q, np.int32).T),
        "ups": np.ascontiguousarray(np.repeat(np.asarray(L1_ffn_w_up_scale, f32), 64, 1).T),
        "b_up": np.asarray(L1_ffn_b_up, f32).reshape(1, 8192),
        "upA": np.asarray(L1_ffn_w_up_lora_a, f32),
        "upB": np.asarray(L1_ffn_w_up_lora_b, f32),
        "dnq": np.ascontiguousarray(np.asarray(L1_ffn_w_down_q, np.int32).T),
        "dns": np.ascontiguousarray(np.repeat(np.asarray(L1_ffn_w_down_scale, f32), 64, 1).T),
        "b_dn": np.asarray(L1_ffn_b_down, f32).reshape(1, D),
        "dnA": np.asarray(L1_ffn_w_down_lora_a, f32),
        "dnB": np.asarray(L1_ffn_w_down_lora_b, f32),
    }
    maps = [dict(common, x=x3[c]) for c in cores]
    r = run_bass_kernel_spmd(build_l1_ffn(), maps, core_ids=cores)
    x4 = [np.asarray(r.results[c]["x4"], f32) for c in cores]

    out = np.concatenate([x4[c].T for c in cores], 0).reshape(1, S, D)
    return out



def build_fused():
    kb = KB()
    nc = kb.nc
    E, H = 32, 8192
    EI = lambda n, s, d: nc.dram_tensor(n, s, d, kind="ExternalInput").ap()
    xfm = EI("xfm", [D, S], F32)
    xr = EI("xr", [D, TC], F32)
    wqT = EI("wqT", [D, 256], F32)
    wkT = EI("wkT", [D, 256], F32)
    wvT = EI("wvT", [D, 256], F32)
    woT = EI("woT", [256, D], F32)
    hb2 = EI("hb2", [2, 1], F32)
    gateT = EI("gateT", [D, E], F32)
    gbias = EI("gbias", [1, E], F32)
    fc1T = EI("fc1T", [D, 1024], F32)
    w1T = EI("w1T", [E, 1024, 1024], F32)
    w2T = EI("w2T", [E, 1024, 1024], F32)
    fc2T = EI("fc2T", [1024, D], F32)
    shupT = EI("shupT", [D, 4096], F32)
    shdnT = EI("shdnT", [4096, D], F32)
    Wq = EI("Wq", [D, D], F32)
    Wk = EI("Wk", [D, D], F32)
    Wv = EI("Wv", [D, D], F32)
    bqT = EI("bqT", [128, 16], F32)
    bkT = EI("bkT", [128, 16], F32)
    bv = EI("bv", [1, D], F32)
    qbase = EI("qbase", [1, 1], F32)
    upq = EI("upq", [D, H], I32)
    ups = EI("ups", [D, H], F32)
    b_up = EI("b_up", [1, H], F32)
    upA = EI("upA", [D, 16], F32)
    upB = EI("upB", [16, H], F32)
    dnq = EI("dnq", [H, D], I32)
    dns = EI("dns", [H, D], F32)
    b_dn = EI("b_dn", [1, D], F32)
    dnA = EI("dnA", [H, 16], F32)
    dnB = EI("dnB", [16, D], F32)
    y = nc.dram_tensor("y", [D, TC], F32, kind="ExternalOutput").ap()

    x1_i = kb.dram("x1_i", [D, TC], F32)
    x2_i = kb.dram("x2_i", [D, TC], F32)
    x3_i = kb.dram("x3_i", [D, TC], F32)
    q1_i = kb.dram("q1_i", [D, TC], BF16)
    k1_i = kb.dram("k1_i", [D, TC], BF16)
    v1p_i = kb.dram("v1p_i", [TC, D + 1], BF16)
    Kg_i = kb.dram("Kg_i", [NC * D, TC], BF16)
    Vg_i = kb.dram("Vg_i", [S, D + 1], BF16)

    kb.begin_phase("pa_", {"x": xfm, "xr": xr, "wqT": wqT, "wkT": wkT, "wvT": wvT,
                           "woT": woT, "hb2": hb2, "x1": x1_i})
    build_l0_attn(kb)
    kb.end_phase()

    kb.begin_phase("pb_", {"x": x1_i, "gateT": gateT, "gbias": gbias, "fc1T": fc1T,
                           "w1T": w1T, "w2T": w2T, "fc2T": fc2T, "shupT": shupT,
                           "shdnT": shdnT, "x2": x2_i})
    build_l0_ffn(kb)
    kb.end_phase()

    kb.begin_phase("pc_", {"x": x2_i, "Wq": Wq, "Wk": Wk, "Wv": Wv, "bqT": bqT,
                           "bkT": bkT, "bv": bv, "q1": q1_i, "k1": k1_i, "v1p": v1p_i})
    build_l1_qkv(kb)
    kb.wait("gpsimd", "stk", kb.val("stk"))
    kb.op("gpsimd", lambda e: e.collective_compute(
        "AllGather", mybir.AluOpType.bypass, replica_groups=[list(range(NC))],
        ins=[k1_i.opt()], outs=[Kg_i.opt()]), "cc", amt=1)
    kb.wait("gpsimd", "stv", kb.val("stv"))
    kb.op("gpsimd", lambda e: e.collective_compute(
        "AllGather", mybir.AluOpType.bypass, replica_groups=[list(range(NC))],
        ins=[v1p_i.opt()], outs=[Vg_i.opt()]), "cc", amt=1)
    kb.end_phase()

    kb.begin_phase("pd_", {"x": x2_i, "q1": q1_i, "Kg": Kg_i, "Vg": Vg_i,
                           "qbase": qbase, "x3": x3_i})
    build_l1_attn(kb)
    kb.end_phase()

    kb.begin_phase("pe_", {"x": x3_i, "upq": upq, "ups": ups, "b_up": b_up,
                           "upA": upA, "upB": upB, "dnq": dnq, "dns": dns,
                           "b_dn": b_dn, "dnA": dnA, "dnB": dnB, "x4": y})
    build_l1_ffn(kb)
    kb.end_phase()
    return kb.finalize()


def kernel_fused(L0_attn_x, L0_attn_w_qkv, L0_attn_w_out, L0_ffn_gate_w,
                 L0_ffn_gate_bias, L0_ffn_fc1_w, L0_ffn_fc2_w, L0_ffn_w1, L0_ffn_w2,
                 L0_ffn_shared_up_w, L0_ffn_shared_down_w, L1_attn_Wq, L1_attn_bq,
                 L1_attn_Wk, L1_attn_bk, L1_attn_Wv, L1_attn_bv, L1_ffn_w_up_q,
                 L1_ffn_w_up_scale, L1_ffn_b_up, L1_ffn_w_up_lora_a, L1_ffn_w_up_lora_b,
                 L1_ffn_w_down_q, L1_ffn_w_down_scale, L1_ffn_b_down,
                 L1_ffn_w_down_lora_a, L1_ffn_w_down_lora_b):
    f32 = np.float32
    X = np.asarray(L0_attn_x, f32).reshape(S, D)
    Xfm = np.ascontiguousarray(X.T)
    w_qkv = np.asarray(L0_attn_w_qkv, f32)
    w_out = np.asarray(L0_attn_w_out, f32)
    cores = list(range(NC))
    common = {
        "xfm": Xfm,
        "gateT": np.ascontiguousarray(np.asarray(L0_ffn_gate_w, f32).T),
        "gbias": np.asarray(L0_ffn_gate_bias, f32).reshape(1, 32),
        "fc1T": np.ascontiguousarray(np.asarray(L0_ffn_fc1_w, f32).T),
        "w1T": np.ascontiguousarray(np.asarray(L0_ffn_w1, f32).transpose(0, 2, 1)),
        "w2T": np.ascontiguousarray(np.asarray(L0_ffn_w2, f32).transpose(0, 2, 1)),
        "fc2T": np.ascontiguousarray(np.asarray(L0_ffn_fc2_w, f32).T),
        "shupT": np.ascontiguousarray(np.asarray(L0_ffn_shared_up_w, f32).T),
        "shdnT": np.ascontiguousarray(np.asarray(L0_ffn_shared_down_w, f32).T),
        "Wq": np.asarray(L1_attn_Wq, f32), "Wk": np.asarray(L1_attn_Wk, f32),
        "Wv": np.asarray(L1_attn_Wv, f32),
        "bqT": np.ascontiguousarray(np.asarray(L1_attn_bq, f32).reshape(16, 128).T),
        "bkT": np.ascontiguousarray(np.asarray(L1_attn_bk, f32).reshape(16, 128).T),
        "bv": np.asarray(L1_attn_bv, f32).reshape(1, D),
        "upq": np.ascontiguousarray(np.asarray(L1_ffn_w_up_q, np.int32).T),
        "ups": np.ascontiguousarray(np.repeat(np.asarray(L1_ffn_w_up_scale, f32), 64, 1).T),
        "b_up": np.asarray(L1_ffn_b_up, f32).reshape(1, 8192),
        "upA": np.asarray(L1_ffn_w_up_lora_a, f32),
        "upB": np.asarray(L1_ffn_w_up_lora_b, f32),
        "dnq": np.ascontiguousarray(np.asarray(L1_ffn_w_down_q, np.int32).T),
        "dns": np.ascontiguousarray(np.repeat(np.asarray(L1_ffn_w_down_scale, f32), 64, 1).T),
        "b_dn": np.asarray(L1_ffn_b_down, f32).reshape(1, D),
        "dnA": np.asarray(L1_ffn_w_down_lora_a, f32),
        "dnB": np.asarray(L1_ffn_w_down_lora_b, f32),
    }
    maps = []
    for c in cores:
        maps.append(dict(
            common,
            xr=np.ascontiguousarray(Xfm[:, c * TC:(c + 1) * TC]),
            wqT=np.ascontiguousarray(w_qkv[256 * c:256 * c + 256, :].T),
            wkT=np.ascontiguousarray(w_qkv[D + 256 * c:D + 256 * c + 256, :].T),
            wvT=np.ascontiguousarray(w_qkv[2 * D + 256 * c:2 * D + 256 * c + 256, :].T),
            woT=np.ascontiguousarray(w_out[:, 256 * c:256 * c + 256].T),
            hb2=np.array([[2.0 * c + 1], [2.0 * c + 2]], f32),
            qbase=np.array([[TC * c]], f32),
        ))
    r = run_bass_kernel_spmd(build_fused(), maps, core_ids=cores)
    out = np.concatenate([np.asarray(r.results[c]["y"], f32).T for c in cores], 0)
    return out.reshape(1, S, D)


kernel = kernel_fused


# version 06
# speedup vs baseline: 1.0060x; 1.0060x over previous
import numpy as np
import ml_dtypes
import concourse.bass as bass
import concourse.mybir as mybir
from concourse.bass_utils import run_bass_kernel_spmd
from concourse.alu_op_type import AluOpType as OP
from contextlib import ExitStack

F32 = mybir.dt.float32
BF16 = mybir.dt.bfloat16
I32 = mybir.dt.int32
AF = mybir.ActivationFunctionType
AX = mybir.AxisListType
BF = ml_dtypes.bfloat16

NC = 8
S = 8192
D = 2048
TC = S // NC
NCH = D // 128

LN_EPS = 1e-5
L0_HEADS = 16
L0_DH = 128
L0_SCALE = 1.0 / np.sqrt(L0_DH)
ALIBI_MAX = 8.0
L1_SCALE = float(np.asarray(D, np.float32) ** (-0.5))
B0 = 24.0

ENGINES = ("sync", "scalar", "vector", "tensor", "gpsimd")


class KB:

    def __init__(self, name="k"):
        self.nc = bass.Bass("TRN2", target_bir_lowering=False)
        self.st = ExitStack()
        self.streams = {e: [] for e in ENGINES}
        self._sems = {}
        self.semv = {}

    def sem(self, name):
        if name not in self._sems:
            self._sems[name] = self.st.enter_context(self.nc.semaphore(name))
            self.semv[name] = 0
        return self._sems[name]

    def sb(self, name, shape, dt):
        name = getattr(self, "_prefix", "") + name
        st = getattr(self, "_phase_st", None) or self.st
        return st.enter_context(self.nc.sbuf_tensor(name, shape, dt))

    def ps(self, name, shape, dt=F32):
        if not hasattr(self, "_ps_cache"):
            self._ps_cache = {}
        if name not in self._ps_cache:
            self._ps_cache[name] = self.st.enter_context(self.nc.psum_tensor(name, shape, dt))
        return self._ps_cache[name]

    def input(self, name, shape, dt):
        io = getattr(self, "io", None)
        if io is not None and name in io:
            return io[name]
        return self.nc.dram_tensor(name, shape, dt, kind="ExternalInput").ap()

    def output(self, name, shape, dt):
        io = getattr(self, "io", None)
        if io is not None and name in io:
            return io[name]
        return self.nc.dram_tensor(name, shape, dt, kind="ExternalOutput").ap()

    def begin_phase(self, prefix, io):
        self._prefix = prefix
        self.io = io
        self._phase_st = ExitStack()
        self._arena_mark = len(getattr(self, "_arenas", []))

    def end_phase(self):
        self.full_barrier()
        for a in reversed(self._arenas[self._arena_mark:] if hasattr(self, "_arenas") else []):
            a.free()
        if hasattr(self, "_arenas"):
            del self._arenas[self._arena_mark:]
        self._phase_st.close()
        self._phase_st = None
        self._prefix = ""
        self.io = None

    def full_barrier(self):
        for eng in ENGINES:
            for s, v in list(self.semv.items()):
                if v > 0:
                    self.wait(eng, s, v)

    def dram(self, name, shape, dt, kind="Internal"):
        return self.nc.dram_tensor(name, shape, dt, kind=kind).ap()

    def arena(self):
        from concourse.bass import AutoArena
        a = AutoArena(self.nc)
        if not hasattr(self, "_arenas"):
            self._arenas = []
        self._arenas.append(a)
        return a

    def barrier(self, sems=("pe", "dv", "ac", "po")):
        for eng in ENGINES:
            for s in sems:
                self.wait(eng, s, self.val(s))

    def wait(self, eng, name, val):
        if val is None or val <= 0:
            return
        s = self.sem(name)
        self.streams[eng].append(lambda e, s=s, v=val: e.wait_ge(s, v))

    DEFSEM = {"vector": "dv", "scalar": "ac", "gpsimd": "po"}

    def op(self, eng, fn, sem=None, amt=1):
        d = self.DEFSEM.get(eng)
        if d is not None:
            if self.semv.get(d, 0) > 0:
                self.wait(eng, d, self.semv[d])
            if sem is None:
                sem, amt = d, 1
        if sem is None:
            self.streams[eng].append(fn)
            return None
        s = self.sem(sem)
        self.semv[sem] += amt
        v = self.semv[sem]

        def run(e, f=fn, s=s, a=amt):
            f(e).then_inc(s, a)

        self.streams[eng].append(run)
        return v

    def dma(self, q, out, in_, sem="ld", **kw):
        return self.op(q, lambda e, o=out, i=in_, k=kw: e.dma_start(out=o, in_=i, **k),
                       sem, 16)

    def val(self, name):
        return self.semv.get(name, 0)

    def finalize(self):
        block = self.st.enter_context(self.nc.Block())
        for eng in ENGINES:
            stream = self.streams[eng]
            if not stream:
                continue
            dec = getattr(block, eng)

            def body(e, st_=stream):
                for f in st_:
                    f(e)

            dec(body)
        for a in reversed(getattr(self, "_arenas", [])):
            a.free()
        self.st.close()
        return self.nc



def emit_const_tiles(kb):
    ones_row = kb.sb("c_ones_row", [2, 512], BF16)
    ones_col = kb.sb("c_ones_col", [128, 1], BF16)
    kb.op("vector", lambda e: e.memset(ones_row[:], 1.0), "dv")
    kb.op("vector", lambda e: e.memset(ones_col[:], 1.0), "dv")
    return ones_row, ones_col


def emit_ln_fm(kb, x_slices, t_w, name, sc, out_cb=None):
    nhalf = (t_w + 511) // 512
    pe_done = None
    for ch in range(16):
        xs = x_slices[ch]
        v = kb.op("vector", lambda e, ch=ch, xs=xs: e.tensor_copy(sc["xbf"][:], xs()), "dv")
        a = kb.op("scalar", lambda e, ch=ch: e.square(sc["xsq"][:], sc["xbf"][:]), "ac")
        kb.wait("tensor", "dv", v)
        kb.wait("tensor", "ac", a)
        for hf in range(nhalf):
            w = min(512, t_w - hf * 512)
            st = (ch == 0)
            sp = (ch == 15)
            kb.op("tensor", lambda e, ch=ch, hf=hf, w=w, st=st, sp=sp: e.matmul(
                sc["ps_sum"][0:1, hf * 512:hf * 512 + w],
                sc["onescol_bf"][:], sc["xbf"][:, hf * 512:hf * 512 + w],
                start=st, stop=sp), "pe")
            pe_done = kb.op("tensor", lambda e, ch=ch, hf=hf, w=w, st=st, sp=sp: e.matmul(
                sc["ps_sq"][0:1, hf * 512:hf * 512 + w],
                sc["onescol_bf"][:], sc["xsq"][:, hf * 512:hf * 512 + w],
                start=st, stop=sp), "pe")
    kb.wait("vector", "pe", pe_done)
    kb.op("vector", lambda e: e.tensor_scalar(sc["mean"][0:1, 0:t_w], sc["ps_sum"][0:1, 0:t_w],
                                              1.0 / D, None, op0=OP.mult), "dv")
    kb.op("vector", lambda e: e.tensor_scalar(sc["msq"][0:1, 0:t_w], sc["ps_sq"][0:1, 0:t_w],
                                              1.0 / D, None, op0=OP.mult), "dv")
    kb.op("vector", lambda e: e.tensor_tensor(sc["tmp"][0:1, 0:t_w], sc["mean"][0:1, 0:t_w],
                                              sc["mean"][0:1, 0:t_w], op=OP.mult), "dv")
    dvv = kb.op("vector", lambda e: e.tensor_tensor(sc["var"][0:1, 0:t_w], sc["msq"][0:1, 0:t_w],
                                                    sc["tmp"][0:1, 0:t_w], op=OP.subtract), "dv")
    kb.wait("scalar", "dv", dvv)
    acv = kb.op("scalar", lambda e: e.activation(sc["std"][0:1, 0:t_w], sc["var"][0:1, 0:t_w],
                                                 AF.Sqrt, bias=LN_EPS, scale=1.0), "ac")
    kb.wait("vector", "ac", acv)
    kb.op("vector", lambda e: e.reciprocal(sc["a_row"][0:1, 0:t_w], sc["std"][0:1, 0:t_w]), "dv")
    dvv = kb.op("vector", lambda e: e.tensor_tensor(sc["b_row"][0:1, 0:t_w], sc["mean"][0:1, 0:t_w],
                                                    sc["a_row"][0:1, 0:t_w], op=OP.mult), "dv")
    dvv = kb.op("vector", lambda e: e.tensor_scalar(sc["b_row"][0:1, 0:t_w], sc["b_row"][0:1, 0:t_w],
                                                    -1.0, None, op0=OP.mult), "dv")
    kb.op("vector", lambda e: e.tensor_copy(sc["a_row_bf"][0:1, 0:t_w], sc["a_row"][0:1, 0:t_w]), "dv")
    dvv = kb.op("vector", lambda e: e.tensor_copy(sc["b_row_bf"][0:1, 0:t_w], sc["b_row"][0:1, 0:t_w]), "dv")
    kb.wait("tensor", "dv", dvv)
    pev = None
    for hf in range(nhalf):
        w = min(512, t_w - hf * 512)
        kb.op("tensor", lambda e, hf=hf, w=w: e.matmul(
            sc["ps_a"][:, hf * 512:hf * 512 + w], sc["onesrow_bf"][0:1, 0:128],
            sc["a_row_bf"][0:1, hf * 512:hf * 512 + w], start=True, stop=True), "pe")
        pev = kb.op("tensor", lambda e, hf=hf, w=w: e.matmul(
            sc["ps_b"][:, hf * 512:hf * 512 + w], sc["onesrow_bf"][0:1, 0:128],
            sc["b_row_bf"][0:1, hf * 512:hf * 512 + w], start=True, stop=True), "pe")
    kb.wait("vector", "pe", pev)
    last = None
    for ch in range(16):
        xs = x_slices[ch]
        kb.op("vector", lambda e, ch=ch, xs=xs: e.tensor_tensor(
            sc["tmp128"][:, 0:t_w], xs(), sc["ps_a"][:, 0:t_w], op=OP.mult), "dv")
        if out_cb is None:
            last = kb.op("vector", lambda e, ch=ch, xs=xs: e.tensor_tensor(
                xs(), sc["tmp128"][:, 0:t_w], sc["ps_b"][:, 0:t_w], op=OP.add), "dv")
        else:
            last = out_cb(ch)
    return last


def emit_ln_scratch(kb, t_w):
    sc = {}
    sc["xbf"] = kb.sb("ln_xbf", [128, t_w], BF16)
    sc["xsq"] = kb.sb("ln_xsq", [128, t_w], BF16)
    sc["onescol_bf"] = kb.sb("ln_ones_col", [128, 1], BF16)
    sc["onesrow_bf"] = kb.sb("ln_ones_row", [1, 128], BF16)
    kb.op("vector", lambda e: e.memset(sc["onescol_bf"][:], 1.0), "dv")
    kb.op("vector", lambda e: e.memset(sc["onesrow_bf"][:], 1.0), "dv")
    for nm in ("mean", "msq", "tmp", "var", "std", "a_row", "b_row"):
        sc[nm] = kb.sb("ln_" + nm, [1, t_w], F32)
    sc["a_row_bf"] = kb.sb("ln_a_bf", [1, t_w], BF16)
    sc["b_row_bf"] = kb.sb("ln_b_bf", [1, t_w], BF16)
    sc["tmp128"] = kb.sb("ln_t128", [128, t_w], F32)
    sc["ps_sum"] = kb.ps("ln_ps_sum", [1, t_w], F32)
    sc["ps_sq"] = kb.ps("ln_ps_sq", [1, t_w], F32)
    sc["ps_a"] = kb.ps("ln_ps_a", [128, t_w], F32)
    sc["ps_b"] = kb.ps("ln_ps_b", [128, t_w], F32)
    return sc



def build_l1_qkv(kb=None):
    standalone = kb is None
    if standalone:
        kb = KB()
    nc = kb.nc
    x = kb.input("x", [D, TC], F32)
    Wq = kb.input("Wq", [D, D], F32)
    Wk = kb.input("Wk", [D, D], F32)
    Wv = kb.input("Wv", [D, D], F32)
    bqT = kb.input("bqT", [128, 16], F32)
    bkT = kb.input("bkT", [128, 16], F32)
    bv = kb.input("bv", [1, D], F32)
    q1 = kb.output("q1", [D, TC], BF16)
    k1 = kb.output("k1", [D, TC], BF16)
    v1p = kb.output("v1p", [TC, D + 1], BF16)

    x_bf = kb.sb("x_bf", [128, 16 * TC], BF16)
    wtile = [kb.sb(f"wt{i}", [128, 16 * 512], BF16) for i in range(2)]
    bq_sb = kb.sb("bq_sb", [128, 16], F32)
    bk_sb = kb.sb("bk_sb", [128, 16], F32)
    bv_bf = kb.sb("bv_bf", [1, D], BF16)
    bv_f32 = kb.sb("bv_f32", [1, D], F32)
    onesrow = kb.sb("onesrow", [1, 128], BF16)
    q_all = kb.sb("q_all", [128, 16 * TC], BF16)
    k_all = kb.sb("k_all", [128, 16 * TC], BF16)
    v_all = kb.sb("v_all", [128, 8 * (D + 1)], BF16)
    ps = [kb.ps(f"pool{i}", [128, 512], F32) for i in range(4)]

    xg = kb.dma("gpsimd", x_bf.ap().rearrange("p (c t) -> p c t", c=16),
                x.rearrange("(c p) t -> p c t", p=128), sem="wgx")
    kb.dma("sync", bq_sb[:], bqT, sem="ld")
    kb.dma("sync", bk_sb[:], bkT, sem="ld")
    kb.dma("sync", bv_f32[:], bv, sem="ld")
    ldv = kb.val("ld")
    kb.wait("vector", "ld", ldv)
    kb.wait("vector", "wgx", xg)
    kb.wait("tensor", "wgx", xg)
    kb.op("vector", lambda e: e.memset(onesrow[:], 1.0), "dv")
    kb.op("vector", lambda e: e.tensor_copy(bv_bf[:], bv_f32[:]), "dv")
    vav = kb.op("vector", lambda e: e.memset(
        v_all.ap().rearrange("p (t d) -> p t d", t=8)[:, :, D:D + 1], 1.0), "dv")
    dvv = kb.op("vector", lambda e: e.tensor_scalar(bq_sb[:], bq_sb[:], L1_SCALE, None,
                                                    op0=OP.mult), "dv")
    kb.wait("scalar", "dv", dvv)
    kb.wait("tensor", "dv", dvv)

    wfree = {0: 0, 1: 0}
    psfree = [0, 0, 0, 0]

    def qk_proj(Wap, bias_sb, out_all, scale):
        for blk in range(4):
            bi = blk % 2
            kb.wait("gpsimd", "pe", wfree[bi])
            wld = kb.dma("gpsimd",
                         wtile[bi].ap().rearrange("p (c d) -> p c d", c=16),
                         Wap[:, blk * 512:(blk + 1) * 512].rearrange("(c p) d -> p c d", p=128),
                         sem=f"wg{bi}")
            kb.wait("tensor", f"wg{bi}", wld)
            pv = None
            for sub in range(4):
                ch = blk * 4 + sub
                for hf in range(2):
                    pi = (sub * 2 + hf) % 4
                    kb.wait("tensor", "ac", psfree[pi])
                    for kc in range(16):
                        pv = kb.op("tensor", lambda e, bi=bi, sub=sub, hf=hf, kc=kc, pi=pi: e.matmul(
                            ps[pi][:, 0:512],
                            wtile[bi][:, kc * 512 + sub * 128: kc * 512 + sub * 128 + 128],
                            x_bf[:, kc * TC + hf * 512: kc * TC + hf * 512 + 512],
                            start=(kc == 0), stop=(kc == 15)), "pe")
                    kb.wait("scalar", "pe", pv)
                    av = kb.op("scalar", lambda e, ch=ch, hf=hf, pi=pi, bias_sb=bias_sb,
                               scale=scale, out_all=out_all: e.activation(
                        out_all[:, ch * TC + hf * 512: ch * TC + hf * 512 + 512],
                        ps[pi][:, 0:512], AF.Identity,
                        bias=bias_sb[:, ch:ch + 1], scale=scale), "ac")
                    psfree[pi] = av
            wfree[bi] = pv

    qk_proj(Wk, bk_sb, k_all, 1.0)
    kv = kb.val("ac")
    kb.wait("sync", "ac", kv)
    kb.dma("sync", k1.rearrange("(c p) t -> p c t", p=128),
           k_all.ap().rearrange("p (c t) -> p c t", c=16), sem="stk")

    for blk in range(4):
        bi = blk % 2
        kb.wait("gpsimd", "pe", wfree[bi])
        wld = kb.dma("gpsimd",
                     wtile[bi].ap().rearrange("p (c d) -> p c d", c=16),
                     Wv[:, blk * 512:(blk + 1) * 512].rearrange("(c p) d -> p c d", p=128),
                     sem=f"wg{bi}")
        kb.wait("tensor", f"wg{bi}", wld)
        pv = None
        for tt in range(8):
            pi = tt % 4
            kb.wait("tensor", "ac", psfree[pi])
            for kc in range(16):
                kb.op("tensor", lambda e, bi=bi, tt=tt, kc=kc, pi=pi: e.matmul(
                    ps[pi][:, 0:512],
                    x_bf[:, kc * TC + tt * 128: kc * TC + tt * 128 + 128],
                    wtile[bi][:, kc * 512: kc * 512 + 512],
                    start=(kc == 0), stop=False), "pe")
            pv = kb.op("tensor", lambda e, blk=blk, pi=pi: e.matmul(
                ps[pi][:, 0:512], onesrow[0:1, 0:128],
                bv_bf[0:1, blk * 512:(blk + 1) * 512], start=False, stop=True), "pe")
            kb.wait("scalar", "pe", pv)
            av = kb.op("scalar", lambda e, blk=blk, tt=tt, pi=pi: e.activation(
                v_all[:, tt * (D + 1) + blk * 512: tt * (D + 1) + blk * 512 + 512],
                ps[pi][:, 0:512], AF.Identity, bias=0.0, scale=1.0), "ac")
            psfree[pi] = av
        wfree[bi] = pv

    acv = kb.val("ac")
    kb.wait("sync", "ac", acv)
    kb.wait("sync", "dv", vav)
    kb.dma("sync", v1p.rearrange("(t p) d -> p t d", p=128),
           v_all.ap().rearrange("p (t d) -> p t d", t=8), sem="stv")
    qk_proj(Wq, bq_sb, q_all, L1_SCALE)
    acv = kb.val("ac")
    kb.wait("sync", "ac", acv)
    kb.dma("sync", q1.rearrange("(c p) t -> p c t", p=128),
           q_all.ap().rearrange("p (c t) -> p c t", c=16), sem="stq")
    kb.wait("sync", "stk", kb.val("stk"))
    kb.wait("sync", "stv", kb.val("stv"))
    kb.wait("sync", "stq", kb.val("stq"))
    if standalone:
        return kb.finalize()



def build_l1_attn(kb=None):
    standalone = kb is None
    if standalone:
        kb = KB()
    nc = kb.nc
    x = kb.input("x", [D, TC], F32)
    q1 = kb.input("q1", [D, TC], BF16)
    Kg = kb.input("Kg", [NC * D, TC], BF16)
    Vg = kb.input("Vg", [S, D + 1], BF16)
    qbase = kb.input("qbase", [1, 1], F32)
    x3 = kb.output("x3", [D, TC], F32)
    z_dram = kb.dram("z_dram", [D, TC], F32)

    EXPB = -20.0

    q_sb = kb.sb("q_sb", [128, 16 * TC], BF16)
    acc = kb.sb("acc", [128, 8 * (D + 1)], F32)
    iota_row = kb.sb("iota_row", [1, TC], F32)
    qb_sb = kb.sb("qb_sb", [1, 1], F32)
    qpos_row = kb.sb("qpos_row", [1, TC], F32)
    qpos_bf = kb.sb("qpos_bf", [1, TC], BF16)
    qpos_bc = kb.sb("qpos_bc", [128, TC], F32)
    kcols = kb.sb("kcols", [128, 64], F32)
    onesrow = kb.sb("onesrow", [1, 128], BF16)
    rcol = kb.sb("rcol", [128, 1], F32)
    cb = kb.sb("cb", [128, 4], F32)
    attn_bf = kb.sb("attn_bf", [128, D], BF16)
    ln_xbf = kb.sb("ln_xbf", [128, 128], BF16)
    ln_xsq = kb.sb("ln_xsq", [128, 128], BF16)
    ln_onescol = kb.sb("ln_onescol", [128, 1], BF16)
    ln_rows = kb.sb("ln_rows", [1, 2 * TC], F32)
    ln_b = kb.sb("ln_b", [1, TC], F32)
    ln_a_bf = kb.sb("ln_a_bf", [1, TC], BF16)
    ln_b_bf = kb.sb("ln_b_bf", [1, TC], BF16)
    arA = kb.arena()
    kblk = [arA.sbuf_tensor(f"kblk{i}", [128, 16 * 512], BF16) for i in range(2)]
    vblk = [arA.sbuf_tensor(f"vblk{i}", [128, 4 * (D + 1)], BF16) for i in range(2)]
    ptil = arA.sbuf_tensor("ptil", [128, 512], BF16)
    ptil2 = arA.sbuf_tensor("ptil2", [128, 512], BF16)
    msk = arA.sbuf_tensor("msk", [128, 512], BF16)
    arA.allocate()

    pool = [kb.ps(f"pool{i}", [128, 512], F32) for i in range(6)]
    pt = kb.ps("pt", [128, 1024], BF16)
    pe1 = kb.ps("pe1", [128, 16], F32)

    qv = kb.dma("sync", q_sb.ap().rearrange("p (c t) -> p c t", c=16),
                q1.rearrange("(c p) t -> p c t", p=128), sem="ld")
    kb.dma("sync", qb_sb[:], qbase, sem="ld")
    ldv = kb.val("ld")
    kb.op("gpsimd", lambda e: e.iota(iota_row[:], pattern=[[1, TC]], base=0,
                                     channel_multiplier=0,
                                     allow_small_or_imprecise_dtypes=True), "po")
    pov = kb.op("gpsimd", lambda e: e.iota(kcols[:], pattern=[[128, 64]], base=0,
                                           channel_multiplier=1,
                                           allow_small_or_imprecise_dtypes=True), "po")
    kb.wait("vector", "ld", ldv)
    kb.wait("vector", "po", pov)
    kb.op("vector", lambda e: e.memset(onesrow[:], 1.0), "dv")
    kb.op("vector", lambda e: e.memset(ln_onescol[:], 1.0), "dv")
    kb.op("vector", lambda e: e.memset(cb[:, 0:1], EXPB), "dv")
    cbv = kb.op("vector", lambda e: e.memset(cb[:, 1:2], LN_EPS), "dv")
    kb.wait("scalar", "dv", cbv)
    kb.op("vector", lambda e: e.tensor_scalar(qpos_row[:], iota_row[:], qb_sb[0:1, 0:1],
                                              None, op0=OP.add), "dv")
    dvv = kb.op("vector", lambda e: e.tensor_copy(qpos_bf[:], qpos_row[:]), "dv")
    kb.wait("tensor", "dv", dvv)
    pev = None
    for hf in range(2):
        pev = kb.op("tensor", lambda e, hf=hf: e.matmul(
            pool[hf][:, 0:512], onesrow[0:1, 0:128],
            qpos_bf[0:1, hf * 512:hf * 512 + 512], start=True, stop=True), "pe")
    kb.wait("vector", "pe", pev)
    dvv = None
    for hf in range(2):
        dvv = kb.op("vector", lambda e, hf=hf: e.tensor_copy(
            qpos_bc[:, hf * 512:hf * 512 + 512], pool[hf][:, 0:512]), "dv")
    qpos_ready = dvv

    kfree = [0, 0]
    vfree = [0, 0]
    psS_free = [0, 0]
    psO_free = 0
    ptil_free = 0
    msk_free = 0
    kb.wait("vector", "dv", qpos_ready)

    for ks in range(16):
        bi = ks % 2
        r, half = ks // 2, ks % 2
        kb.wait("sync", "pe", kfree[bi])
        kv = kb.dma("sync", kblk[bi].ap().rearrange("p (c t) -> p c t", c=16),
                    Kg[r * D:(r + 1) * D, half * 512:(half + 1) * 512]
                    .rearrange("(c p) t -> p c t", p=128), sem=f"kld{bi}")
        kb.wait("sync", "pe", vfree[bi])
        vv = kb.dma("sync", vblk[bi].ap().rearrange("p (c d) -> p c d", c=4),
                    Vg[ks * 512:(ks + 1) * 512, :].rearrange("(c p) d -> p c d", p=128),
                    sem=f"vld{bi}")
        kb.wait("tensor", f"kld{bi}", kv)
        kb.wait("tensor", f"vld{bi}", vv)
        for qt in range(8):
            si = qt % 2
            psS = pool[4 + si]
            kb.wait("tensor", "ac", psS_free[si])
            spev = None
            for j in range(4):
                for kc in range(16):
                    spev = kb.op("tensor", lambda e, bi=bi, j=j, kc=kc, qt=qt, psS=psS: e.matmul(
                        psS[:, j * 128:j * 128 + 128],
                        kblk[bi][:, kc * 512 + j * 128: kc * 512 + j * 128 + 128],
                        q_sb[:, kc * TC + qt * 128: kc * TC + qt * 128 + 128],
                        start=(kc == 0), stop=(kc == 15)), "pe")
            kb.wait("scalar", "pe", spev)
            kb.wait("scalar", "pe", ptil_free)
            av = kb.op("scalar", lambda e, psS=psS: e.activation(
                ptil[:], psS[:, 0:512], AF.Exp, bias=cb[:, 0:1], scale=1.0), "ac")
            psS_free[si] = av
            kb.wait("vector", "pe", msk_free)
            mv = None
            for j in range(4):
                kc = ks * 4 + j
                mv = kb.op("vector", lambda e, j=j, kc=kc, qt=qt: e.tensor_scalar(
                    msk[:, j * 128:j * 128 + 128],
                    qpos_bc[:, qt * 128:qt * 128 + 128],
                    kcols[:, kc:kc + 1], None, op0=OP.is_ge), "dv")
            kb.wait("vector", "ac", av)
            mav = kb.op("vector", lambda e: e.tensor_tensor(ptil2[:], ptil[:], msk[:],
                                                            op=OP.mult), "dv")
            kb.wait("tensor", "dv", mav)
            kb.wait("tensor", "dv", psO_free)
            ppv = None
            for j in range(4):
                for nb in range(4):
                    ppv = kb.op("tensor", lambda e, bi=bi, j=j, nb=nb: e.matmul(
                        pool[nb][:, 0:512],
                        ptil2[:, j * 128:j * 128 + 128],
                        vblk[bi][:, j * (D + 1) + nb * 512: j * (D + 1) + nb * 512 + 512],
                        start=(j == 0), stop=(j == 3)), "pe")
                ppv = kb.op("tensor", lambda e, bi=bi, j=j: e.matmul(
                    pe1[:, 0:1],
                    ptil2[:, j * 128:j * 128 + 128],
                    vblk[bi][:, j * (D + 1) + D: j * (D + 1) + D + 1],
                    start=(j == 0), stop=(j == 3)), "pe")
            ptil_free = ppv
            msk_free = ppv
            if qt == 7:
                kfree[bi] = ppv
                vfree[bi] = ppv
            kb.wait("vector", "pe", ppv)
            dvv = None
            top = OP.add if ks > 0 else None
            for nb in range(4):
                c0 = qt * (D + 1) + nb * 512
                if ks == 0:
                    dvv = kb.op("vector", lambda e, nb=nb, c0=c0: e.tensor_copy(
                        acc[:, c0:c0 + 512], pool[nb][:, 0:512]), "dv")
                else:
                    dvv = kb.op("vector", lambda e, nb=nb, c0=c0: e.tensor_tensor(
                        acc[:, c0:c0 + 512], pool[nb][:, 0:512],
                        acc[:, c0:c0 + 512], op=OP.add), "dv")
            c1 = qt * (D + 1) + D
            if ks == 0:
                dvv = kb.op("vector", lambda e, c1=c1: e.tensor_copy(
                    acc[:, c1:c1 + 1], pe1[:, 0:1]), "dv")
            else:
                dvv = kb.op("vector", lambda e, c1=c1: e.tensor_tensor(
                    acc[:, c1:c1 + 1], pe1[:, 0:1], acc[:, c1:c1 + 1], op=OP.add), "dv")
            psO_free = dvv

    kb.barrier()
    arA.free()
    arB = kb.arena()
    xq_sb = [arB.sbuf_tensor(f"xq_sb{i}", [128, 16 * 128], F32) for i in range(2)]
    z_stage = [arB.sbuf_tensor(f"z_stage{i}", [128, 16 * 128], F32) for i in range(2)]
    zc = [arB.sbuf_tensor(f"zc{i}", [128, TC], F32) for i in range(2)]
    xo = [arB.sbuf_tensor(f"xo{i}", [128, TC], F32) for i in range(2)]
    idn = arB.sbuf_tensor("idn", [128, 128], BF16)
    onesq = arB.sbuf_tensor("onesq", [128, 128], BF16)
    arB.allocate()
    kb.op("vector", lambda e: e.memset(onesq[:], 1.0), "dv")
    kb.wait("gpsimd", "dv", kb.val("dv"))
    idv = kb.op("gpsimd", lambda e: e.affine_select(idn[:], onesq[:], pattern=[[1, 128]],
                                                    compare_op=OP.is_equal, fill=0.0,
                                                    base=0, channel_multiplier=-1), "po")
    kb.wait("tensor", "po", idv)
    accv = kb.val("dv")
    xq_cons = [0, 0]
    abf_cons = 0
    lnx_cons = 0
    pt_free = 0
    zst_free = [0, 0]
    statv = None
    for qt in range(8):
        bi = qt % 2
        kb.wait("vector", "dv", accv)
        kb.wait("sync", "dv", xq_cons[bi])
        xv = kb.dma("sync", xq_sb[bi].ap().rearrange("p (c q) -> p c q", c=16),
                    x[:, qt * 128:(qt + 1) * 128].rearrange("(c p) q -> p c q", p=128),
                    sem=f"xq{bi}")
        kb.wait("vector", "pe", abf_cons)
        kb.op("vector", lambda e, qt=qt: e.reciprocal(
            rcol[:], acc[:, qt * (D + 1) + D: qt * (D + 1) + D + 1]), "dv")
        dvv = None
        for nb in range(4):
            dvv = kb.op("vector", lambda e, qt=qt, nb=nb: e.tensor_scalar(
                attn_bf[:, nb * 512:nb * 512 + 512],
                acc[:, qt * (D + 1) + nb * 512: qt * (D + 1) + nb * 512 + 512],
                rcol[:], None, op0=OP.mult), "dv")
        kb.wait("tensor", "dv", dvv)
        kb.wait("vector", f"xq{bi}", xv)
        for rd in range(2):
            kb.wait("tensor", "dv", pt_free)
            pev = None
            for c8 in range(8):
                ch = rd * 8 + c8
                pev = kb.op("tensor", lambda e, ch=ch, c8=c8: e.transpose(
                    pt[:, c8 * 128: c8 * 128 + 128],
                    attn_bf[:, ch * 128:ch * 128 + 128], idn[:]), "pe")
            kb.wait("vector", "pe", pev)
            kb.wait("vector", f"zst{bi}", zst_free[bi])
            pt_free = kb.op("vector", lambda e, rd=rd, bi=bi: e.tensor_tensor(
                z_stage[bi][:, rd * 1024:(rd + 1) * 1024], pt[:, 0:1024],
                xq_sb[bi][:, rd * 1024:(rd + 1) * 1024], op=OP.add), "dv")
            if rd == 1:
                xq_cons[bi] = pt_free
                abf_cons = pev
        kb.wait("sync", "dv", pt_free)
        zv = kb.dma("sync", z_dram[:, qt * 128:(qt + 1) * 128]
                    .rearrange("(c p) q -> p c q", p=128),
                    z_stage[bi].ap().rearrange("p (c q) -> p c q", c=16), sem=f"zst{bi}")
        zst_free[bi] = zv
        pse_s = pool[0] if qt < 4 else pool[1]
        pse_q = pool[2] if qt < 4 else pool[3]
        col = (qt % 4) * 128
        for ch in range(16):
            kb.wait("vector", "pe", lnx_cons)
            dvv = kb.op("vector", lambda e, ch=ch, bi=bi: e.tensor_copy(
                ln_xbf[:], z_stage[bi][:, ch * 128:ch * 128 + 128]), "dv")
            kb.wait("scalar", "dv", dvv)
            acv = kb.op("scalar", lambda e: e.square(ln_xsq[:], ln_xbf[:]), "ac")
            kb.wait("tensor", "dv", dvv)
            kb.wait("tensor", "ac", acv)
            kb.op("tensor", lambda e, ch=ch, col=col, pse_s=pse_s: e.matmul(
                pse_s[0:1, col:col + 128], ln_onescol[:], ln_xbf[:],
                start=(ch == 0), stop=(ch == 15)), "pe")
            statv = kb.op("tensor", lambda e, ch=ch, col=col, pse_q=pse_q: e.matmul(
                pse_q[0:1, col:col + 128], ln_onescol[:], ln_xsq[:],
                start=(ch == 0), stop=(ch == 15)), "pe")
            lnx_cons = statv

    mean = lambda: ln_rows[0:1, 0:TC]
    msq = lambda: ln_rows[0:1, TC:2 * TC]
    tmp = lambda: ln_b[0:1, 0:TC]
    var = lambda: ln_rows[0:1, TC:2 * TC]
    std = lambda: ln_rows[0:1, TC:2 * TC]
    arow = lambda: ln_rows[0:1, TC:2 * TC]
    kb.wait("vector", "pe", statv)
    for hf in range(2):
        kb.op("vector", lambda e, hf=hf: e.tensor_scalar(
            mean()[0:1, hf * 512:hf * 512 + 512], pool[hf][0:1, 0:512],
            1.0 / D, None, op0=OP.mult), "dv")
        kb.op("vector", lambda e, hf=hf: e.tensor_scalar(
            msq()[0:1, hf * 512:hf * 512 + 512], pool[2 + hf][0:1, 0:512],
            1.0 / D, None, op0=OP.mult), "dv")
    kb.op("vector", lambda e: e.tensor_tensor(tmp(), mean(), mean(), op=OP.mult), "dv")
    dvv = kb.op("vector", lambda e: e.tensor_tensor(var(), msq(), tmp(), op=OP.subtract), "dv")
    kb.wait("scalar", "dv", dvv)
    acv = kb.op("scalar", lambda e: e.activation(std(), var(), AF.Sqrt,
                                                 bias=cb[0:1, 1:2], scale=1.0), "ac")
    kb.wait("vector", "ac", acv)
    kb.op("vector", lambda e: e.reciprocal(arow(), std()), "dv")
    kb.op("vector", lambda e: e.tensor_tensor(ln_b[0:1, :], mean(), arow(), op=OP.mult), "dv")
    kb.op("vector", lambda e: e.tensor_scalar(ln_b[0:1, :], ln_b[0:1, :], -1.0, None,
                                              op0=OP.mult), "dv")
    kb.op("vector", lambda e: e.tensor_copy(ln_a_bf[:], arow()), "dv")
    dvv = kb.op("vector", lambda e: e.tensor_copy(ln_b_bf[:], ln_b[0:1, :]), "dv")
    kb.wait("tensor", "dv", dvv)
    pev = None
    for hf in range(2):
        kb.op("tensor", lambda e, hf=hf: e.matmul(
            pool[hf][:, 0:512], onesrow[0:1, 0:128],
            ln_a_bf[0:1, hf * 512:hf * 512 + 512], start=True, stop=True), "pe")
        pev = kb.op("tensor", lambda e, hf=hf: e.matmul(
            pool[2 + hf][:, 0:512], onesrow[0:1, 0:128],
            ln_b_bf[0:1, hf * 512:hf * 512 + 512], start=True, stop=True), "pe")
    kb.wait("vector", "pe", pev)
    zld = [0, 0]
    xov = [0, 0]
    for ch in range(16):
        bi = ch % 2
        kb.wait("sync", "dv", zld[bi])
        kb.wait("sync", f"zst{0}", zst_free[0])
        kb.wait("sync", f"zst{1}", zst_free[1])
        zv = kb.dma("sync", zc[bi][:], z_dram[ch * 128:(ch + 1) * 128, :], sem=f"zld{bi}")
        kb.wait("vector", f"zld{bi}", zv)
        kb.wait("vector", f"xo{bi}", xov[bi])
        dvv = None
        for hf in range(2):
            kb.op("vector", lambda e, bi=bi, hf=hf: e.tensor_tensor(
                xo[bi][:, hf * 512:hf * 512 + 512], zc[bi][:, hf * 512:hf * 512 + 512],
                pool[hf][:, 0:512], op=OP.mult), "dv")
            dvv = kb.op("vector", lambda e, bi=bi, hf=hf: e.tensor_tensor(
                xo[bi][:, hf * 512:hf * 512 + 512], xo[bi][:, hf * 512:hf * 512 + 512],
                pool[2 + hf][:, 0:512], op=OP.add), "dv")
        zld[bi] = dvv
        kb.wait("sync", "dv", dvv)
        xov[bi] = kb.dma("sync", x3[ch * 128:(ch + 1) * 128, :], xo[bi][:], sem=f"xo{bi}")
    kb.wait("sync", f"xo{0}", xov[0])
    kb.wait("sync", f"xo{1}", xov[1])
    if standalone:
        return kb.finalize()



def build_l1_ffn(kb=None):
    standalone = kb is None
    if standalone:
        kb = KB()
    nc = kb.nc
    H = 8192
    x = kb.input("x", [D, TC], F32)
    upq = kb.input("upq", [D, H], I32)
    ups = kb.input("ups", [D, H], F32)
    b_up = kb.input("b_up", [1, H], F32)
    upA = kb.input("upA", [D, 16], F32)
    upB = kb.input("upB", [16, H], F32)
    dnq = kb.input("dnq", [H, D], I32)
    dns = kb.input("dns", [H, D], F32)
    b_dn = kb.input("b_dn", [1, D], F32)
    dnA = kb.input("dnA", [H, 16], F32)
    dnB = kb.input("dnB", [16, D], F32)
    x4 = kb.output("x4", [D, TC], F32)
    z_dram = kb.dram("zf_dram", [D, TC], F32)
    wupc = kb.dram("wupc_dq", [D, H], BF16)
    wdnc = kb.dram("wdnc_dq", [H, D], BF16)

    x_bf = kb.sb("x_bf", [128, 16 * 512], BF16)
    x2_bf = kb.sb("x2_bf", [128, 64 * 512], BF16)
    upA_sb = kb.sb("upA_sb", [128, 16 * 16], BF16)
    dnA_sb = kb.sb("dnA_sb", [128, 64 * 16], BF16)
    upB_sb = kb.sb("upB_sb", [16, H], BF16)
    dnB_sb = kb.sb("dnB_sb", [16, D], BF16)
    bup_bf = kb.sb("bup_bf", [1, H], BF16)
    bdn_bf = kb.sb("bdn_bf", [1, D], BF16)
    u1_bf = kb.sb("u1_bf", [16, 512], BF16)
    u2_bf = kb.sb("u2_bf", [16, 512], BF16)
    onesrow = kb.sb("onesrow", [1, 512], BF16)
    xr = [kb.sb(f"xr{i}", [128, 512], F32) for i in range(2)]
    zs = [kb.sb(f"zs{i}", [128, 512], F32) for i in range(2)]
    zbf = kb.sb("zbf", [128, 512], BF16)
    zsq = kb.sb("zsq", [128, 512], BF16)
    onescol = kb.sb("onescol", [128, 1], BF16)
    cb = kb.sb("cb", [128, 2], F32)
    ln_rows = kb.sb("ln_rows", [1, 2 * TC], F32)
    ln_b = kb.sb("ln_b", [1, TC], F32)
    ln_a_bf = kb.sb("ln_a_bf", [1, TC], BF16)
    ln_b_bf = kb.sb("ln_b_bf", [1, TC], BF16)
    arW = kb.arena()
    wq = [arW.sbuf_tensor(f"wq{i}", [128, 4096], BF16) for i in range(2)]
    ws = [arW.sbuf_tensor(f"ws{i}", [128, 4096], BF16) for i in range(2)]
    arW.allocate()

    pool = [kb.ps(f"pool{i}", [128, 512], F32) for i in range(6)]

    kb.dma("gpsimd", upA_sb.ap().rearrange("p (c r) -> p c r", c=16),
           upA.rearrange("(c p) r -> p c r", p=128), sem="wgx")
    kb.dma("gpsimd", dnA_sb.ap().rearrange("p (c r) -> p c r", c=64),
           dnA.rearrange("(c p) r -> p c r", p=128), sem="wgx")
    kb.dma("gpsimd", upB_sb[:], upB, sem="wgx")
    kb.dma("gpsimd", dnB_sb[:], dnB, sem="wgx")
    kb.dma("gpsimd", bup_bf[:], b_up, sem="wgx")
    xgv = kb.dma("gpsimd", bdn_bf[:], b_dn, sem="wgx")
    kb.op("vector", lambda e: e.memset(onesrow[:], 1.0))
    kb.op("vector", lambda e: e.memset(onescol[:], 1.0))
    dvv = kb.op("vector", lambda e: e.memset(cb[:, 0:1], LN_EPS), "dv")
    kb.wait("tensor", "dv", dvv)
    kb.wait("tensor", "wgx", xgv)
    kb.wait("scalar", "wgx", xgv)

    wfree = {0: 0, 1: 0}
    wback = {0: 0, 1: 0}
    psfree = {4: 0, 5: 0}
    statv = None
    zsfree = [0, 0]
    xrfree = [0, 0]
    zstv = [0, 0]
    lnx_cons = 0
    xbf_cons = 0
    x2_cons = 0
    for th in range(2):
        kb.wait("gpsimd", "pe", xbf_cons)
        xv = kb.dma("gpsimd", x_bf.ap().rearrange("p (c t) -> p c t", c=16),
                    x[:, th * 512:(th + 1) * 512].rearrange("(c p) t -> p c t", p=128),
                    sem="xbf")
        kb.wait("tensor", "xbf", xv)
        kb.wait("tensor", "ac", psfree[5])
        pv = None
        for kc in range(16):
            pv = kb.op("tensor", lambda e, kc=kc: e.matmul(
                pool[5][0:16, 0:512], upA_sb[:, kc * 16:kc * 16 + 16],
                x_bf[:, kc * 512:(kc + 1) * 512],
                start=(kc == 0), stop=(kc == 15)), "pe")
        kb.wait("scalar", "pe", pv)
        u1v = kb.op("scalar", lambda e: e.activation(
            u1_bf[0:16, 0:512], pool[5][0:16, 0:512],
            AF.Identity, bias=0.0, scale=1.0), "ac")
        psfree[5] = u1v
        kb.wait("tensor", "ac", u1v)
        for hb in range(32):
            bi = hb % 2
            if th == 0:
                kb.wait("gpsimd", "pe", wfree[bi])
                kb.wait("gpsimd", f"wb{bi}", wback[bi])
                kb.dma("gpsimd", wq[bi].ap().rearrange("p (c d) -> p c d", c=16),
                       upq[:, hb * 256:(hb + 1) * 256].rearrange("(c p) d -> p c d", p=128),
                       sem=f"wq{bi}")
                wv1 = kb.val(f"wq{bi}")
                kb.dma("gpsimd", ws[bi].ap().rearrange("p (c d) -> p c d", c=16),
                       ups[:, hb * 256:(hb + 1) * 256].rearrange("(c p) d -> p c d", p=128),
                       sem=f"ws{bi}")
                wv2 = kb.val(f"ws{bi}")
                kb.wait("vector", f"wq{bi}", wv1)
                kb.wait("vector", f"ws{bi}", wv2)
                dqv = kb.op("vector", lambda e, bi=bi: e.scalar_tensor_tensor(
                    wq[bi][:], wq[bi][:], -7.5, ws[bi][:], op0=OP.add, op1=OP.mult), "dv")
                kb.wait("sync", "dv", dqv)
                wback[bi] = kb.dma(
                    "sync", wupc[:, hb * 256:(hb + 1) * 256].rearrange("(c p) d -> p c d", p=128),
                    wq[bi].ap().rearrange("p (c d) -> p c d", c=16), sem=f"wb{bi}")
                kb.wait("tensor", "dv", dqv)
            else:
                kb.wait("sync", "pe", wfree[bi])
                kb.wait("sync", f"wb{bi}", wback[bi])
                wv1 = kb.dma("sync", wq[bi].ap().rearrange("p (c d) -> p c d", c=16),
                             wupc[:, hb * 256:(hb + 1) * 256]
                             .rearrange("(c p) d -> p c d", p=128), sem=f"wc{bi}")
                kb.wait("tensor", f"wc{bi}", wv1)
            for sub in range(2):
                hchunk = hb * 2 + sub
                pi = 4 + (hchunk % 2)
                kb.wait("tensor", "ac", psfree[pi])
                for kc in range(16):
                    kb.op("tensor", lambda e, bi=bi, sub=sub, kc=kc, pi=pi: e.matmul(
                        pool[pi][:, 0:512],
                        wq[bi][:, kc * 256 + sub * 128: kc * 256 + sub * 128 + 128],
                        x_bf[:, kc * 512:(kc + 1) * 512],
                        start=(kc == 0), stop=False), "pe")
                kb.op("tensor", lambda e, hchunk=hchunk, pi=pi: e.matmul(
                    pool[pi][:, 0:512],
                    upB_sb[0:16, hchunk * 128:(hchunk + 1) * 128],
                    u1_bf[0:16, 0:512], start=False, stop=False), "pe")
                pv = kb.op("tensor", lambda e, hchunk=hchunk, pi=pi: e.matmul(
                    pool[pi][:, 0:512],
                    bup_bf[0:1, hchunk * 128:(hchunk + 1) * 128],
                    onesrow[0:1, 0:512], start=False, stop=True), "pe")
                if sub == 1:
                    wfree[bi] = pv
                kb.wait("scalar", "pe", pv)
                psfree[pi] = kb.op("scalar", lambda e, hchunk=hchunk, pi=pi: e.activation(
                    x2_bf[:, hchunk * 512:(hchunk + 1) * 512],
                    pool[pi][:, 0:512], AF.Relu, bias=0.0, scale=1.0), "ac")
        x2v = kb.val("ac")
        xbf_cons = pv
        kb.wait("tensor", "ac", x2v)
        for hc in range(64):
            pv = kb.op("tensor", lambda e, hc=hc: e.matmul(
                pool[5][0:16, 0:512], dnA_sb[:, hc * 16:hc * 16 + 16],
                x2_bf[:, hc * 512:(hc + 1) * 512],
                start=(hc == 0), stop=(hc == 63)), "pe")
        kb.wait("scalar", "pe", pv)
        u2v = kb.op("scalar", lambda e: e.activation(
            u2_bf[0:16, 0:512], pool[5][0:16, 0:512],
            AF.Identity, bias=0.0, scale=1.0), "ac")
        psfree[5] = u2v
        kb.wait("tensor", "ac", u2v)
        for ch in range(16):
            pi = 4 + (ch % 2)
            kb.wait("tensor", "dv", zsfree[pi - 4])
            for hh in range(2):
                bi = (ch * 2 + hh) % 2
                if th == 0:
                    kb.wait("gpsimd", "pe", wfree[bi])
                    kb.wait("gpsimd", f"wb{bi}", wback[bi])
                    kb.dma("gpsimd", wq[bi].ap().rearrange("p (c d) -> p c d", c=32),
                           dnq[hh * 4096:(hh + 1) * 4096, ch * 128:(ch + 1) * 128]
                           .rearrange("(c p) d -> p c d", p=128), sem=f"wq{bi}")
                    wv1 = kb.val(f"wq{bi}")
                    kb.dma("gpsimd", ws[bi].ap().rearrange("p (c d) -> p c d", c=32),
                           dns[hh * 4096:(hh + 1) * 4096, ch * 128:(ch + 1) * 128]
                           .rearrange("(c p) d -> p c d", p=128), sem=f"ws{bi}")
                    wv2 = kb.val(f"ws{bi}")
                    kb.wait("vector", f"wq{bi}", wv1)
                    kb.wait("vector", f"ws{bi}", wv2)
                    dqv = kb.op("vector", lambda e, bi=bi: e.scalar_tensor_tensor(
                        wq[bi][:], wq[bi][:], -7.5, ws[bi][:], op0=OP.add, op1=OP.mult), "dv")
                    kb.wait("sync", "dv", dqv)
                    wback[bi] = kb.dma(
                        "sync", wdnc[hh * 4096:(hh + 1) * 4096, ch * 128:(ch + 1) * 128]
                        .rearrange("(c p) d -> p c d", p=128),
                        wq[bi].ap().rearrange("p (c d) -> p c d", c=32), sem=f"wb{bi}")
                    kb.wait("tensor", "dv", dqv)
                else:
                    kb.wait("sync", "pe", wfree[bi])
                    kb.wait("sync", f"wb{bi}", wback[bi])
                    wv1 = kb.dma("sync", wq[bi].ap().rearrange("p (c d) -> p c d", c=32),
                                 wdnc[hh * 4096:(hh + 1) * 4096, ch * 128:(ch + 1) * 128]
                                 .rearrange("(c p) d -> p c d", p=128), sem=f"wc{bi}")
                    kb.wait("tensor", f"wc{bi}", wv1)
                for hc in range(32):
                    pv = kb.op("tensor", lambda e, bi=bi, hh=hh, hc=hc, pi=pi: e.matmul(
                        pool[pi][:, 0:512],
                        wq[bi][:, hc * 128:(hc + 1) * 128],
                        x2_bf[:, (hh * 32 + hc) * 512:(hh * 32 + hc) * 512 + 512],
                        start=(hh == 0 and hc == 0), stop=False), "pe")
                wfree[bi] = pv
            kb.op("tensor", lambda e, ch=ch, pi=pi: e.matmul(
                pool[pi][:, 0:512], dnB_sb[0:16, ch * 128:(ch + 1) * 128],
                u2_bf[0:16, 0:512], start=False, stop=False), "pe")
            pv = kb.op("tensor", lambda e, ch=ch, pi=pi: e.matmul(
                pool[pi][:, 0:512], bdn_bf[0:1, ch * 128:(ch + 1) * 128],
                onesrow[0:1, 0:512], start=False, stop=True), "pe")
            xbi = ch % 2
            kb.wait("sync", "dv", xrfree[xbi])
            xrv = kb.dma("sync", xr[xbi][:],
                         x[ch * 128:(ch + 1) * 128, th * 512:(th + 1) * 512], sem=f"xr{xbi}")
            kb.wait("vector", "pe", pv)
            kb.wait("vector", f"xr{xbi}", xrv)
            zi = ch % 2
            kb.wait("vector", f"zst{zi}", zstv[zi])
            kb.wait("vector", "pe", lnx_cons)
            zv = kb.op("vector", lambda e, zi=zi, xbi=xbi, pi=pi: e.tensor_tensor(
                zs[zi][:], pool[pi][:, 0:512], xr[xbi][:], op=OP.add), "dv")
            zsfree[pi - 4] = zv
            xrfree[xbi] = zv
            zbv = kb.op("vector", lambda e, zi=zi: e.tensor_copy(zbf[:], zs[zi][:]), "dv")
            kb.wait("scalar", "dv", zbv)
            sqv = kb.op("scalar", lambda e: e.square(zsq[:], zbf[:]), "ac")
            kb.wait("tensor", "dv", zbv)
            kb.wait("tensor", "ac", sqv)
            pse_s = pool[0] if th == 0 else pool[1]
            pse_q = pool[2] if th == 0 else pool[3]
            kb.op("tensor", lambda e, ch=ch, pse_s=pse_s: e.matmul(
                pse_s[0:1, 0:512], onescol[:], zbf[:],
                start=(ch == 0), stop=(ch == 15)), "pe")
            statv = kb.op("tensor", lambda e, ch=ch, pse_q=pse_q: e.matmul(
                pse_q[0:1, 0:512], onescol[:], zsq[:],
                start=(ch == 0), stop=(ch == 15)), "pe")
            lnx_cons = statv
            kb.wait("sync", "dv", zv)
            zstv[zi] = kb.dma("sync",
                              z_dram[ch * 128:(ch + 1) * 128, th * 512:(th + 1) * 512],
                              zs[zi][:], sem=f"zst{zi}")

    kb.barrier()
    arW.free()
    arZ = kb.arena()
    zc = [arZ.sbuf_tensor(f"zc{i}", [128, TC], F32) for i in range(2)]
    xo = [arZ.sbuf_tensor(f"xo{i}", [128, TC], F32) for i in range(2)]
    arZ.allocate()
    mean = lambda: ln_rows[0:1, 0:TC]
    s2 = lambda: ln_rows[0:1, TC:2 * TC]
    kb.wait("vector", "pe", statv)
    for th in range(2):
        kb.op("vector", lambda e, th=th: e.tensor_scalar(
            mean()[0:1, th * 512:th * 512 + 512], pool[th][0:1, 0:512],
            1.0 / D, None, op0=OP.mult), "dv")
        kb.op("vector", lambda e, th=th: e.tensor_scalar(
            s2()[0:1, th * 512:th * 512 + 512], pool[2 + th][0:1, 0:512],
            1.0 / D, None, op0=OP.mult), "dv")
    kb.op("vector", lambda e: e.tensor_tensor(ln_b[0:1, :], mean(), mean(), op=OP.mult), "dv")
    dvv = kb.op("vector", lambda e: e.tensor_tensor(s2(), s2(), ln_b[0:1, :],
                                                    op=OP.subtract), "dv")
    kb.wait("scalar", "dv", dvv)
    acv = kb.op("scalar", lambda e: e.activation(s2(), s2(), AF.Sqrt,
                                                 bias=cb[0:1, 0:1], scale=1.0), "ac")
    kb.wait("vector", "ac", acv)
    kb.op("vector", lambda e: e.reciprocal(s2(), s2()), "dv")
    kb.op("vector", lambda e: e.tensor_tensor(ln_b[0:1, :], mean(), s2(), op=OP.mult), "dv")
    kb.op("vector", lambda e: e.tensor_scalar(ln_b[0:1, :], ln_b[0:1, :], -1.0, None,
                                              op0=OP.mult), "dv")
    kb.op("vector", lambda e: e.tensor_copy(ln_a_bf[:], s2()), "dv")
    dvv = kb.op("vector", lambda e: e.tensor_copy(ln_b_bf[:], ln_b[0:1, :]), "dv")
    kb.wait("tensor", "dv", dvv)
    pev = None
    for hf in range(2):
        kb.op("tensor", lambda e, hf=hf: e.matmul(
            pool[hf][:, 0:512], onesrow[0:1, 0:128],
            ln_a_bf[0:1, hf * 512:hf * 512 + 512], start=True, stop=True), "pe")
        pev = kb.op("tensor", lambda e, hf=hf: e.matmul(
            pool[2 + hf][:, 0:512], onesrow[0:1, 0:128],
            ln_b_bf[0:1, hf * 512:hf * 512 + 512], start=True, stop=True), "pe")
    kb.wait("vector", "pe", pev)
    zld = [0, 0]
    xov = [0, 0]
    kb.wait("sync", "zst0", kb.val("zst0"))
    kb.wait("sync", "zst1", kb.val("zst1"))
    for ch in range(16):
        bi = ch % 2
        kb.wait("sync", "dv", zld[bi])
        zv = kb.dma("sync", zc[bi][:], z_dram[ch * 128:(ch + 1) * 128, :], sem=f"zld{bi}")
        kb.wait("vector", f"zld{bi}", zv)
        kb.wait("vector", f"xo{bi}", xov[bi])
        dvv = None
        for hf in range(2):
            kb.op("vector", lambda e, bi=bi, hf=hf: e.tensor_tensor(
                xo[bi][:, hf * 512:hf * 512 + 512], zc[bi][:, hf * 512:hf * 512 + 512],
                pool[hf][:, 0:512], op=OP.mult), "dv")
            dvv = kb.op("vector", lambda e, bi=bi, hf=hf: e.tensor_tensor(
                xo[bi][:, hf * 512:hf * 512 + 512], xo[bi][:, hf * 512:hf * 512 + 512],
                pool[2 + hf][:, 0:512], op=OP.add), "dv")
        zld[bi] = dvv
        kb.wait("sync", "dv", dvv)
        xov[bi] = kb.dma("sync", x4[ch * 128:(ch + 1) * 128, :], xo[bi][:], sem=f"xo{bi}")
    kb.wait("sync", "xo0", xov[0])
    kb.wait("sync", "xo1", xov[1])
    if standalone:
        return kb.finalize()



F16 = mybir.dt.float16

def build_l0_attn(kb=None, dbg=False):
    standalone = kb is None
    if standalone:
        kb = KB()
    nc = kb.nc
    x = kb.input("x", [D, S], F32)
    xr = kb.input("xr", [D, TC], F32)
    wqT = kb.input("wqT", [D, 256], F32)
    wkT = kb.input("wkT", [D, 256], F32)
    wvT = kb.input("wvT", [D, 256], F32)
    woT = kb.input("woT", [256, D], F32)
    hb2 = kb.input("hb2", [2, 1], F32)
    x1 = kb.output("x1", [D, TC], F32)
    rs_in = kb.dram("rs_in", [S, D], BF16)
    rs_out = kb.dram("rs_out", [TC, D], BF16)
    z_dram = kb.dram("za_dram", [D, TC], F32)
    sl_dram = kb.dram("sl_dram", [1, 2], F32)
    attn_dbg = nc.dram_tensor("attn_dbg", [S, 256], BF16,
                              kind="ExternalOutput").ap() if dbg else None

    LN2 = float(np.log(2.0))

    q_sb = kb.sb("q_sb", [128, 2 * S], BF16)
    k_sb = kb.sb("k_sb", [128, 2 * S], BF16)
    vp_sb = kb.sb("vp_sb", [128, 64 * 258], BF16)
    wo_sb = kb.sb("wo_sb", [128, 2 * D], BF16)
    ptil = kb.sb("ptil", [128, 512], BF16)
    ptil_b2 = kb.sb("ptil_b", [128, 512], BF16)
    ptmask = kb.sb("ptmask", [128, 128], BF16)
    abf = kb.sb("abf", [128, 256], BF16)
    afm_t = kb.sb("afm_t", [128, 256], BF16)
    rs_stage = [kb.sb(f"rs_stage{i}", [128, D], BF16) for i in range(2)]
    m_diag = kb.sb("m_diag", [128, 128], BF16)
    onesq = kb.sb("onesq", [128, 128], BF16)
    idn = kb.sb("idn", [128, 128], BF16)
    onesrow = kb.sb("onesrow", [1, 512], BF16)
    onescol = kb.sb("onescol", [128, 1], BF16)
    rcol = kb.sb("rcol", [128, 1], F32)
    cb = kb.sb("cb", [128, 2], F32)
    slopes = kb.sb("slopes", [2, 1], F32)
    hcol = kb.sb("hcol", [2, 1], F32)
    slope0 = [kb.sb("slope0a", [1, 1], F32), kb.sb("slope0b", [1, 1], F32)]
    ln_xbf = kb.sb("ln_xbf", [128, 128], BF16)
    ln_xsq = kb.sb("ln_xsq", [128, 128], BF16)
    ln_rows = kb.sb("ln_rows", [1, 2 * TC], F32)
    ln_b = kb.sb("ln_b", [1, TC], F32)
    ln_a_bf = kb.sb("ln_a_bf", [1, TC], BF16)
    ln_b_bf = kb.sb("ln_b_bf", [1, TC], BF16)

    arT = kb.arena()
    crhs = arT.sbuf_tensor("crhs", [3, 2 * S], F16)
    clhs = arT.sbuf_tensor("clhs", [3, 256], F16)
    arT.allocate()
    arS = kb.arena()
    scr = arS.sbuf_tensor("scr", [1, S], F32)
    arS.allocate()

    pool = [kb.ps(f"pool{i}", [128, 512], F32) for i in range(6)]
    pt = kb.ps("pt", [128, 1024], BF16)

    ldv = kb.dma("sync", slopes[:], hb2, sem="ld")
    kb.op("vector", lambda e: e.memset(onesrow[:], 1.0))
    kb.op("vector", lambda e: e.memset(onescol[:], 1.0))
    kb.op("vector", lambda e: e.memset(onesq[:], 1.0))
    kb.op("vector", lambda e: e.memset(cb[:, 0:1], LN_EPS))
    kb.op("vector", lambda e: e.memset(
        vp_sb.ap().rearrange("p (t d) -> p t d", t=64)[:, :, 128:129], 1.0))
    dvv = kb.op("vector", lambda e: e.memset(
        vp_sb.ap().rearrange("p (t d) -> p t d", t=64)[:, :, 257:258], 1.0), "dv")
    kb.wait("gpsimd", "dv", dvv)
    idv = kb.op("gpsimd", lambda e: e.affine_select(idn[:], onesq[:], pattern=[[1, 128]],
                                                    compare_op=OP.is_equal, fill=0.0,
                                                    base=0, channel_multiplier=-1), "po")
    idv = kb.op("gpsimd", lambda e: e.affine_select(m_diag[:], onesq[:], pattern=[[1, 128]],
                                                    compare_op=OP.is_ge, fill=0.0,
                                                    base=0, channel_multiplier=-1), "po")
    kb.wait("scalar", "ld", ldv)
    acv = kb.op("scalar", lambda e: e.activation(slopes[:], slopes[:], AF.Exp,
                                                 bias=0.0, scale=-LN2 * 0.5), "ac")
    kb.wait("sync", "ac", acv)
    s1 = kb.dma("sync", sl_dram.rearrange("a b -> b a"), slopes[:], sem="ld")
    kb.wait("sync", "ld", s1)
    kb.dma("sync", slope0[0][:], sl_dram[0:1, 0:1], sem="ld")
    s2v = kb.dma("sync", slope0[1][:], sl_dram[0:1, 1:2], sem="ld")
    kb.wait("vector", "ld", s2v)
    kb.wait("gpsimd", "ld", s2v)

    for h in range(2):
        kb.wait("gpsimd", "tb", kb.val("tb"))
        kb.wait("vector", "tb", kb.val("tb"))
        pov = kb.op("gpsimd", lambda e: e.iota(scr[:], pattern=[[1, 64], [0, 128]],
                                               base=0, channel_multiplier=0,
                                               allow_small_or_imprecise_dtypes=True), "po")
        kb.wait("vector", "po", pov)
        dvv = kb.op("vector", lambda e, h=h: e.tensor_scalar(
            scr[:], scr[:], slope0[h][0:1, 0:1], -128.0, op0=OP.mult, op1=OP.mult), "dv")
        kb.wait("gpsimd", "dv", dvv)
        g1 = kb.dma("gpsimd", crhs[2:3, h * S:(h + 1) * S], scr[:], sem="tb")
        kb.wait("gpsimd", "tb", g1)
        kb.wait("gpsimd", "tb", kb.val("tb"))
        pov = kb.op("gpsimd", lambda e: e.iota(scr[:], pattern=[[0, 64], [1, 128]],
                                               base=0, channel_multiplier=0,
                                               allow_small_or_imprecise_dtypes=True), "po")
        kb.wait("vector", "po", pov)
        kb.wait("vector", "tb", g1)
        dvv = kb.op("vector", lambda e, h=h: e.tensor_scalar(
            scr[:], scr[:], slope0[h][0:1, 0:1], -1.0, op0=OP.mult, op1=OP.mult), "dv")
        dvv = kb.op("vector", lambda e: e.tensor_scalar(
            scr[:], scr[:], B0, None, op0=OP.subtract), "dv")
        kb.wait("gpsimd", "dv", dvv)
        g2 = kb.dma("gpsimd", crhs[1:2, h * S:(h + 1) * S], scr[:], sem="tb")
        kb.wait("gpsimd", "tb", g2)
        kb.wait("vector", "tb", kb.val("tb"))
        dvv = kb.op("vector", lambda e: e.memset(scr[:], 1.0), "dv")
        kb.wait("gpsimd", "dv", dvv)
        g3 = kb.dma("gpsimd", crhs[0:1, h * S:(h + 1) * S], scr[:], sem="tb")
        kb.wait("gpsimd", "tb", g3)
        kb.wait("gpsimd", "tb", kb.val("tb"))
        pov = kb.op("gpsimd", lambda e: e.iota(scr[0:1, 0:128], pattern=[[1, 128]],
                                               base=0, channel_multiplier=0,
                                               allow_small_or_imprecise_dtypes=True), "po")
        kb.wait("vector", "po", pov)
        dvv = kb.op("vector", lambda e, h=h: e.tensor_scalar(
            scr[0:1, 0:128], scr[0:1, 0:128], slope0[h][0:1, 0:1], None,
            op0=OP.mult), "dv")
        kb.wait("gpsimd", "dv", dvv)
        g4 = kb.dma("gpsimd", clhs[0:1, h * 128:(h + 1) * 128], scr[0:1, 0:128], sem="tb")
        kb.wait("gpsimd", "tb", g4)
        kb.wait("vector", "tb", kb.val("tb"))
        dvv = kb.op("vector", lambda e: e.memset(scr[0:1, 0:256], 1.0), "dv")
        kb.wait("gpsimd", "dv", dvv)
        g5 = kb.dma("gpsimd", clhs[1:2, h * 128:(h + 1) * 128], scr[0:1, 0:128], sem="tb")
        g6 = kb.dma("gpsimd", clhs[2:3, h * 128:(h + 1) * 128], scr[0:1, 128:256], sem="tb")
        kb.wait("gpsimd", "tb", g6)
    tbv = kb.val("tb")
    kb.wait("tensor", "tb", tbv)

    kb.barrier()
    arS.free()
    arP = kb.arena()
    xb = [arP.sbuf_tensor(f"xb{i}", [128, 16 * 256], BF16) for i in range(2)]
    wq_sb = arP.sbuf_tensor("wq_sb", [128, 16 * 256], BF16)
    wk_sb = arP.sbuf_tensor("wk_sb", [128, 16 * 256], BF16)
    wv_sb = arP.sbuf_tensor("wv_sb", [128, 16 * 256], BF16)
    arP.allocate()

    wv1 = kb.dma("gpsimd", wq_sb.ap().rearrange("p (c d) -> p c d", c=16),
                 wqT.rearrange("(c p) d -> p c d", p=128), sem="wg0")
    wv2 = kb.dma("gpsimd", wk_sb.ap().rearrange("p (c d) -> p c d", c=16),
                 wkT.rearrange("(c p) d -> p c d", p=128), sem="wg1")
    wv3 = kb.dma("gpsimd", wv_sb.ap().rearrange("p (c d) -> p c d", c=16),
                 wvT.rearrange("(c p) d -> p c d", p=128), sem="wg2")
    wv4 = kb.dma("gpsimd", wo_sb.ap().rearrange("p (c d) -> p c d", c=2),
                 woT.rearrange("(c p) d -> p c d", p=128), sem="wg3")
    for i, s in enumerate(["wg0", "wg1", "wg2", "wg3"]):
        kb.wait("tensor", s, [wv1, wv2, wv3, wv4][i])

    xbv = [0, 0]
    psfree = [0, 0, 0, 0]
    for tb in range(32):
        bi = tb % 2
        kb.wait("gpsimd", "pe", xbv[bi])
        xv = kb.dma("gpsimd", xb[bi].ap().rearrange("p (c t) -> p c t", c=16),
                    x[:, tb * 256:(tb + 1) * 256].rearrange("(c p) t -> p c t", p=128),
                    sem=f"xb{bi}")
        kb.wait("tensor", f"xb{bi}", xv)
        pv = None
        for hc in range(2):
            for src, dst, scale in ((wq_sb, q_sb, L0_SCALE), (wk_sb, k_sb, 1.0)):
                pi = hc % 2
                kb.wait("tensor", "ac", psfree[pi])
                for kc in range(16):
                    pv = kb.op("tensor", lambda e, bi=bi, hc=hc, kc=kc, src=src, pi=pi: e.matmul(
                        pool[pi][:, 0:256],
                        src[:, kc * 256 + hc * 128: kc * 256 + hc * 128 + 128],
                        xb[bi][:, kc * 256:(kc + 1) * 256],
                        start=(kc == 0), stop=(kc == 15)), "pe")
                kb.wait("scalar", "pe", pv)
                psfree[pi] = kb.op("scalar", lambda e, tb=tb, hc=hc, dst=dst, scale=scale,
                                   pi=pi: e.activation(
                    dst[:, hc * S + tb * 256: hc * S + tb * 256 + 256],
                    pool[pi][:, 0:256], AF.Identity, bias=0.0, scale=scale), "ac")
        for ts in range(2):
            tt = tb * 2 + ts
            pi = 2 + ts % 2
            kb.wait("tensor", "ac", psfree[pi])
            for kc in range(16):
                pv = kb.op("tensor", lambda e, bi=bi, ts=ts, kc=kc, pi=pi: e.matmul(
                    pool[pi][:, 0:256],
                    xb[bi][:, kc * 256 + ts * 128: kc * 256 + ts * 128 + 128],
                    wv_sb[:, kc * 256:(kc + 1) * 256],
                    start=(kc == 0), stop=(kc == 15)), "pe")
            kb.wait("scalar", "pe", pv)
            kb.op("scalar", lambda e, tt=tt, pi=pi: e.activation(
                vp_sb[:, tt * 258: tt * 258 + 128], pool[pi][:, 0:128],
                AF.Identity, bias=0.0, scale=1.0), "ac")
            psfree[pi] = kb.op("scalar", lambda e, tt=tt, pi=pi: e.activation(
                vp_sb[:, tt * 258 + 129: tt * 258 + 257], pool[pi][:, 128:256],
                AF.Identity, bias=0.0, scale=1.0), "ac")
        xbv[bi] = pv

    kb.barrier()
    arP.free()
    projv = kb.val("ac")
    psS_free = [0, 0]
    pfree = [0, 0]
    psO_free = [("ac", projv), ("ac", projv)]
    ptil_free = 0
    abf_cons = 0
    pt_free = 0
    rs_free = [0, 0]
    opfree = [0, 0]
    ptil2b = [ptil, ptil_b2]
    for qt in range(64):
        for h in range(2):
            psO = pool[2 + h]
            kb.wait("tensor", psO_free[h][0], psO_free[h][1])
            ngr = qt // 4 + 1

            def emit_S(g, h=h, qt=qt, ngr=ngr):
                si = g % 2
                psS = pool[si]
                nsl = min(4, qt + 1 - g * 4)
                kb.wait("tensor", "ac", psS_free[si])
                spev = None
                for j in range(nsl):
                    kc = g * 4 + j
                    dd = qt - kc
                    kb.op("tensor", lambda e, h=h, kc=kc, qt=qt, j=j, psS=psS: e.matmul(
                        psS[:, j * 128:(j + 1) * 128],
                        k_sb[:, h * S + kc * 128: h * S + kc * 128 + 128],
                        q_sb[:, h * S + qt * 128: h * S + qt * 128 + 128],
                        start=True, stop=False), "pe")
                    spev = kb.op("tensor", lambda e, h=h, dd=dd, j=j, psS=psS: e.matmul(
                        psS[:, j * 128:(j + 1) * 128],
                        clhs[0:3, h * 128:(h + 1) * 128],
                        crhs[0:3, h * S + dd * 128: h * S + dd * 128 + 128],
                        start=False, stop=True), "pe")
                return spev

            def emit_P(g, spev, h=h, qt=qt, ngr=ngr):
                nonlocal psO_free, ptil_free, pfree
                si = g % 2
                pi = g % 2
                psS = pool[si]
                pt_b = ptil2b[pi]
                nsl = min(4, qt + 1 - g * 4)
                kb.wait("scalar", "pe", spev)
                kb.wait("scalar", "pe", pfree[pi])
                av = kb.op("scalar", lambda e, nsl=nsl, psS=psS, pt_b=pt_b: e.activation(
                    pt_b[:, 0:nsl * 128], psS[:, 0:nsl * 128], AF.Exp,
                    bias=0.0, scale=1.0), "ac")
                psS_free[si] = av
                mav = av
                if g == ngr - 1:
                    kb.wait("vector", "ac", av)
                    kb.wait("vector", "pe", ptil_free)
                    mav = kb.op("vector", lambda e, nsl=nsl, pt_b=pt_b: e.tensor_tensor(
                        ptmask[:], pt_b[:, (nsl - 1) * 128: nsl * 128], m_diag[:],
                        op=OP.mult), "dv")
                    kb.wait("tensor", "dv", mav)
                kb.wait("tensor", "ac", av)
                last = None
                for j in range(nsl):
                    kc = g * 4 + j
                    lhs = (lambda e, nsl=nsl: ptmask[:]) if (g == ngr - 1 and j == nsl - 1) \
                        else (lambda e, j=j, pt_b=pt_b: pt_b[:, j * 128:(j + 1) * 128])
                    last = kb.op("tensor", lambda e, h=h, kc=kc, g=g, j=j,
                                 lhs=lhs, psO=psO, ngr=ngr, nsl=nsl: e.matmul(
                        psO[:, 0:129], lhs(e),
                        vp_sb[:, kc * 258 + 129 * h: kc * 258 + 129 * h + 129],
                        start=(g == 0 and j == 0), stop=(g == ngr - 1 and j == nsl - 1)), "pe")
                pfree[pi] = last
                ptil_free = last

            pend = emit_S(0)
            for g in range(1, ngr):
                pend2 = emit_S(g)
                emit_P(g - 1, pend)
                pend = pend2
            emit_P(ngr - 1, pend)
            kb.wait("vector", "pe", ptil_free)
            kb.wait("vector", "pe", abf_cons)
            kb.op("vector", lambda e, psO=psO: e.reciprocal(rcol[:], psO[:, 128:129]), "dv")
            nv = kb.op("vector", lambda e, h=h, psO=psO: e.tensor_scalar(
                abf[:, h * 128:(h + 1) * 128], psO[:, 0:128], rcol[:], None,
                op0=OP.mult), "dv")
            psO_free[h] = ("dv", nv)
        if dbg:
            kb.wait("sync", "dv", nv)
            kb.dma("sync", nc.__dict__.setdefault("_attn_dbg", None) or attn_dbg[qt * 128:(qt + 1) * 128, :], abf[:], sem="dbg2")
        kb.wait("tensor", "dv", nv)
        kb.wait("tensor", "ac", pt_free)
        pev = None
        for hc in range(2):
            pev = kb.op("tensor", lambda e, hc=hc: e.transpose(
                pt[:, hc * 128:(hc + 1) * 128], abf[:, hc * 128:(hc + 1) * 128],
                idn[:]), "pe")
        kb.wait("scalar", "pe", pev)
        abf_cons = pev
        tev = kb.op("scalar", lambda e: e.activation(
            afm_t[:, 0:256], pt[:, 0:256], AF.Identity, bias=0.0, scale=1.0), "ac")
        pt_free = tev
        kb.wait("tensor", "ac", tev)
        ri = qt % 2
        kb.wait("tensor", f"st{ri}", rs_free[ri])
        opv = None
        for nb in range(4):
            pi = 4 + nb % 2
            kb.wait("tensor", "ac", opfree[pi - 4])
            for hc in range(2):
                opv = kb.op("tensor", lambda e, hc=hc, nb=nb, pi=pi: e.matmul(
                    pool[pi][:, 0:512],
                    afm_t[:, hc * 128:(hc + 1) * 128],
                    wo_sb[:, hc * D + nb * 512: hc * D + nb * 512 + 512],
                    start=(hc == 0), stop=(hc == 1)), "pe")
            kb.wait("scalar", "pe", opv)
            opfree[pi - 4] = kb.op("scalar", lambda e, nb=nb, pi=pi, ri=ri: e.activation(
                rs_stage[ri][:, nb * 512:(nb + 1) * 512], pool[pi][:, 0:512],
                AF.Identity, bias=0.0, scale=1.0), "ac")
        sv = kb.val("ac")
        kb.wait("sync", "ac", sv)
        rs_free[ri] = kb.dma("sync", rs_in[qt * 128:(qt + 1) * 128, :],
                             rs_stage[ri][:], sem=f"st{ri}")
    kb.wait("gpsimd", "st0", kb.val("st0"))
    kb.wait("gpsimd", "st1", kb.val("st1"))

    if dbg:
        kb.wait("sync", "dbg2", kb.val("dbg2"))
        cdbg = kb.output("cdbg", [3, 2 * S], F32)
        ldbg = kb.output("ldbg", [3, 256], F32)
        kb.dma("gpsimd", cdbg, crhs[0:3, :], sem="dbg3")
        db3 = kb.dma("gpsimd", ldbg, clhs[0:3, :], sem="dbg3")
        kb.wait("sync", "dbg3", db3)
        qdbg = kb.output("qdbg", [128, 2 * S], BF16)
        kdbg = kb.output("kdbg", [128, 2 * S], BF16)
        vdbg = kb.output("vdbg", [128, 64 * 258], BF16)
        rdbg = kb.output("rdbg", [S, D], BF16)
        kb.dma("sync", qdbg, q_sb[:], sem="dbg")
        kb.dma("sync", kdbg, k_sb[:], sem="dbg")
        kb.dma("sync", vdbg, vp_sb[:], sem="dbg")
        dbgv = kb.dma("sync", rdbg, rs_in, sem="dbg")
        kb.wait("sync", "dbg", dbgv)

    ccv = kb.op("gpsimd", lambda e: e.collective_compute(
        "ReduceScatter", mybir.AluOpType.add,
        replica_groups=[list(range(NC))],
        ins=[rs_in.opt()], outs=[rs_out.opt()]), "cc", amt=1)
    kb.wait("sync", "cc", ccv)
    kb.wait("vector", "cc", ccv)
    kb.wait("tensor", "cc", ccv)
    kb.wait("scalar", "cc", ccv)

    kb.barrier()
    arT.free()
    arZ = kb.arena()
    rst = [arZ.sbuf_tensor(f"rst{i}", [128, D], BF16) for i in range(2)]
    xq_sb = [arZ.sbuf_tensor(f"xq_sb{i}", [128, 16 * 128], F32) for i in range(2)]
    z_stage = [arZ.sbuf_tensor(f"z_stage{i}", [128, 16 * 128], F32) for i in range(2)]
    zc = [arZ.sbuf_tensor(f"zc{i}", [128, TC], F32) for i in range(2)]
    xo = [arZ.sbuf_tensor(f"xo{i}", [128, TC], F32) for i in range(2)]
    arZ.allocate()
    pt_free = 0
    zst_free = [0, 0]
    statv = None
    lnx_cons = 0
    rst_cons = [0, 0]
    xq_cons = [0, 0]
    for qt in range(8):
        bi = qt % 2
        kb.wait("sync", "pe", rst_cons[bi])
        rv = kb.dma("sync", rst[bi][:], rs_out[qt * 128:(qt + 1) * 128, :], sem=f"rst{bi}")
        kb.wait("sync", "dv", xq_cons[bi])
        xv = kb.dma("sync", xq_sb[bi].ap().rearrange("p (c q) -> p c q", c=16),
                    xr[:, qt * 128:(qt + 1) * 128].rearrange("(c p) q -> p c q", p=128),
                    sem=f"xq{bi}")
        kb.wait("tensor", f"rst{bi}", rv)
        kb.wait("vector", f"xq{bi}", xv)
        for rd in range(2):
            kb.wait("tensor", "dv", pt_free)
            pev = None
            for c8 in range(8):
                ch = rd * 8 + c8
                pev = kb.op("tensor", lambda e, ch=ch, c8=c8, bi=bi: e.transpose(
                    pt[:, c8 * 128: c8 * 128 + 128],
                    rst[bi][:, ch * 128:ch * 128 + 128], idn[:]), "pe")
            kb.wait("vector", "pe", pev)
            kb.wait("vector", f"zst{bi}", zst_free[bi])
            pt_free = kb.op("vector", lambda e, rd=rd, bi=bi: e.tensor_tensor(
                z_stage[bi][:, rd * 1024:(rd + 1) * 1024], pt[:, 0:1024],
                xq_sb[bi][:, rd * 1024:(rd + 1) * 1024], op=OP.add), "dv")
            if rd == 1:
                xq_cons[bi] = pt_free
                rst_cons[bi] = pev
        kb.wait("sync", "dv", pt_free)
        zv = kb.dma("sync", z_dram[:, qt * 128:(qt + 1) * 128]
                    .rearrange("(c p) q -> p c q", p=128),
                    z_stage[bi].ap().rearrange("p (c q) -> p c q", c=16), sem=f"zst{bi}")
        zst_free[bi] = zv
        pse_s = pool[0] if qt < 4 else pool[1]
        pse_q = pool[2] if qt < 4 else pool[3]
        col = (qt % 4) * 128
        for ch in range(16):
            kb.wait("vector", "pe", lnx_cons)
            dvv = kb.op("vector", lambda e, ch=ch, bi=bi: e.tensor_copy(
                ln_xbf[:], z_stage[bi][:, ch * 128:ch * 128 + 128]), "dv")
            kb.wait("scalar", "dv", dvv)
            acv = kb.op("scalar", lambda e: e.square(ln_xsq[:], ln_xbf[:]), "ac")
            kb.wait("tensor", "dv", dvv)
            kb.wait("tensor", "ac", acv)
            kb.op("tensor", lambda e, ch=ch, col=col, pse_s=pse_s: e.matmul(
                pse_s[0:1, col:col + 128], onescol[:], ln_xbf[:],
                start=(ch == 0), stop=(ch == 15)), "pe")
            statv = kb.op("tensor", lambda e, ch=ch, col=col, pse_q=pse_q: e.matmul(
                pse_q[0:1, col:col + 128], onescol[:], ln_xsq[:],
                start=(ch == 0), stop=(ch == 15)), "pe")
            lnx_cons = statv

    mean = lambda: ln_rows[0:1, 0:TC]
    s2r = lambda: ln_rows[0:1, TC:2 * TC]
    kb.wait("vector", "pe", statv)
    for hf2 in range(2):
        kb.op("vector", lambda e, hf2=hf2: e.tensor_scalar(
            mean()[0:1, hf2 * 512:hf2 * 512 + 512], pool[hf2][0:1, 0:512],
            1.0 / D, None, op0=OP.mult), "dv")
        kb.op("vector", lambda e, hf2=hf2: e.tensor_scalar(
            s2r()[0:1, hf2 * 512:hf2 * 512 + 512], pool[2 + hf2][0:1, 0:512],
            1.0 / D, None, op0=OP.mult), "dv")
    kb.op("vector", lambda e: e.tensor_tensor(ln_b[0:1, :], mean(), mean(), op=OP.mult), "dv")
    dvv = kb.op("vector", lambda e: e.tensor_tensor(s2r(), s2r(), ln_b[0:1, :],
                                                    op=OP.subtract), "dv")
    kb.wait("scalar", "dv", dvv)
    acv = kb.op("scalar", lambda e: e.activation(s2r(), s2r(), AF.Sqrt,
                                                 bias=cb[0:1, 0:1], scale=1.0), "ac")
    kb.wait("vector", "ac", acv)
    kb.op("vector", lambda e: e.reciprocal(s2r(), s2r()), "dv")
    kb.op("vector", lambda e: e.tensor_tensor(ln_b[0:1, :], mean(), s2r(), op=OP.mult), "dv")
    kb.op("vector", lambda e: e.tensor_scalar(ln_b[0:1, :], ln_b[0:1, :], -1.0, None,
                                              op0=OP.mult), "dv")
    kb.op("vector", lambda e: e.tensor_copy(ln_a_bf[:], s2r()), "dv")
    dvv = kb.op("vector", lambda e: e.tensor_copy(ln_b_bf[:], ln_b[0:1, :]), "dv")
    kb.wait("tensor", "dv", dvv)
    pev = None
    for hf2 in range(2):
        kb.op("tensor", lambda e, hf2=hf2: e.matmul(
            pool[hf2][:, 0:512], onesrow[0:1, 0:128],
            ln_a_bf[0:1, hf2 * 512:hf2 * 512 + 512], start=True, stop=True), "pe")
        pev = kb.op("tensor", lambda e, hf2=hf2: e.matmul(
            pool[2 + hf2][:, 0:512], onesrow[0:1, 0:128],
            ln_b_bf[0:1, hf2 * 512:hf2 * 512 + 512], start=True, stop=True), "pe")
    kb.wait("vector", "pe", pev)
    zld = [0, 0]
    xov = [0, 0]
    kb.wait("sync", "zst0", kb.val("zst0"))
    kb.wait("sync", "zst1", kb.val("zst1"))
    for ch in range(16):
        bi = ch % 2
        kb.wait("sync", "dv", zld[bi])
        zv = kb.dma("sync", zc[bi][:], z_dram[ch * 128:(ch + 1) * 128, :], sem=f"zld{bi}")
        kb.wait("vector", f"zld{bi}", zv)
        kb.wait("vector", f"xo{bi}", xov[bi])
        dvv = None
        for hf2 in range(2):
            kb.op("vector", lambda e, bi=bi, hf2=hf2: e.tensor_tensor(
                xo[bi][:, hf2 * 512:hf2 * 512 + 512], zc[bi][:, hf2 * 512:hf2 * 512 + 512],
                pool[hf2][:, 0:512], op=OP.mult), "dv")
            dvv = kb.op("vector", lambda e, bi=bi, hf2=hf2: e.tensor_tensor(
                xo[bi][:, hf2 * 512:hf2 * 512 + 512], xo[bi][:, hf2 * 512:hf2 * 512 + 512],
                pool[2 + hf2][:, 0:512], op=OP.add), "dv")
        zld[bi] = dvv
        kb.wait("sync", "dv", dvv)
        xov[bi] = kb.dma("sync", x1[ch * 128:(ch + 1) * 128, :], xo[bi][:], sem=f"xo{bi}")
    kb.wait("sync", "xo0", xov[0])
    kb.wait("sync", "xo1", xov[1])
    if standalone:
        return kb.finalize()



def build_l0_ffn(kb=None, dbg=False):
    standalone = kb is None
    if standalone:
        kb = KB()
    nc = kb.nc
    E, G, LAT, INTER, SHI = 32, 4, 1024, 1024, 4096
    x = kb.input("x", [D, TC], F32)
    gateT = kb.input("gateT", [D, E], F32)
    gbias = kb.input("gbias", [1, E], F32)
    fc1T = kb.input("fc1T", [D, LAT], F32)
    w1T = kb.input("w1T", [E, LAT, INTER], F32)
    w2T = kb.input("w2T", [E, INTER, LAT], F32)
    fc2T = kb.input("fc2T", [LAT, D], F32)
    shupT = kb.input("shupT", [D, SHI], F32)
    shdnT = kb.input("shdnT", [SHI, D], F32)
    x2 = kb.output("x2", [D, TC], F32)
    z_dram = kb.dram("zm_dram", [D, TC], F32)
    y0d = kb.output("y0d", [128, 8 * TC], F32) if dbg else None
    h0d = kb.output("h0d", [128, 8 * TC], BF16) if dbg else None
    cw0d = kb.output("cw0d", [128, 8 * E], F32) if dbg else None
    w2d = kb.output("w2d", [128, 8 * 1024], BF16) if dbg else None
    psd = kb.output("psd", [128, 512], F32) if dbg else None
    ysd1 = kb.output("ysd1", [128, 512], F32) if dbg else None
    yse0 = kb.output("yse0", [128, 8 * TC], F32) if dbg else None
    s1d = kb.output("s1d", [128, 32 * 512], BF16) if dbg else None
    zd0 = kb.output("zd0", [128, 512], F32) if dbg else None
    pad = kb.output("pad", [128, 512], F32) if dbg else None
    yse1 = kb.output("yse1", [128, 8 * TC], F32) if dbg else None

    x_bf = kb.sb("x_bf", [128, 16 * TC], BF16)
    x_lat = kb.sb("x_lat", [128, 8 * TC], BF16)
    y_sb = kb.sb("y_sb", [128, 8 * TC], F32)
    hsb = kb.sb("hsb", [128, 8 * TC], BF16)
    cw_all = kb.sb("cw_all", [128, 8 * E], F32)
    gate_sb = kb.sb("gate_sb", [128, 16 * E], BF16)
    glog = kb.sb("glog", [128, 8 * E], F32)
    glog_bf = kb.sb("glog_bf", [32, TC], F32)
    idn32 = kb.sb("idn32", [32, 32], F32)
    onesq32 = kb.sb("onesq32", [32, 32], F32)
    gb_bc = kb.sb("gb_bc", [128, E], F32)
    gb_bf = kb.sb("gb_bf", [1, E], BF16)
    onesrow = kb.sb("onesrow", [1, 512], BF16)
    onescol = kb.sb("onescol", [128, 1], BF16)
    idn = kb.sb("idn", [128, 128], BF16)
    onesq = kb.sb("onesq", [128, 128], BF16)
    cb = kb.sb("cb", [128, 2], F32)
    scr_s = kb.sb("scr_s", [128, E], F32)
    sbv = kb.sb("sbv", [128, E], F32)
    gs = kb.sb("gs", [128, G], F32)
    gwk = kb.sb("gwk", [128, 8], F32)
    gme = kb.sb("gme", [128, E], F32)
    mskd = kb.sb("mskd", [128, E], F32)
    sel = kb.sb("sel", [128, E], F32)
    eqt = kb.sb("eqt", [128, E], F32)
    mcol = kb.sb("mcol", [128, 1], F32)
    m2col = kb.sb("m2col", [128, 1], F32)
    rcol = kb.sb("rcol", [128, 1], F32)
    xr = [kb.sb(f"xr{i}", [128, 512], F32) for i in range(2)]
    zs = [kb.sb(f"zs{i}", [128, 512], F32) for i in range(2)]
    zbf = kb.sb("zbf", [128, 512], BF16)
    zsq = kb.sb("zsq", [128, 512], BF16)
    ln_rows = kb.sb("ln_rows", [1, 2 * TC], F32)
    ln_b = kb.sb("ln_b", [1, TC], F32)
    ln_a_bf = kb.sb("ln_a_bf", [1, TC], BF16)
    ln_b_bf = kb.sb("ln_b_bf", [1, TC], BF16)
    gfl = kb.sb("gfl", [128, 16 * 32], F32)
    glo = kb.sb("glo", [128, 16 * 32], BF16)
    xfc = [kb.sb("xfc0", [128, 512], F32)] * 2
    xlo = [kb.sb("xlo0", [128, 512], BF16)] * 2
    arW = kb.arena()
    wstrip = [arW.sbuf_tensor(f"wstrip{i}", [128, 8 * 1024], BF16) for i in range(2)]
    arW.allocate()

    pool = [kb.ps(f"pool{i}", [128, 512], F32) for i in range(6)]
    pt = kb.ps("pt", [128, 1024], BF16)

    xg = kb.dma("gpsimd", x_bf.ap().rearrange("p (c t) -> p c t", c=16),
                x.rearrange("(c p) t -> p c t", p=128), sem="xbf")
    gldv = kb.dma("gpsimd", gate_sb.ap().rearrange("p (c e) -> p c e", c=16),
                  gateT.rearrange("(c p) e -> p c e", p=128), sem="gld")
    gbv = kb.dma("gpsimd", gb_bf[:], gbias, sem="gbl")
    kb.op("vector", lambda e: e.memset(onesrow[:], 1.0))
    kb.op("vector", lambda e: e.memset(onescol[:], 1.0))
    kb.op("vector", lambda e: e.memset(onesq[:], 1.0))
    dvv = kb.op("vector", lambda e: e.memset(cb[:, 0:1], LN_EPS), "dv")
    for tt in range(8):
        dvv = kb.op("vector", lambda e, tt=tt: e.memset(y_sb[:, tt * LAT:(tt + 1) * LAT], 0.0), "dv")
    kb.wait("gpsimd", "dv", dvv)
    if dbg:
        kb.wait("sync", "dv", dvv)
        ymv = kb.dma("sync", y0d, y_sb[:], sem="dbgm")
        kb.wait("vector", "dbgm", ymv)
    idv = kb.op("gpsimd", lambda e: e.affine_select(idn[:], onesq[:], pattern=[[1, 128]],
                                                    compare_op=OP.is_equal, fill=0.0,
                                                    base=0, channel_multiplier=-1), "po")
    kb.op("vector", lambda e: e.memset(onesq32[:], 1.0), "dv")
    kb.wait("gpsimd", "dv", kb.val("dv"))
    idv = kb.op("gpsimd", lambda e: e.affine_select(idn32[:], onesq32[:], pattern=[[1, 32]],
                                                    compare_op=OP.is_equal, fill=0.0,
                                                    base=0, channel_multiplier=-1), "po")
    kb.wait("tensor", "po", idv)
    kb.wait("tensor", "xbf", xg)
    kb.wait("tensor", "gld", gldv)
    kb.wait("tensor", "gbl", gbv)

    gfv = kb.dma("sync", gfl.ap().rearrange("p (c e) -> p c e", c=16),
                 gateT.rearrange("(c p) e -> p c e", p=128), sem="gf")
    kb.wait("vector", "gf", gfv)
    kb.wait("vector", "gld", gldv)
    kb.wait("vector", "xbf", xg)
    dvv = kb.op("vector", lambda e: e.tensor_tensor(glo[:], gfl[:], gate_sb[:],
                                                    op=OP.subtract), "dv")
    kb.wait("tensor", "dv", dvv)
    pv = None
    xlov = [0, 0]
    xlo_pe = [0, 0]
    for th in range(2):
        pi = th % 2
        for kc in range(16):
            xbi = 0
            kb.wait("sync", "dv", xlov[xbi])
            kb.wait("sync", "pe", xlo_pe[xbi])
            xcv = kb.dma("sync", xfc[xbi][:],
                         x[kc * 128:(kc + 1) * 128, th * 512:(th + 1) * 512], sem=f"xf{xbi}")
            kb.wait("vector", f"xf{xbi}", xcv)
            kb.wait("vector", "pe", xlo_pe[xbi])
            xlov[xbi] = kb.op("vector", lambda e, kc=kc, th=th, xbi=xbi: e.tensor_tensor(
                xlo[xbi][:], xfc[xbi][:],
                x_bf[:, kc * TC + th * 512: kc * TC + th * 512 + 512], op=OP.subtract), "dv")
            kb.wait("tensor", "dv", xlov[xbi])
            kb.op("tensor", lambda e, th=th, kc=kc, pi=pi: e.matmul(
                pool[pi][0:32, 0:512], gate_sb[:, kc * E:(kc + 1) * E],
                x_bf[:, kc * TC + th * 512: kc * TC + th * 512 + 512],
                start=(kc == 0), stop=False), "pe")
            xlo_pe[xbi] = kb.op("tensor", lambda e, th=th, kc=kc, pi=pi, xbi=xbi: e.matmul(
                pool[pi][0:32, 0:512], gate_sb[:, kc * E:(kc + 1) * E],
                xlo[xbi][:], start=False, stop=False), "pe")
            pv = kb.op("tensor", lambda e, th=th, kc=kc, pi=pi: e.matmul(
                pool[pi][0:32, 0:512], glo[:, kc * E:(kc + 1) * E],
                x_bf[:, kc * TC + th * 512: kc * TC + th * 512 + 512],
                start=False, stop=(kc == 15)), "pe")
        kb.wait("scalar", "pe", pv)
        kb.op("scalar", lambda e, th=th, pi=pi: e.activation(
            glog_bf[0:32, th * 512:(th + 1) * 512], pool[pi][0:32, 0:512],
            AF.Identity, bias=0.0, scale=1.0), "ac")
    acv = kb.val("ac")
    kb.wait("tensor", "ac", acv)
    pv = kb.op("tensor", lambda e: e.matmul(
        pool[2][:, 0:E], onesrow[0:1, 0:128], gb_bf[0:1, 0:E], start=True, stop=True), "pe")
    kb.wait("vector", "pe", pv)
    dvv = kb.op("vector", lambda e: e.tensor_copy(gb_bc[:], pool[2][:, 0:E]), "dv")
    ptfree = 0
    for tt in range(8):
        kb.wait("tensor", "dv", ptfree)
        pv = kb.op("tensor", lambda e, tt=tt: e.transpose(
            pool[3][:, 0:32], glog_bf[0:32, tt * 128:(tt + 1) * 128], idn32[:]), "pe")
        kb.wait("vector", "pe", pv)
        ptfree = kb.op("vector", lambda e, tt=tt: e.tensor_copy(
            glog[:, tt * E:(tt + 1) * E], pool[3][:, 0:32]), "dv")

    NEG = -1.0e30
    kb.wait("scalar", "dv", kb.val("dv"))
    prev_dv = 0
    for tt in range(8):
        kb.wait("scalar", "dv", prev_dv)
        acv = kb.op("scalar", lambda e, tt=tt: e.activation(
            scr_s[:], glog[:, tt * E:(tt + 1) * E], AF.Sigmoid, bias=0.0, scale=1.0), "ac")
        kb.wait("vector", "ac", acv)
        kb.op("vector", lambda e: e.tensor_tensor(sbv[:], scr_s[:], gb_bc[:], op=OP.add), "dv")
        for g in range(G):
            kb.op("vector", lambda e, g=g: e.reduce_max(mcol[:], sbv[:, g * 8:(g + 1) * 8],
                                                        axis=AX.X), "dv")
            kb.op("vector", lambda e, g=g: e.tensor_scalar(gwk[:, 0:8], sbv[:, g * 8:(g + 1) * 8],
                                                           mcol[:], None, op0=OP.is_equal), "dv")
            kb.op("vector", lambda e, g=g: e.scalar_tensor_tensor(
                gwk[:, 0:8], gwk[:, 0:8], NEG, sbv[:, g * 8:(g + 1) * 8],
                op0=OP.mult, op1=OP.add), "dv")
            kb.op("vector", lambda e, g=g: e.reduce_max(m2col[:], gwk[:, 0:8], axis=AX.X), "dv")
            kb.op("vector", lambda e, g=g: e.tensor_tensor(gs[:, g:g + 1], mcol[:], m2col[:],
                                                           op=OP.add), "dv")
        kb.op("vector", lambda e: e.reduce_max(mcol[:], gs[:], axis=AX.X), "dv")
        kb.op("vector", lambda e: e.tensor_scalar(gwk[:, 0:G], gs[:], mcol[:], None,
                                                  op0=OP.is_equal), "dv")
        kb.op("vector", lambda e: e.scalar_tensor_tensor(gwk[:, 0:G], gwk[:, 0:G], NEG, gs[:],
                                                         op0=OP.mult, op1=OP.add), "dv")
        kb.op("vector", lambda e: e.reduce_max(m2col[:], gwk[:, 0:G], axis=AX.X), "dv")
        for g in range(G):
            kb.op("vector", lambda e, g=g: e.tensor_scalar(
                mcol[:], gs[:, g:g + 1], m2col[:], None, op0=OP.is_ge), "dv")
            kb.op("vector", lambda e, g=g: e.tensor_scalar(
                gme[:, g * 8:(g + 1) * 8], onesq[:, 0:8].bitcast(F32) if False else gme[:, g * 8:(g + 1) * 8],
                mcol[:], None, op0=OP.mult), "dv") if False else None
            kb.op("vector", lambda e, g=g: e.memset(gme[:, g * 8:(g + 1) * 8], 1.0), "dv")
            kb.op("vector", lambda e, g=g: e.tensor_scalar(
                gme[:, g * 8:(g + 1) * 8], gme[:, g * 8:(g + 1) * 8], mcol[:], None,
                op0=OP.mult), "dv")
        kb.op("vector", lambda e: e.tensor_scalar(gme[:], gme[:], 1.0, NEG,
                                                  op0=OP.subtract, op1=OP.mult), "dv")
        kb.op("vector", lambda e: e.tensor_tensor(mskd[:], sbv[:], gme[:], op=OP.subtract), "dv")
        kb.op("vector", lambda e: e.memset(sel[:], 0.0), "dv")
        for it in range(8):
            kb.op("vector", lambda e: e.reduce_max(mcol[:], mskd[:], axis=AX.X), "dv")
            kb.op("vector", lambda e: e.tensor_scalar(eqt[:], mskd[:], mcol[:], None,
                                                      op0=OP.is_equal), "dv")
            kb.op("vector", lambda e: e.tensor_tensor(sel[:], sel[:], eqt[:], op=OP.max), "dv")
            kb.op("vector", lambda e: e.scalar_tensor_tensor(mskd[:], eqt[:], NEG, mskd[:],
                                                             op0=OP.mult, op1=OP.add), "dv")
        kb.op("vector", lambda e: e.tensor_tensor(eqt[:], scr_s[:], sel[:], op=OP.mult), "dv")
        kb.op("vector", lambda e: e.reduce_sum(mcol[:], eqt[:], axis=AX.X), "dv")
        kb.op("vector", lambda e: e.reciprocal(rcol[:], mcol[:]), "dv")
        prev_dv = kb.op("vector", lambda e, tt=tt: e.tensor_scalar(
            cw_all[:, tt * E:(tt + 1) * E], eqt[:], rcol[:], 2.5,
            op0=OP.mult, op1=OP.mult), "dv")
    cw_done = prev_dv

    kb.wait("tensor", "dv", ptfree)
    wfree = [0, 0]
    psfree = [0, 0, 0, 0]
    for blk in range(2):
        bi = blk % 2
        kb.wait("gpsimd", "pe", wfree[bi])
        wld = kb.dma("gpsimd", wstrip[bi].ap().rearrange("p (c d) -> p c d", c=16)[:, :, 0:512],
                     fc1T[:, blk * 512:(blk + 1) * 512].rearrange("(c p) d -> p c d", p=128),
                     sem=f"w{bi}")
        kb.wait("tensor", f"w{bi}", wld)
        for sub in range(4):
            lc = blk * 4 + sub
            for th in range(2):
                pi = (sub * 2 + th) % 4
                kb.wait("tensor", "ac", psfree[pi])
                pv = None
                for kc in range(16):
                    pv = kb.op("tensor", lambda e, bi=bi, sub=sub, th=th, kc=kc, pi=pi: e.matmul(
                        pool[pi][:, 0:512],
                        wstrip[bi][:, kc * 512 + sub * 128: kc * 512 + sub * 128 + 128],
                        x_bf[:, kc * TC + th * 512: kc * TC + th * 512 + 512],
                        start=(kc == 0), stop=(kc == 15)), "pe")
                kb.wait("scalar", "pe", pv)
                psfree[pi] = kb.op("scalar", lambda e, lc=lc, th=th, pi=pi: e.activation(
                    x_lat[:, lc * TC + th * 512: lc * TC + th * 512 + 512],
                    pool[pi][:, 0:512], AF.Identity, bias=0.0, scale=1.0), "ac")
        wfree[bi] = pv
    xlat_done = kb.val("ac")

    kb.wait("tensor", "ac", xlat_done)
    kb.wait("vector", "dv", cw_done)
    hfree = 0
    yfree = [0, 0]
    for ee in range(E):
        bi = ee % 2
        kb.wait("gpsimd", "pe", wfree[bi])
        wld = kb.dma("gpsimd", wstrip[bi].ap().rearrange("p (c d) -> p c d", c=8),
                     w1T[ee].rearrange("(c p) d -> p c d", p=128), sem=f"w{bi}")
        kb.wait("tensor", f"w{bi}", wld)
        sqv = None
        sq_guard = 0
        for ic in range(8):
            for th in range(2):
                pi = th % 2
                kb.wait("tensor", "ac", psfree[pi])
                kb.wait("tensor", "dv", sq_guard)
                pv = None
                for lc in range(8):
                    pv = kb.op("tensor", lambda e, bi=bi, ic=ic, th=th, lc=lc, pi=pi: e.matmul(
                        pool[pi][:, 0:512],
                        wstrip[bi][:, lc * 1024 + ic * 128: lc * 1024 + ic * 128 + 128],
                        x_lat[:, lc * TC + th * 512: lc * TC + th * 512 + 512],
                        start=(lc == 0), stop=(lc == 7)), "pe")
                kb.wait("scalar", "pe", pv)
                psfree[pi] = kb.op("scalar", lambda e, ic=ic, th=th, pi=pi: e.activation(
                    hsb[:, ic * TC + th * 512: ic * TC + th * 512 + 512],
                    pool[pi][:, 0:512], AF.Relu, bias=0.0, scale=1.0), "ac")
            kb.wait("vector", "ac", psfree[1])
            sqv = kb.op("vector", lambda e, ic=ic: e.tensor_tensor(
                hsb[:, ic * TC:(ic + 1) * TC], hsb[:, ic * TC:(ic + 1) * TC],
                hsb[:, ic * TC:(ic + 1) * TC], op=OP.mult), "dv")
        bi2 = (ee + 1) % 2
        kb.wait("gpsimd", "pe", wfree[bi2])
        wld2 = kb.dma("gpsimd", wstrip[bi2].ap().rearrange("p (c d) -> p c d", c=8),
                      w2T[ee].rearrange("(c p) d -> p c d", p=128), sem=f"w{bi2}")
        kb.wait("tensor", f"w{bi2}", wld2)
        if dbg and ee == 31:
            kb.wait("sync", f"w{bi2}", wld2)
            w2v = kb.dma("sync", w2d, wstrip[bi2][:], sem="dbgw")
            kb.wait("gpsimd", "dbgw", w2v)
        kb.wait("tensor", "dv", sqv)
        pv = None
        for ts2 in range(8):
            for lh in range(2):
                pi = 4 + (ts2 * 2 + lh) % 2
                kb.wait("tensor", "dv", yfree[pi - 4])
                for ic in range(8):
                    pv = kb.op("tensor", lambda e, bi2=bi2, ts2=ts2, lh=lh, ic=ic, pi=pi: e.matmul(
                        pool[pi][:, 0:512],
                        hsb[:, ic * TC + ts2 * 128: ic * TC + ts2 * 128 + 128],
                        wstrip[bi2][:, ic * 1024 + lh * 512: ic * 1024 + lh * 512 + 512],
                        start=(ic == 0), stop=(ic == 7)), "pe")
                kb.wait("vector", "pe", pv)
                if dbg and ee == 0 and ts2 == 0 and lh == 0:
                    cpv = kb.op("vector", lambda e, pi=pi: e.tensor_copy(zs[0][:], pool[pi][:, 0:512]), "dv")
                    kb.wait("sync", "dv", cpv)
                    pdv = kb.dma("sync", psd, zs[0][:], sem="dbgp")
                    kb.wait("sync", "dbgp", pdv)
                    kb.wait("vector", "dbgp", pdv)
                yfree[pi - 4] = kb.op("vector", lambda e, ee=ee, ts2=ts2, lh=lh, pi=pi: e.scalar_tensor_tensor(
                    y_sb[:, ts2 * LAT + lh * 512: ts2 * LAT + lh * 512 + 512],
                    pool[pi][:, 0:512], cw_all[:, ts2 * E + ee: ts2 * E + ee + 1],
                    y_sb[:, ts2 * LAT + lh * 512: ts2 * LAT + lh * 512 + 512],
                    op0=OP.mult, op1=OP.add), "dv")
                if dbg and ee == 0 and ts2 == 0 and lh == 0:
                    kb.wait("sync", "dv", yfree[pi - 4])
                    y1v = kb.dma("sync", ysd1, y_sb[:, 0:512], sem="dbgy")
                    kb.wait("sync", "dbgy", y1v)
                    kb.wait("vector", "dbgy", y1v)
        wfree[bi2] = pv
        hfree = pv

    if dbg:
        cwd = kb.output("cwd", [128, 8 * E], F32)
        xld = kb.output("xld", [128, 8 * TC], BF16)
        ysd = kb.output("ysd", [128, 8 * TC], F32)
        gld2 = kb.output("gld2", [128, 8 * E], F32)
        kb.wait("sync", "dv", kb.val("dv"))
        kb.wait("sync", "ac", kb.val("ac"))
        hd = kb.output("hd", [128, 8 * TC], BF16)
        kb.dma("sync", hd, hsb[:], sem="dbg")
        kb.dma("sync", cwd, cw_all[:], sem="dbg")
        kb.dma("sync", xld, x_lat[:], sem="dbg")
        kb.dma("sync", ysd, y_sb[:], sem="dbg")
        dgv = kb.dma("sync", gld2, glog[:], sem="dbg")
        kb.wait("sync", "dbg", dgv)
        kb.wait("vector", "dbg", dgv)

    ydone = kb.val("dv")
    kb.wait("vector", "dv", ydone)
    kb.wait("vector", "pe", hfree)
    ptfree = 0
    ycast = 0
    for ts2 in range(8):
        dvv = kb.op("vector", lambda e, ts2=ts2: e.tensor_copy(
            x_lat[:, ts2 * 1024: ts2 * 1024 + 1024],
            y_sb[:, ts2 * LAT:(ts2 + 1) * LAT]), "dv")
        kb.wait("tensor", "dv", dvv)
        kb.wait("tensor", "ac", ptfree)
        pv = None
        for lc8 in range(8):
            pv = kb.op("tensor", lambda e, ts2=ts2, lc8=lc8: e.transpose(
                pt[:, lc8 * 128:(lc8 + 1) * 128],
                x_lat[:, ts2 * 1024 + lc8 * 128: ts2 * 1024 + lc8 * 128 + 128], idn[:]), "pe")
        kb.wait("scalar", "pe", pv)
        ptfree = kb.op("scalar", lambda e, ts2=ts2: e.activation(
            hsb.ap().rearrange("p (c t) -> p c t", c=8)[:, :, ts2 * 128:(ts2 + 1) * 128],
            pt.ap().rearrange("p (c t) -> p c t", c=8), AF.Identity,
            bias=0.0, scale=1.0), "ac")
    yfm_done = ptfree

    arS1 = kb.arena()
    s1 = arS1.sbuf_tensor("s1", [128, 32 * 512], BF16)
    fcs = [arS1.sbuf_tensor(f"fcs{i}", [128, 1024], BF16) for i in range(2)]
    arS1.allocate()
    kb.wait("tensor", "ac", yfm_done)
    statv = None
    zstv = [0, 0]
    xrfree = [0, 0]
    zsfree = [0, 0]
    lnx_cons = 0
    s1_cons = 0
    for th in range(2):
        kb.wait("vector", "ac", s1_cons)
        sq_last = None
        for blk8 in range(8):
            bi = blk8 % 2
            kb.wait("gpsimd", "pe", wfree[bi])
            wld = kb.dma("gpsimd", wstrip[bi].ap().rearrange("p (c d) -> p c d", c=16)[:, :, 0:512],
                         shupT[:, blk8 * 512:(blk8 + 1) * 512].rearrange("(c p) d -> p c d", p=128),
                         sem=f"w{bi}")
            kb.wait("tensor", f"w{bi}", wld)
            for sub in range(4):
                sc = blk8 * 4 + sub
                pi = sub % 2
                kb.wait("tensor", "ac", psfree[pi])
                pv = None
                for kc in range(16):
                    pv = kb.op("tensor", lambda e, bi=bi, sub=sub, th=th, kc=kc, pi=pi: e.matmul(
                        pool[pi][:, 0:512],
                        wstrip[bi][:, kc * 512 + sub * 128: kc * 512 + sub * 128 + 128],
                        x_bf[:, kc * TC + th * 512: kc * TC + th * 512 + 512],
                        start=(kc == 0), stop=(kc == 15)), "pe")
                kb.wait("scalar", "pe", pv)
                psfree[pi] = kb.op("scalar", lambda e, sc=sc, pi=pi: e.activation(
                    s1[:, sc * 512:(sc + 1) * 512], pool[pi][:, 0:512],
                    AF.Relu, bias=0.0, scale=1.0), "ac")
            wfree[bi] = pv
        rlv = kb.val("ac")
        kb.wait("vector", "ac", rlv)
        for sc in range(32):
            sq_last = kb.op("vector", lambda e, sc=sc: e.tensor_tensor(
                s1[:, sc * 512:(sc + 1) * 512], s1[:, sc * 512:(sc + 1) * 512],
                s1[:, sc * 512:(sc + 1) * 512], op=OP.mult), "dv")
        kb.wait("tensor", "dv", sq_last)
        if dbg and th == 0:
            kb.wait("sync", "dv", kb.val("dv"))
            sdv = kb.dma("sync", s1d, s1[:], sem="dbgs")
            kb.wait("sync", "dbgs", sdv)
            kb.wait("vector", "dbgs", sdv)
            kb.wait("scalar", "dbgs", sdv)
        for dc in range(16):
            bi = dc % 2
            kb.wait("gpsimd", "pe", wfree[bi])
            wld = kb.dma("gpsimd",
                         wstrip[bi][:, 0:4096].rearrange("p (c d) -> p c d", c=32),
                         shdnT[:, dc * 128:(dc + 1) * 128].rearrange("(c p) d -> p c d", p=128),
                         sem=f"w{bi}")
            kb.wait("gpsimd", "ac", kb.val("ac") if th == 0 and dc < 2 else 0)
            fv = kb.dma("gpsimd", fcs[bi].ap().rearrange("p (c d) -> p c d", c=8),
                        fc2T[:, dc * 128:(dc + 1) * 128].rearrange("(c p) d -> p c d", p=128),
                        sem=f"fc{bi}")
            kb.wait("tensor", f"w{bi}", wld)
            kb.wait("tensor", f"fc{bi}", fv)
            kb.wait("tensor", "dv", zsfree[0 if dc % 2 == 0 else 1])
            pvA = None
            for lc in range(8):
                pvA = kb.op("tensor", lambda e, bi=bi, lc=lc, th=th: e.matmul(
                    pool[4][:, 0:512], fcs[bi][:, lc * 128:(lc + 1) * 128],
                    hsb[:, lc * TC + th * 512: lc * TC + th * 512 + 512],
                    start=(lc == 0), stop=(lc == 7)), "pe")
            pvB = None
            for sc in range(32):
                pvB = kb.op("tensor", lambda e, bi=bi, sc=sc, th=th: e.matmul(
                    pool[5][:, 0:512], wstrip[bi][:, sc * 128:(sc + 1) * 128],
                    s1[:, sc * 512:(sc + 1) * 512],
                    start=(sc == 0), stop=(sc == 31)), "pe")
            wfree[bi] = pvB
            if th == 1 and dc >= 14:
                s1_cons = 0
            xbi = dc % 2
            kb.wait("sync", "dv", xrfree[xbi])
            xrv = kb.dma("sync", xr[xbi][:],
                         x[dc * 128:(dc + 1) * 128, th * 512:(th + 1) * 512], sem=f"xr{xbi}")
            kb.wait("vector", "pe", pvB)
            kb.wait("vector", f"xr{xbi}", xrv)
            zi = dc % 2
            kb.wait("vector", f"zst{zi}", zstv[zi])
            kb.wait("vector", "pe", lnx_cons)
            kb.op("vector", lambda e, zi=zi, xbi=xbi: e.scalar_tensor_tensor(
                zs[zi][:], pool[4][:, 0:512], 1.0, xr[xbi][:], op0=OP.mult, op1=OP.add), "dv")
            zv = kb.op("vector", lambda e, zi=zi: e.tensor_tensor(
                zs[zi][:], zs[zi][:], pool[5][:, 0:512], op=OP.add), "dv")
            zsfree[zi] = zv
            xrfree[xbi] = zv
            if dbg and th == 0 and dc == 0:
                cpv = kb.op("vector", lambda e: e.tensor_copy(zbf[:], pool[4][:, 0:512]), "dv")
                kb.wait("sync", "dv", cpv)
                kb.dma("sync", pad, zbf.ap().bitcast(F32) if False else zs[1][:], sem="dbgt") if False else None
                cpv2 = kb.op("vector", lambda e: e.tensor_copy(zs[1][:], pool[4][:, 0:512]), "dv")
                kb.wait("sync", "dv", cpv2)
                pdv2 = kb.dma("sync", pad, zs[1][:], sem="dbgt")
                kb.wait("sync", "dbgt", pdv2)
                kb.wait("vector", "dbgt", pdv2)
                kb.wait("sync", "dv", zv)
                zdv = kb.dma("sync", zd0, zs[zi][:], sem="dbgt")
                kb.wait("sync", "dbgt", zdv)
                kb.wait("vector", "dbgt", zdv)
            zbv = kb.op("vector", lambda e, zi=zi: e.tensor_copy(zbf[:], zs[zi][:]), "dv")
            kb.wait("scalar", "dv", zbv)
            sqv = kb.op("scalar", lambda e: e.square(zsq[:], zbf[:]), "ac")
            kb.wait("tensor", "dv", zbv)
            kb.wait("tensor", "ac", sqv)
            pse_s = pool[0] if th == 0 else pool[1]
            pse_q = pool[2] if th == 0 else pool[3]
            kb.op("tensor", lambda e, dc=dc, pse_s=pse_s: e.matmul(
                pse_s[0:1, 0:512], onescol[:], zbf[:],
                start=(dc == 0), stop=(dc == 15)), "pe")
            statv = kb.op("tensor", lambda e, dc=dc, pse_q=pse_q: e.matmul(
                pse_q[0:1, 0:512], onescol[:], zsq[:],
                start=(dc == 0), stop=(dc == 15)), "pe")
            lnx_cons = statv
            kb.wait("sync", "dv", zv)
            zstv[zi] = kb.dma("sync",
                              z_dram[dc * 128:(dc + 1) * 128, th * 512:(th + 1) * 512],
                              zs[zi][:], sem=f"zst{zi}")

    kb.barrier()
    arS1.free()
    arZ = kb.arena()
    zc = [arZ.sbuf_tensor("zc0", [128, TC], F32)] * 2
    xo = [arZ.sbuf_tensor("xo0", [128, TC], F32)] * 2
    arZ.allocate()
    mean = lambda: ln_rows[0:1, 0:TC]
    s2 = lambda: ln_rows[0:1, TC:2 * TC]
    kb.wait("vector", "pe", statv)
    for th in range(2):
        kb.op("vector", lambda e, th=th: e.tensor_scalar(
            mean()[0:1, th * 512:th * 512 + 512], pool[th][0:1, 0:512],
            1.0 / D, None, op0=OP.mult), "dv")
        kb.op("vector", lambda e, th=th: e.tensor_scalar(
            s2()[0:1, th * 512:th * 512 + 512], pool[2 + th][0:1, 0:512],
            1.0 / D, None, op0=OP.mult), "dv")
    kb.op("vector", lambda e: e.tensor_tensor(ln_b[0:1, :], mean(), mean(), op=OP.mult), "dv")
    dvv = kb.op("vector", lambda e: e.tensor_tensor(s2(), s2(), ln_b[0:1, :],
                                                    op=OP.subtract), "dv")
    kb.wait("scalar", "dv", dvv)
    acv = kb.op("scalar", lambda e: e.activation(s2(), s2(), AF.Sqrt,
                                                 bias=cb[0:1, 0:1], scale=1.0), "ac")
    kb.wait("vector", "ac", acv)
    kb.op("vector", lambda e: e.reciprocal(s2(), s2()), "dv")
    kb.op("vector", lambda e: e.tensor_tensor(ln_b[0:1, :], mean(), s2(), op=OP.mult), "dv")
    kb.op("vector", lambda e: e.tensor_scalar(ln_b[0:1, :], ln_b[0:1, :], -1.0, None,
                                              op0=OP.mult), "dv")
    kb.op("vector", lambda e: e.tensor_copy(ln_a_bf[:], s2()), "dv")
    dvv = kb.op("vector", lambda e: e.tensor_copy(ln_b_bf[:], ln_b[0:1, :]), "dv")
    kb.wait("tensor", "dv", dvv)
    pev = None
    for hf in range(2):
        kb.op("tensor", lambda e, hf=hf: e.matmul(
            pool[hf][:, 0:512], onesrow[0:1, 0:128],
            ln_a_bf[0:1, hf * 512:hf * 512 + 512], start=True, stop=True), "pe")
        pev = kb.op("tensor", lambda e, hf=hf: e.matmul(
            pool[2 + hf][:, 0:512], onesrow[0:1, 0:128],
            ln_b_bf[0:1, hf * 512:hf * 512 + 512], start=True, stop=True), "pe")
    kb.wait("vector", "pe", pev)
    zld = [0, 0]
    xov = [0, 0]
    kb.wait("sync", "zst0", kb.val("zst0"))
    kb.wait("sync", "zst1", kb.val("zst1"))
    for ch in range(16):
        bi = 0
        kb.wait("sync", "dv", zld[bi])
        zv = kb.dma("sync", zc[bi][:], z_dram[ch * 128:(ch + 1) * 128, :], sem=f"zld{bi}")
        kb.wait("vector", f"zld{bi}", zv)
        kb.wait("vector", f"xo{bi}", xov[bi])
        dvv = None
        for hf in range(2):
            kb.op("vector", lambda e, bi=bi, hf=hf: e.tensor_tensor(
                xo[bi][:, hf * 512:hf * 512 + 512], zc[bi][:, hf * 512:hf * 512 + 512],
                pool[hf][:, 0:512], op=OP.mult), "dv")
            dvv = kb.op("vector", lambda e, bi=bi, hf=hf: e.tensor_tensor(
                xo[bi][:, hf * 512:hf * 512 + 512], xo[bi][:, hf * 512:hf * 512 + 512],
                pool[2 + hf][:, 0:512], op=OP.add), "dv")
        zld[bi] = dvv
        kb.wait("sync", "dv", dvv)
        xov[bi] = kb.dma("sync", x2[ch * 128:(ch + 1) * 128, :], xo[bi][:], sem=f"xo{bi}")
    kb.wait("sync", "xo0", xov[0])
    if standalone:
        return kb.finalize()



def _bf(a):
    a = np.asarray(a)
    return a.view(BF) if a.dtype == np.uint16 else a


def kernel_unfused(L0_attn_x, L0_attn_w_qkv, L0_attn_w_out, L0_ffn_gate_w, L0_ffn_gate_bias,
           L0_ffn_fc1_w, L0_ffn_fc2_w, L0_ffn_w1, L0_ffn_w2, L0_ffn_shared_up_w,
           L0_ffn_shared_down_w, L1_attn_Wq, L1_attn_bq, L1_attn_Wk, L1_attn_bk,
           L1_attn_Wv, L1_attn_bv, L1_ffn_w_up_q, L1_ffn_w_up_scale, L1_ffn_b_up,
           L1_ffn_w_up_lora_a, L1_ffn_w_up_lora_b, L1_ffn_w_down_q, L1_ffn_w_down_scale,
           L1_ffn_b_down, L1_ffn_w_down_lora_a, L1_ffn_w_down_lora_b):
    f32 = np.float32
    X = np.asarray(L0_attn_x, f32).reshape(S, D)
    Xfm = np.ascontiguousarray(X.T)
    w_qkv = np.asarray(L0_attn_w_qkv, f32)
    w_out = np.asarray(L0_attn_w_out, f32)
    cores = list(range(NC))

    maps = []
    for c in cores:
        maps.append({
            "x": Xfm,
            "xr": np.ascontiguousarray(Xfm[:, c * TC:(c + 1) * TC]),
            "wqT": np.ascontiguousarray(w_qkv[256 * c:256 * c + 256, :].T),
            "wkT": np.ascontiguousarray(w_qkv[D + 256 * c:D + 256 * c + 256, :].T),
            "wvT": np.ascontiguousarray(w_qkv[2 * D + 256 * c:2 * D + 256 * c + 256, :].T),
            "woT": np.ascontiguousarray(w_out[:, 256 * c:256 * c + 256].T),
            "hb2": np.array([[2.0 * c + 1], [2.0 * c + 2]], f32),
        })
    r = run_bass_kernel_spmd(build_l0_attn(), maps, core_ids=cores)
    x1 = [np.asarray(r.results[c]["x1"], f32) for c in cores]

    common = {
        "gateT": np.ascontiguousarray(np.asarray(L0_ffn_gate_w, f32).T),
        "gbias": np.asarray(L0_ffn_gate_bias, f32).reshape(1, 32),
        "fc1T": np.ascontiguousarray(np.asarray(L0_ffn_fc1_w, f32).T),
        "w1T": np.ascontiguousarray(np.asarray(L0_ffn_w1, f32).transpose(0, 2, 1)),
        "w2T": np.ascontiguousarray(np.asarray(L0_ffn_w2, f32).transpose(0, 2, 1)),
        "fc2T": np.ascontiguousarray(np.asarray(L0_ffn_fc2_w, f32).T),
        "shupT": np.ascontiguousarray(np.asarray(L0_ffn_shared_up_w, f32).T),
        "shdnT": np.ascontiguousarray(np.asarray(L0_ffn_shared_down_w, f32).T),
    }
    maps = [dict(common, x=x1[c]) for c in cores]
    r = run_bass_kernel_spmd(build_l0_ffn(), maps, core_ids=cores)
    x2 = [np.asarray(r.results[c]["x2"], f32) for c in cores]

    common = {
        "Wq": np.asarray(L1_attn_Wq, f32), "Wk": np.asarray(L1_attn_Wk, f32),
        "Wv": np.asarray(L1_attn_Wv, f32),
        "bqT": np.ascontiguousarray(np.asarray(L1_attn_bq, f32).reshape(16, 128).T),
        "bkT": np.ascontiguousarray(np.asarray(L1_attn_bk, f32).reshape(16, 128).T),
        "bv": np.asarray(L1_attn_bv, f32).reshape(1, D),
    }
    maps = [dict(common, x=x2[c]) for c in cores]
    r = run_bass_kernel_spmd(build_l1_qkv(), maps, core_ids=cores)
    q1 = [_bf(r.results[c]["q1"]) for c in cores]
    Kg = np.concatenate([_bf(r.results[c]["k1"]) for c in cores], 0)
    Vg = np.concatenate([_bf(r.results[c]["v1p"]) for c in cores], 0)

    maps = []
    for c in cores:
        maps.append({
            "x": x2[c], "q1": np.ascontiguousarray(q1[c]), "Kg": Kg, "Vg": Vg,
            "qbase": np.array([[TC * c]], f32),
        })
    r = run_bass_kernel_spmd(build_l1_attn(), maps, core_ids=cores)
    x3 = [np.asarray(r.results[c]["x3"], f32) for c in cores]

    common = {
        "upq": np.ascontiguousarray(np.asarray(L1_ffn_w_up_q, np.int32).T),
        "ups": np.ascontiguousarray(np.repeat(np.asarray(L1_ffn_w_up_scale, f32), 64, 1).T),
        "b_up": np.asarray(L1_ffn_b_up, f32).reshape(1, 8192),
        "upA": np.asarray(L1_ffn_w_up_lora_a, f32),
        "upB": np.asarray(L1_ffn_w_up_lora_b, f32),
        "dnq": np.ascontiguousarray(np.asarray(L1_ffn_w_down_q, np.int32).T),
        "dns": np.ascontiguousarray(np.repeat(np.asarray(L1_ffn_w_down_scale, f32), 64, 1).T),
        "b_dn": np.asarray(L1_ffn_b_down, f32).reshape(1, D),
        "dnA": np.asarray(L1_ffn_w_down_lora_a, f32),
        "dnB": np.asarray(L1_ffn_w_down_lora_b, f32),
    }
    maps = [dict(common, x=x3[c]) for c in cores]
    r = run_bass_kernel_spmd(build_l1_ffn(), maps, core_ids=cores)
    x4 = [np.asarray(r.results[c]["x4"], f32) for c in cores]

    out = np.concatenate([x4[c].T for c in cores], 0).reshape(1, S, D)
    return out



def build_fused():
    kb = KB()
    nc = kb.nc
    E, H = 32, 8192
    EI = lambda n, s, d: nc.dram_tensor(n, s, d, kind="ExternalInput").ap()
    xfm = EI("xfm", [D, S], F32)
    xr = EI("xr", [D, TC], F32)
    wqT = EI("wqT", [D, 256], F32)
    wkT = EI("wkT", [D, 256], F32)
    wvT = EI("wvT", [D, 256], F32)
    woT = EI("woT", [256, D], F32)
    hb2 = EI("hb2", [2, 1], F32)
    gateT = EI("gateT", [D, E], F32)
    gbias = EI("gbias", [1, E], F32)
    fc1T = EI("fc1T", [D, 1024], F32)
    w1T = EI("w1T", [E, 1024, 1024], F32)
    w2T = EI("w2T", [E, 1024, 1024], F32)
    fc2T = EI("fc2T", [1024, D], F32)
    shupT = EI("shupT", [D, 4096], F32)
    shdnT = EI("shdnT", [4096, D], F32)
    Wq = EI("Wq", [D, D], F32)
    Wk = EI("Wk", [D, D], F32)
    Wv = EI("Wv", [D, D], F32)
    bqT = EI("bqT", [128, 16], F32)
    bkT = EI("bkT", [128, 16], F32)
    bv = EI("bv", [1, D], F32)
    qbase = EI("qbase", [1, 1], F32)
    upq = EI("upq", [D, H], I32)
    ups = EI("ups", [D, H], F32)
    b_up = EI("b_up", [1, H], F32)
    upA = EI("upA", [D, 16], F32)
    upB = EI("upB", [16, H], F32)
    dnq = EI("dnq", [H, D], I32)
    dns = EI("dns", [H, D], F32)
    b_dn = EI("b_dn", [1, D], F32)
    dnA = EI("dnA", [H, 16], F32)
    dnB = EI("dnB", [16, D], F32)
    y = nc.dram_tensor("y", [D, TC], F32, kind="ExternalOutput").ap()

    x1_i = kb.dram("x1_i", [D, TC], F32)
    x2_i = kb.dram("x2_i", [D, TC], F32)
    x3_i = kb.dram("x3_i", [D, TC], F32)
    q1_i = kb.dram("q1_i", [D, TC], BF16)
    k1_i = kb.dram("k1_i", [D, TC], BF16)
    v1p_i = kb.dram("v1p_i", [TC, D + 1], BF16)
    Kg_i = kb.dram("Kg_i", [NC * D, TC], BF16)
    Vg_i = kb.dram("Vg_i", [S, D + 1], BF16)

    kb.begin_phase("pa_", {"x": xfm, "xr": xr, "wqT": wqT, "wkT": wkT, "wvT": wvT,
                           "woT": woT, "hb2": hb2, "x1": x1_i})
    build_l0_attn(kb)
    kb.end_phase()

    kb.begin_phase("pb_", {"x": x1_i, "gateT": gateT, "gbias": gbias, "fc1T": fc1T,
                           "w1T": w1T, "w2T": w2T, "fc2T": fc2T, "shupT": shupT,
                           "shdnT": shdnT, "x2": x2_i})
    build_l0_ffn(kb)
    kb.end_phase()

    kb.begin_phase("pc_", {"x": x2_i, "Wq": Wq, "Wk": Wk, "Wv": Wv, "bqT": bqT,
                           "bkT": bkT, "bv": bv, "q1": q1_i, "k1": k1_i, "v1p": v1p_i})
    build_l1_qkv(kb)
    kb.wait("gpsimd", "stk", kb.val("stk"))
    kb.op("gpsimd", lambda e: e.collective_compute(
        "AllGather", mybir.AluOpType.bypass, replica_groups=[list(range(NC))],
        ins=[k1_i.opt()], outs=[Kg_i.opt()]), "cc", amt=1)
    kb.wait("gpsimd", "stv", kb.val("stv"))
    kb.op("gpsimd", lambda e: e.collective_compute(
        "AllGather", mybir.AluOpType.bypass, replica_groups=[list(range(NC))],
        ins=[v1p_i.opt()], outs=[Vg_i.opt()]), "cc", amt=1)
    kb.end_phase()

    kb.begin_phase("pd_", {"x": x2_i, "q1": q1_i, "Kg": Kg_i, "Vg": Vg_i,
                           "qbase": qbase, "x3": x3_i})
    build_l1_attn(kb)
    kb.end_phase()

    kb.begin_phase("pe_", {"x": x3_i, "upq": upq, "ups": ups, "b_up": b_up,
                           "upA": upA, "upB": upB, "dnq": dnq, "dns": dns,
                           "b_dn": b_dn, "dnA": dnA, "dnB": dnB, "x4": y})
    build_l1_ffn(kb)
    kb.end_phase()
    return kb.finalize()


def kernel_fused(L0_attn_x, L0_attn_w_qkv, L0_attn_w_out, L0_ffn_gate_w,
                 L0_ffn_gate_bias, L0_ffn_fc1_w, L0_ffn_fc2_w, L0_ffn_w1, L0_ffn_w2,
                 L0_ffn_shared_up_w, L0_ffn_shared_down_w, L1_attn_Wq, L1_attn_bq,
                 L1_attn_Wk, L1_attn_bk, L1_attn_Wv, L1_attn_bv, L1_ffn_w_up_q,
                 L1_ffn_w_up_scale, L1_ffn_b_up, L1_ffn_w_up_lora_a, L1_ffn_w_up_lora_b,
                 L1_ffn_w_down_q, L1_ffn_w_down_scale, L1_ffn_b_down,
                 L1_ffn_w_down_lora_a, L1_ffn_w_down_lora_b):
    f32 = np.float32
    X = np.asarray(L0_attn_x, f32).reshape(S, D)
    Xfm = np.ascontiguousarray(X.T)
    w_qkv = np.asarray(L0_attn_w_qkv, f32)
    w_out = np.asarray(L0_attn_w_out, f32)
    cores = list(range(NC))
    common = {
        "xfm": Xfm,
        "gateT": np.ascontiguousarray(np.asarray(L0_ffn_gate_w, f32).T),
        "gbias": np.asarray(L0_ffn_gate_bias, f32).reshape(1, 32),
        "fc1T": np.ascontiguousarray(np.asarray(L0_ffn_fc1_w, f32).T),
        "w1T": np.ascontiguousarray(np.asarray(L0_ffn_w1, f32).transpose(0, 2, 1)),
        "w2T": np.ascontiguousarray(np.asarray(L0_ffn_w2, f32).transpose(0, 2, 1)),
        "fc2T": np.ascontiguousarray(np.asarray(L0_ffn_fc2_w, f32).T),
        "shupT": np.ascontiguousarray(np.asarray(L0_ffn_shared_up_w, f32).T),
        "shdnT": np.ascontiguousarray(np.asarray(L0_ffn_shared_down_w, f32).T),
        "Wq": np.asarray(L1_attn_Wq, f32), "Wk": np.asarray(L1_attn_Wk, f32),
        "Wv": np.asarray(L1_attn_Wv, f32),
        "bqT": np.ascontiguousarray(np.asarray(L1_attn_bq, f32).reshape(16, 128).T),
        "bkT": np.ascontiguousarray(np.asarray(L1_attn_bk, f32).reshape(16, 128).T),
        "bv": np.asarray(L1_attn_bv, f32).reshape(1, D),
        "upq": np.ascontiguousarray(np.asarray(L1_ffn_w_up_q, np.int32).T),
        "ups": np.ascontiguousarray(np.repeat(np.asarray(L1_ffn_w_up_scale, f32), 64, 1).T),
        "b_up": np.asarray(L1_ffn_b_up, f32).reshape(1, 8192),
        "upA": np.asarray(L1_ffn_w_up_lora_a, f32),
        "upB": np.asarray(L1_ffn_w_up_lora_b, f32),
        "dnq": np.ascontiguousarray(np.asarray(L1_ffn_w_down_q, np.int32).T),
        "dns": np.ascontiguousarray(np.repeat(np.asarray(L1_ffn_w_down_scale, f32), 64, 1).T),
        "b_dn": np.asarray(L1_ffn_b_down, f32).reshape(1, D),
        "dnA": np.asarray(L1_ffn_w_down_lora_a, f32),
        "dnB": np.asarray(L1_ffn_w_down_lora_b, f32),
    }
    maps = []
    for c in cores:
        maps.append(dict(
            common,
            xr=np.ascontiguousarray(Xfm[:, c * TC:(c + 1) * TC]),
            wqT=np.ascontiguousarray(w_qkv[256 * c:256 * c + 256, :].T),
            wkT=np.ascontiguousarray(w_qkv[D + 256 * c:D + 256 * c + 256, :].T),
            wvT=np.ascontiguousarray(w_qkv[2 * D + 256 * c:2 * D + 256 * c + 256, :].T),
            woT=np.ascontiguousarray(w_out[:, 256 * c:256 * c + 256].T),
            hb2=np.array([[2.0 * c + 1], [2.0 * c + 2]], f32),
            qbase=np.array([[TC * c]], f32),
        ))
    r = run_bass_kernel_spmd(build_fused(), maps, core_ids=cores)
    out = np.concatenate([np.asarray(r.results[c]["y"], f32).T for c in cores], 0)
    return out.reshape(1, S, D)


kernel = kernel_fused


# version 07 (final)
# speedup vs baseline: 1.0193x; 1.0132x over previous
import numpy as np
import ml_dtypes
import concourse.bass as bass
import concourse.mybir as mybir
from concourse.bass_utils import run_bass_kernel_spmd
from concourse.alu_op_type import AluOpType as OP
from contextlib import ExitStack

F32 = mybir.dt.float32
BF16 = mybir.dt.bfloat16
I32 = mybir.dt.int32
AF = mybir.ActivationFunctionType
AX = mybir.AxisListType
BF = ml_dtypes.bfloat16

NC = 8
S = 8192
D = 2048
TC = S // NC
NCH = D // 128

LN_EPS = 1e-5
L0_HEADS = 16
L0_DH = 128
L0_SCALE = 1.0 / np.sqrt(L0_DH)
ALIBI_MAX = 8.0
L1_SCALE = float(np.asarray(D, np.float32) ** (-0.5))
B0 = 24.0

ENGINES = ("sync", "scalar", "vector", "tensor", "gpsimd")


class KB:

    def __init__(self, name="k"):
        self.nc = bass.Bass("TRN2", target_bir_lowering=False)
        self.st = ExitStack()
        self.streams = {e: [] for e in ENGINES}
        self._sems = {}
        self.semv = {}

    def sem(self, name):
        if name not in self._sems:
            self._sems[name] = self.st.enter_context(self.nc.semaphore(name))
            self.semv[name] = 0
        return self._sems[name]

    def sb(self, name, shape, dt):
        name = getattr(self, "_prefix", "") + name
        st = getattr(self, "_phase_st", None) or self.st
        return st.enter_context(self.nc.sbuf_tensor(name, shape, dt))

    def ps(self, name, shape, dt=F32):
        if not hasattr(self, "_ps_cache"):
            self._ps_cache = {}
        if name not in self._ps_cache:
            self._ps_cache[name] = self.st.enter_context(self.nc.psum_tensor(name, shape, dt))
        return self._ps_cache[name]

    def input(self, name, shape, dt):
        io = getattr(self, "io", None)
        if io is not None and name in io:
            return io[name]
        return self.nc.dram_tensor(name, shape, dt, kind="ExternalInput").ap()

    def output(self, name, shape, dt):
        io = getattr(self, "io", None)
        if io is not None and name in io:
            return io[name]
        return self.nc.dram_tensor(name, shape, dt, kind="ExternalOutput").ap()

    def begin_phase(self, prefix, io):
        self._prefix = prefix
        self.io = io
        self._phase_st = ExitStack()
        self._arena_mark = len(getattr(self, "_arenas", []))

    def end_phase(self):
        self.full_barrier()
        for a in reversed(self._arenas[self._arena_mark:] if hasattr(self, "_arenas") else []):
            a.free()
        if hasattr(self, "_arenas"):
            del self._arenas[self._arena_mark:]
        self._phase_st.close()
        self._phase_st = None
        self._prefix = ""
        self.io = None

    def full_barrier(self):
        for eng in ENGINES:
            for s, v in list(self.semv.items()):
                if v > 0:
                    self.wait(eng, s, v)

    def dram(self, name, shape, dt, kind="Internal"):
        return self.nc.dram_tensor(name, shape, dt, kind=kind).ap()

    def arena(self):
        from concourse.bass import AutoArena
        a = AutoArena(self.nc)
        if not hasattr(self, "_arenas"):
            self._arenas = []
        self._arenas.append(a)
        return a

    def barrier(self, sems=("pe", "dv", "ac", "po")):
        for eng in ENGINES:
            for s in sems:
                self.wait(eng, s, self.val(s))

    def wait(self, eng, name, val):
        if val is None or val <= 0:
            return
        s = self.sem(name)
        self.streams[eng].append(lambda e, s=s, v=val: e.wait_ge(s, v))

    DEFSEM = {"vector": "dv", "scalar": "ac", "gpsimd": "po"}

    def op(self, eng, fn, sem=None, amt=1):
        d = self.DEFSEM.get(eng)
        if d is not None:
            if self.semv.get(d, 0) > 0:
                self.wait(eng, d, self.semv[d])
            if sem is None:
                sem, amt = d, 1
        if sem is None:
            self.streams[eng].append(fn)
            return None
        s = self.sem(sem)
        self.semv[sem] += amt
        v = self.semv[sem]

        def run(e, f=fn, s=s, a=amt):
            f(e).then_inc(s, a)

        self.streams[eng].append(run)
        return v

    def dma(self, q, out, in_, sem="ld", **kw):
        return self.op(q, lambda e, o=out, i=in_, k=kw: e.dma_start(out=o, in_=i, **k),
                       sem, 16)

    def val(self, name):
        return self.semv.get(name, 0)

    def finalize(self):
        block = self.st.enter_context(self.nc.Block())
        for eng in ENGINES:
            stream = self.streams[eng]
            if not stream:
                continue
            dec = getattr(block, eng)

            def body(e, st_=stream):
                for f in st_:
                    f(e)

            dec(body)
        for a in reversed(getattr(self, "_arenas", [])):
            a.free()
        self.st.close()
        return self.nc



def emit_const_tiles(kb):
    ones_row = kb.sb("c_ones_row", [2, 512], BF16)
    ones_col = kb.sb("c_ones_col", [128, 1], BF16)
    kb.op("vector", lambda e: e.memset(ones_row[:], 1.0), "dv")
    kb.op("vector", lambda e: e.memset(ones_col[:], 1.0), "dv")
    return ones_row, ones_col


def emit_ln_fm(kb, x_slices, t_w, name, sc, out_cb=None):
    nhalf = (t_w + 511) // 512
    pe_done = None
    for ch in range(16):
        xs = x_slices[ch]
        v = kb.op("vector", lambda e, ch=ch, xs=xs: e.tensor_copy(sc["xbf"][:], xs()), "dv")
        a = kb.op("scalar", lambda e, ch=ch: e.square(sc["xsq"][:], sc["xbf"][:]), "ac")
        kb.wait("tensor", "dv", v)
        kb.wait("tensor", "ac", a)
        for hf in range(nhalf):
            w = min(512, t_w - hf * 512)
            st = (ch == 0)
            sp = (ch == 15)
            kb.op("tensor", lambda e, ch=ch, hf=hf, w=w, st=st, sp=sp: e.matmul(
                sc["ps_sum"][0:1, hf * 512:hf * 512 + w],
                sc["onescol_bf"][:], sc["xbf"][:, hf * 512:hf * 512 + w],
                start=st, stop=sp), "pe")
            pe_done = kb.op("tensor", lambda e, ch=ch, hf=hf, w=w, st=st, sp=sp: e.matmul(
                sc["ps_sq"][0:1, hf * 512:hf * 512 + w],
                sc["onescol_bf"][:], sc["xsq"][:, hf * 512:hf * 512 + w],
                start=st, stop=sp), "pe")
    kb.wait("vector", "pe", pe_done)
    kb.op("vector", lambda e: e.tensor_scalar(sc["mean"][0:1, 0:t_w], sc["ps_sum"][0:1, 0:t_w],
                                              1.0 / D, None, op0=OP.mult), "dv")
    kb.op("vector", lambda e: e.tensor_scalar(sc["msq"][0:1, 0:t_w], sc["ps_sq"][0:1, 0:t_w],
                                              1.0 / D, None, op0=OP.mult), "dv")
    kb.op("vector", lambda e: e.tensor_tensor(sc["tmp"][0:1, 0:t_w], sc["mean"][0:1, 0:t_w],
                                              sc["mean"][0:1, 0:t_w], op=OP.mult), "dv")
    dvv = kb.op("vector", lambda e: e.tensor_tensor(sc["var"][0:1, 0:t_w], sc["msq"][0:1, 0:t_w],
                                                    sc["tmp"][0:1, 0:t_w], op=OP.subtract), "dv")
    kb.wait("scalar", "dv", dvv)
    acv = kb.op("scalar", lambda e: e.activation(sc["std"][0:1, 0:t_w], sc["var"][0:1, 0:t_w],
                                                 AF.Sqrt, bias=LN_EPS, scale=1.0), "ac")
    kb.wait("vector", "ac", acv)
    kb.op("vector", lambda e: e.reciprocal(sc["a_row"][0:1, 0:t_w], sc["std"][0:1, 0:t_w]), "dv")
    dvv = kb.op("vector", lambda e: e.tensor_tensor(sc["b_row"][0:1, 0:t_w], sc["mean"][0:1, 0:t_w],
                                                    sc["a_row"][0:1, 0:t_w], op=OP.mult), "dv")
    dvv = kb.op("vector", lambda e: e.tensor_scalar(sc["b_row"][0:1, 0:t_w], sc["b_row"][0:1, 0:t_w],
                                                    -1.0, None, op0=OP.mult), "dv")
    kb.op("vector", lambda e: e.tensor_copy(sc["a_row_bf"][0:1, 0:t_w], sc["a_row"][0:1, 0:t_w]), "dv")
    dvv = kb.op("vector", lambda e: e.tensor_copy(sc["b_row_bf"][0:1, 0:t_w], sc["b_row"][0:1, 0:t_w]), "dv")
    kb.wait("tensor", "dv", dvv)
    pev = None
    for hf in range(nhalf):
        w = min(512, t_w - hf * 512)
        kb.op("tensor", lambda e, hf=hf, w=w: e.matmul(
            sc["ps_a"][:, hf * 512:hf * 512 + w], sc["onesrow_bf"][0:1, 0:128],
            sc["a_row_bf"][0:1, hf * 512:hf * 512 + w], start=True, stop=True), "pe")
        pev = kb.op("tensor", lambda e, hf=hf, w=w: e.matmul(
            sc["ps_b"][:, hf * 512:hf * 512 + w], sc["onesrow_bf"][0:1, 0:128],
            sc["b_row_bf"][0:1, hf * 512:hf * 512 + w], start=True, stop=True), "pe")
    kb.wait("vector", "pe", pev)
    last = None
    for ch in range(16):
        xs = x_slices[ch]
        kb.op("vector", lambda e, ch=ch, xs=xs: e.tensor_tensor(
            sc["tmp128"][:, 0:t_w], xs(), sc["ps_a"][:, 0:t_w], op=OP.mult), "dv")
        if out_cb is None:
            last = kb.op("vector", lambda e, ch=ch, xs=xs: e.tensor_tensor(
                xs(), sc["tmp128"][:, 0:t_w], sc["ps_b"][:, 0:t_w], op=OP.add), "dv")
        else:
            last = out_cb(ch)
    return last


def emit_ln_scratch(kb, t_w):
    sc = {}
    sc["xbf"] = kb.sb("ln_xbf", [128, t_w], BF16)
    sc["xsq"] = kb.sb("ln_xsq", [128, t_w], BF16)
    sc["onescol_bf"] = kb.sb("ln_ones_col", [128, 1], BF16)
    sc["onesrow_bf"] = kb.sb("ln_ones_row", [1, 128], BF16)
    kb.op("vector", lambda e: e.memset(sc["onescol_bf"][:], 1.0), "dv")
    kb.op("vector", lambda e: e.memset(sc["onesrow_bf"][:], 1.0), "dv")
    for nm in ("mean", "msq", "tmp", "var", "std", "a_row", "b_row"):
        sc[nm] = kb.sb("ln_" + nm, [1, t_w], F32)
    sc["a_row_bf"] = kb.sb("ln_a_bf", [1, t_w], BF16)
    sc["b_row_bf"] = kb.sb("ln_b_bf", [1, t_w], BF16)
    sc["tmp128"] = kb.sb("ln_t128", [128, t_w], F32)
    sc["ps_sum"] = kb.ps("ln_ps_sum", [1, t_w], F32)
    sc["ps_sq"] = kb.ps("ln_ps_sq", [1, t_w], F32)
    sc["ps_a"] = kb.ps("ln_ps_a", [128, t_w], F32)
    sc["ps_b"] = kb.ps("ln_ps_b", [128, t_w], F32)
    return sc



def build_l1_qkv(kb=None):
    standalone = kb is None
    if standalone:
        kb = KB()
    nc = kb.nc
    x = kb.input("x", [D, TC], F32)
    Wq = kb.input("Wq", [D, D], F32)
    Wk = kb.input("Wk", [D, D], F32)
    Wv = kb.input("Wv", [D, D], F32)
    bqT = kb.input("bqT", [128, 16], F32)
    bkT = kb.input("bkT", [128, 16], F32)
    bv = kb.input("bv", [1, D], F32)
    q1 = kb.output("q1", [D, TC], BF16)
    k1 = kb.output("k1", [D, TC], BF16)
    v1p = kb.output("v1p", [TC, D + 1], BF16)

    x_bf = kb.sb("x_bf", [128, 16 * TC], BF16)
    wtile = [kb.sb(f"wt{i}", [128, 16 * 512], BF16) for i in range(2)]
    bq_sb = kb.sb("bq_sb", [128, 16], F32)
    bk_sb = kb.sb("bk_sb", [128, 16], F32)
    bv_bf = kb.sb("bv_bf", [1, D], BF16)
    bv_f32 = kb.sb("bv_f32", [1, D], F32)
    onesrow = kb.sb("onesrow", [1, 128], BF16)
    q_all = kb.sb("q_all", [128, 16 * TC], BF16)
    k_all = kb.sb("k_all", [128, 16 * TC], BF16)
    v_all = kb.sb("v_all", [128, 8 * (D + 1)], BF16)
    ps = [kb.ps(f"pool{i}", [128, 512], F32) for i in range(4)]

    xg = kb.dma("gpsimd", x_bf.ap().rearrange("p (c t) -> p c t", c=16),
                x.rearrange("(c p) t -> p c t", p=128), sem="wgx")
    kb.dma("sync", bq_sb[:], bqT, sem="ld")
    kb.dma("sync", bk_sb[:], bkT, sem="ld")
    kb.dma("sync", bv_f32[:], bv, sem="ld")
    ldv = kb.val("ld")
    kb.wait("vector", "ld", ldv)
    kb.wait("vector", "wgx", xg)
    kb.wait("tensor", "wgx", xg)
    kb.op("vector", lambda e: e.memset(onesrow[:], 1.0), "dv")
    kb.op("vector", lambda e: e.tensor_copy(bv_bf[:], bv_f32[:]), "dv")
    vav = kb.op("vector", lambda e: e.memset(
        v_all.ap().rearrange("p (t d) -> p t d", t=8)[:, :, D:D + 1], 1.0), "dv")
    dvv = kb.op("vector", lambda e: e.tensor_scalar(bq_sb[:], bq_sb[:], L1_SCALE, None,
                                                    op0=OP.mult), "dv")
    kb.wait("scalar", "dv", dvv)
    kb.wait("tensor", "dv", dvv)

    wfree = {0: 0, 1: 0}
    psfree = [0, 0, 0, 0]

    def qk_proj(Wap, bias_sb, out_all, scale):
        for blk in range(4):
            bi = blk % 2
            kb.wait("gpsimd", "pe", wfree[bi])
            wld = kb.dma("gpsimd",
                         wtile[bi].ap().rearrange("p (c d) -> p c d", c=16),
                         Wap[:, blk * 512:(blk + 1) * 512].rearrange("(c p) d -> p c d", p=128),
                         sem=f"wg{bi}")
            kb.wait("tensor", f"wg{bi}", wld)
            pv = None
            for sub in range(4):
                ch = blk * 4 + sub
                for hf in range(2):
                    pi = (sub * 2 + hf) % 4
                    kb.wait("tensor", "ac", psfree[pi])
                    for kc in range(16):
                        pv = kb.op("tensor", lambda e, bi=bi, sub=sub, hf=hf, kc=kc, pi=pi: e.matmul(
                            ps[pi][:, 0:512],
                            wtile[bi][:, kc * 512 + sub * 128: kc * 512 + sub * 128 + 128],
                            x_bf[:, kc * TC + hf * 512: kc * TC + hf * 512 + 512],
                            start=(kc == 0), stop=(kc == 15)), "pe")
                    kb.wait("scalar", "pe", pv)
                    av = kb.op("scalar", lambda e, ch=ch, hf=hf, pi=pi, bias_sb=bias_sb,
                               scale=scale, out_all=out_all: e.activation(
                        out_all[:, ch * TC + hf * 512: ch * TC + hf * 512 + 512],
                        ps[pi][:, 0:512], AF.Identity,
                        bias=bias_sb[:, ch:ch + 1], scale=scale), "ac")
                    psfree[pi] = av
            wfree[bi] = pv

    qk_proj(Wk, bk_sb, k_all, 1.0)
    kv = kb.val("ac")
    kb.wait("sync", "ac", kv)
    kb.dma("sync", k1.rearrange("(c p) t -> p c t", p=128),
           k_all.ap().rearrange("p (c t) -> p c t", c=16), sem="stk")

    for blk in range(4):
        bi = blk % 2
        kb.wait("gpsimd", "pe", wfree[bi])
        wld = kb.dma("gpsimd",
                     wtile[bi].ap().rearrange("p (c d) -> p c d", c=16),
                     Wv[:, blk * 512:(blk + 1) * 512].rearrange("(c p) d -> p c d", p=128),
                     sem=f"wg{bi}")
        kb.wait("tensor", f"wg{bi}", wld)
        pv = None
        for tt in range(8):
            pi = tt % 4
            kb.wait("tensor", "ac", psfree[pi])
            for kc in range(16):
                kb.op("tensor", lambda e, bi=bi, tt=tt, kc=kc, pi=pi: e.matmul(
                    ps[pi][:, 0:512],
                    x_bf[:, kc * TC + tt * 128: kc * TC + tt * 128 + 128],
                    wtile[bi][:, kc * 512: kc * 512 + 512],
                    start=(kc == 0), stop=False), "pe")
            pv = kb.op("tensor", lambda e, blk=blk, pi=pi: e.matmul(
                ps[pi][:, 0:512], onesrow[0:1, 0:128],
                bv_bf[0:1, blk * 512:(blk + 1) * 512], start=False, stop=True), "pe")
            kb.wait("scalar", "pe", pv)
            av = kb.op("scalar", lambda e, blk=blk, tt=tt, pi=pi: e.activation(
                v_all[:, tt * (D + 1) + blk * 512: tt * (D + 1) + blk * 512 + 512],
                ps[pi][:, 0:512], AF.Identity, bias=0.0, scale=1.0), "ac")
            psfree[pi] = av
        wfree[bi] = pv

    acv = kb.val("ac")
    kb.wait("sync", "ac", acv)
    kb.wait("sync", "dv", vav)
    kb.dma("sync", v1p.rearrange("(t p) d -> p t d", p=128),
           v_all.ap().rearrange("p (t d) -> p t d", t=8), sem="stv")
    qk_proj(Wq, bq_sb, q_all, L1_SCALE)
    acv = kb.val("ac")
    kb.wait("sync", "ac", acv)
    kb.dma("sync", q1.rearrange("(c p) t -> p c t", p=128),
           q_all.ap().rearrange("p (c t) -> p c t", c=16), sem="stq")
    kb.wait("sync", "stk", kb.val("stk"))
    kb.wait("sync", "stv", kb.val("stv"))
    kb.wait("sync", "stq", kb.val("stq"))
    if standalone:
        return kb.finalize()



def build_l1_attn(kb=None):
    standalone = kb is None
    if standalone:
        kb = KB()
    nc = kb.nc
    x = kb.input("x", [D, TC], F32)
    q1 = kb.input("q1", [D, TC], BF16)
    Kg = kb.input("Kg", [NC * D, TC], BF16)
    Vg = kb.input("Vg", [S, D + 1], BF16)
    qbase = kb.input("qbase", [1, 1], F32)
    x3 = kb.output("x3", [D, TC], F32)
    z_dram = kb.dram("z_dram", [D, TC], F32)

    EXPB = -20.0

    q_sb = kb.sb("q_sb", [128, 16 * TC], BF16)
    acc = kb.sb("acc", [128, 8 * (D + 1)], F32)
    iota_row = kb.sb("iota_row", [1, TC], F32)
    qb_sb = kb.sb("qb_sb", [1, 1], F32)
    qpos_row = kb.sb("qpos_row", [1, TC], F32)
    qpos_bf = kb.sb("qpos_bf", [1, TC], BF16)
    qpos_bc = kb.sb("qpos_bc", [128, TC], F32)
    kcols = kb.sb("kcols", [128, 64], F32)
    onesrow = kb.sb("onesrow", [1, 128], BF16)
    rcol = kb.sb("rcol", [128, 1], F32)
    cb = kb.sb("cb", [128, 4], F32)
    attn_bf = kb.sb("attn_bf", [128, D], BF16)
    ln_xbf = [kb.sb("ln_xbf0", [128, 128], BF16), kb.sb("ln_xbf1", [128, 128], BF16)]
    ln_xsq = [kb.sb("ln_xsq0", [128, 128], BF16), kb.sb("ln_xsq1", [128, 128], BF16)]
    ln_onescol = kb.sb("ln_onescol", [128, 1], BF16)
    ln_rows = kb.sb("ln_rows", [1, 2 * TC], F32)
    ln_b = kb.sb("ln_b", [1, TC], F32)
    ln_a_bf = kb.sb("ln_a_bf", [1, TC], BF16)
    ln_b_bf = kb.sb("ln_b_bf", [1, TC], BF16)
    arA = kb.arena()
    kblk = [arA.sbuf_tensor(f"kblk{i}", [128, 16 * 512], BF16) for i in range(2)]
    vblk = [arA.sbuf_tensor(f"vblk{i}", [128, 4 * (D + 1)], BF16) for i in range(2)]
    ptil = arA.sbuf_tensor("ptil", [128, 512], BF16)
    ptil2 = arA.sbuf_tensor("ptil2", [128, 512], BF16)
    msk = arA.sbuf_tensor("msk", [128, 512], BF16)
    arA.allocate()

    pool = [kb.ps(f"pool{i}", [128, 512], F32) for i in range(6)]
    pt = kb.ps("pt", [128, 1024], BF16)
    pe1 = kb.ps("pe1", [128, 16], F32)

    qv = kb.dma("sync", q_sb.ap().rearrange("p (c t) -> p c t", c=16),
                q1.rearrange("(c p) t -> p c t", p=128), sem="ld")
    kb.dma("sync", qb_sb[:], qbase, sem="ld")
    ldv = kb.val("ld")
    kb.op("gpsimd", lambda e: e.iota(iota_row[:], pattern=[[1, TC]], base=0,
                                     channel_multiplier=0,
                                     allow_small_or_imprecise_dtypes=True), "po")
    pov = kb.op("gpsimd", lambda e: e.iota(kcols[:], pattern=[[128, 64]], base=0,
                                           channel_multiplier=1,
                                           allow_small_or_imprecise_dtypes=True), "po")
    kb.wait("vector", "ld", ldv)
    kb.wait("vector", "po", pov)
    kb.op("vector", lambda e: e.memset(onesrow[:], 1.0), "dv")
    kb.op("vector", lambda e: e.memset(ln_onescol[:], 1.0), "dv")
    kb.op("vector", lambda e: e.memset(cb[:, 0:1], EXPB), "dv")
    cbv = kb.op("vector", lambda e: e.memset(cb[:, 1:2], LN_EPS), "dv")
    kb.wait("scalar", "dv", cbv)
    kb.op("vector", lambda e: e.tensor_scalar(qpos_row[:], iota_row[:], qb_sb[0:1, 0:1],
                                              None, op0=OP.add), "dv")
    dvv = kb.op("vector", lambda e: e.tensor_copy(qpos_bf[:], qpos_row[:]), "dv")
    kb.wait("tensor", "dv", dvv)
    pev = None
    for hf in range(2):
        pev = kb.op("tensor", lambda e, hf=hf: e.matmul(
            pool[hf][:, 0:512], onesrow[0:1, 0:128],
            qpos_bf[0:1, hf * 512:hf * 512 + 512], start=True, stop=True), "pe")
    kb.wait("vector", "pe", pev)
    dvv = None
    for hf in range(2):
        dvv = kb.op("vector", lambda e, hf=hf: e.tensor_copy(
            qpos_bc[:, hf * 512:hf * 512 + 512], pool[hf][:, 0:512]), "dv")
    qpos_ready = dvv

    kfree = [0, 0]
    vfree = [0, 0]
    psS_free = [0, 0]
    psO_free = 0
    ptil_free = 0
    msk_free = 0
    kb.wait("vector", "dv", qpos_ready)

    for ks in range(16):
        bi = ks % 2
        r, half = ks // 2, ks % 2
        kb.wait("sync", "pe", kfree[bi])
        kv = kb.dma("sync", kblk[bi].ap().rearrange("p (c t) -> p c t", c=16),
                    Kg[r * D:(r + 1) * D, half * 512:(half + 1) * 512]
                    .rearrange("(c p) t -> p c t", p=128), sem=f"kld{bi}")
        kb.wait("sync", "pe", vfree[bi])
        vv = kb.dma("sync", vblk[bi].ap().rearrange("p (c d) -> p c d", c=4),
                    Vg[ks * 512:(ks + 1) * 512, :].rearrange("(c p) d -> p c d", p=128),
                    sem=f"vld{bi}")
        kb.wait("tensor", f"kld{bi}", kv)
        kb.wait("tensor", f"vld{bi}", vv)
        for qt in range(8):
            si = qt % 2
            psS = pool[4 + si]
            kb.wait("tensor", "ac", psS_free[si])
            spev = None
            for j in range(4):
                for kc in range(16):
                    spev = kb.op("tensor", lambda e, bi=bi, j=j, kc=kc, qt=qt, psS=psS: e.matmul(
                        psS[:, j * 128:j * 128 + 128],
                        kblk[bi][:, kc * 512 + j * 128: kc * 512 + j * 128 + 128],
                        q_sb[:, kc * TC + qt * 128: kc * TC + qt * 128 + 128],
                        start=(kc == 0), stop=(kc == 15)), "pe")
            kb.wait("scalar", "pe", spev)
            kb.wait("scalar", "pe", ptil_free)
            av = kb.op("scalar", lambda e, psS=psS: e.activation(
                ptil[:], psS[:, 0:512], AF.Exp, bias=cb[:, 0:1], scale=1.0), "ac")
            psS_free[si] = av
            kb.wait("vector", "pe", msk_free)
            mv = None
            for j in range(4):
                kc = ks * 4 + j
                mv = kb.op("vector", lambda e, j=j, kc=kc, qt=qt: e.tensor_scalar(
                    msk[:, j * 128:j * 128 + 128],
                    qpos_bc[:, qt * 128:qt * 128 + 128],
                    kcols[:, kc:kc + 1], None, op0=OP.is_ge), "dv")
            kb.wait("vector", "ac", av)
            mav = kb.op("vector", lambda e: e.tensor_tensor(ptil2[:], ptil[:], msk[:],
                                                            op=OP.mult), "dv")
            kb.wait("tensor", "dv", mav)
            kb.wait("tensor", "dv", psO_free)
            ppv = None
            for j in range(4):
                for nb in range(4):
                    ppv = kb.op("tensor", lambda e, bi=bi, j=j, nb=nb: e.matmul(
                        pool[nb][:, 0:512],
                        ptil2[:, j * 128:j * 128 + 128],
                        vblk[bi][:, j * (D + 1) + nb * 512: j * (D + 1) + nb * 512 + 512],
                        start=(j == 0), stop=(j == 3)), "pe")
                ppv = kb.op("tensor", lambda e, bi=bi, j=j: e.matmul(
                    pe1[:, 0:1],
                    ptil2[:, j * 128:j * 128 + 128],
                    vblk[bi][:, j * (D + 1) + D: j * (D + 1) + D + 1],
                    start=(j == 0), stop=(j == 3)), "pe")
            ptil_free = ppv
            msk_free = ppv
            if qt == 7:
                kfree[bi] = ppv
                vfree[bi] = ppv
            kb.wait("vector", "pe", ppv)
            dvv = None
            top = OP.add if ks > 0 else None
            for nb in range(4):
                c0 = qt * (D + 1) + nb * 512
                if ks == 0:
                    dvv = kb.op("vector", lambda e, nb=nb, c0=c0: e.tensor_copy(
                        acc[:, c0:c0 + 512], pool[nb][:, 0:512]), "dv")
                else:
                    dvv = kb.op("vector", lambda e, nb=nb, c0=c0: e.tensor_tensor(
                        acc[:, c0:c0 + 512], pool[nb][:, 0:512],
                        acc[:, c0:c0 + 512], op=OP.add), "dv")
            c1 = qt * (D + 1) + D
            if ks == 0:
                dvv = kb.op("vector", lambda e, c1=c1: e.tensor_copy(
                    acc[:, c1:c1 + 1], pe1[:, 0:1]), "dv")
            else:
                dvv = kb.op("vector", lambda e, c1=c1: e.tensor_tensor(
                    acc[:, c1:c1 + 1], pe1[:, 0:1], acc[:, c1:c1 + 1], op=OP.add), "dv")
            psO_free = dvv

    kb.barrier()
    arA.free()
    arB = kb.arena()
    xq_sb = [arB.sbuf_tensor(f"xq_sb{i}", [128, 16 * 128], F32) for i in range(2)]
    z_stage = [arB.sbuf_tensor(f"z_stage{i}", [128, 16 * 128], F32) for i in range(2)]
    zc = [arB.sbuf_tensor(f"zc{i}", [128, TC], F32) for i in range(2)]
    xo = [arB.sbuf_tensor(f"xo{i}", [128, TC], F32) for i in range(2)]
    idn = arB.sbuf_tensor("idn", [128, 128], BF16)
    onesq = arB.sbuf_tensor("onesq", [128, 128], BF16)
    arB.allocate()
    kb.op("vector", lambda e: e.memset(onesq[:], 1.0), "dv")
    kb.wait("gpsimd", "dv", kb.val("dv"))
    idv = kb.op("gpsimd", lambda e: e.affine_select(idn[:], onesq[:], pattern=[[1, 128]],
                                                    compare_op=OP.is_equal, fill=0.0,
                                                    base=0, channel_multiplier=-1), "po")
    kb.wait("tensor", "po", idv)
    accv = kb.val("dv")
    xq_cons = [0, 0]
    abf_cons = 0
    lnx_cons = 0
    lnx_cons2 = [0, 0]
    pt_free = 0
    zst_free = [0, 0]
    statv = None
    for qt in range(8):
        bi = qt % 2
        kb.wait("vector", "dv", accv)
        kb.wait("sync", "dv", xq_cons[bi])
        xv = kb.dma("sync", xq_sb[bi].ap().rearrange("p (c q) -> p c q", c=16),
                    x[:, qt * 128:(qt + 1) * 128].rearrange("(c p) q -> p c q", p=128),
                    sem=f"xq{bi}")
        kb.wait("vector", "pe", abf_cons)
        kb.op("vector", lambda e, qt=qt: e.reciprocal(
            rcol[:], acc[:, qt * (D + 1) + D: qt * (D + 1) + D + 1]), "dv")
        dvv = None
        for nb in range(4):
            dvv = kb.op("vector", lambda e, qt=qt, nb=nb: e.tensor_scalar(
                attn_bf[:, nb * 512:nb * 512 + 512],
                acc[:, qt * (D + 1) + nb * 512: qt * (D + 1) + nb * 512 + 512],
                rcol[:], None, op0=OP.mult), "dv")
        kb.wait("tensor", "dv", dvv)
        kb.wait("vector", f"xq{bi}", xv)
        for rd in range(2):
            kb.wait("tensor", "dv", pt_free)
            pev = None
            for c8 in range(8):
                ch = rd * 8 + c8
                pev = kb.op("tensor", lambda e, ch=ch, c8=c8: e.transpose(
                    pt[:, c8 * 128: c8 * 128 + 128],
                    attn_bf[:, ch * 128:ch * 128 + 128], idn[:]), "pe")
            kb.wait("vector", "pe", pev)
            kb.wait("vector", f"zst{bi}", zst_free[bi])
            pt_free = kb.op("vector", lambda e, rd=rd, bi=bi: e.tensor_tensor(
                z_stage[bi][:, rd * 1024:(rd + 1) * 1024], pt[:, 0:1024],
                xq_sb[bi][:, rd * 1024:(rd + 1) * 1024], op=OP.add), "dv")
            if rd == 1:
                xq_cons[bi] = pt_free
                abf_cons = pev
        kb.wait("sync", "dv", pt_free)
        zv = kb.dma("sync", z_dram[:, qt * 128:(qt + 1) * 128]
                    .rearrange("(c p) q -> p c q", p=128),
                    z_stage[bi].ap().rearrange("p (c q) -> p c q", c=16), sem=f"zst{bi}")
        zst_free[bi] = zv
        pse_s = pool[0] if qt < 4 else pool[1]
        pse_q = pool[2] if qt < 4 else pool[3]
        col = (qt % 4) * 128
        for ch in range(16):
            li = ch % 2
            kb.wait("vector", "pe", lnx_cons2[li])
            dvv = kb.op("vector", lambda e, ch=ch, bi=bi, li=li: e.tensor_copy(
                ln_xbf[li][:], z_stage[bi][:, ch * 128:ch * 128 + 128]), "dv")
            kb.wait("scalar", "dv", dvv)
            acv = kb.op("scalar", lambda e, li=li: e.square(ln_xsq[li][:], ln_xbf[li][:]), "ac")
            kb.wait("tensor", "dv", dvv)
            kb.wait("tensor", "ac", acv)
            kb.op("tensor", lambda e, ch=ch, col=col, pse_s=pse_s, li=li: e.matmul(
                pse_s[0:1, col:col + 128], ln_onescol[:], ln_xbf[li][:],
                start=(ch == 0), stop=(ch == 15)), "pe")
            statv = kb.op("tensor", lambda e, ch=ch, col=col, pse_q=pse_q, li=li: e.matmul(
                pse_q[0:1, col:col + 128], ln_onescol[:], ln_xsq[li][:],
                start=(ch == 0), stop=(ch == 15)), "pe")
            lnx_cons2[li] = statv

    mean = lambda: ln_rows[0:1, 0:TC]
    msq = lambda: ln_rows[0:1, TC:2 * TC]
    tmp = lambda: ln_b[0:1, 0:TC]
    var = lambda: ln_rows[0:1, TC:2 * TC]
    std = lambda: ln_rows[0:1, TC:2 * TC]
    arow = lambda: ln_rows[0:1, TC:2 * TC]
    kb.wait("vector", "pe", statv)
    for hf in range(2):
        kb.op("vector", lambda e, hf=hf: e.tensor_scalar(
            mean()[0:1, hf * 512:hf * 512 + 512], pool[hf][0:1, 0:512],
            1.0 / D, None, op0=OP.mult), "dv")
        kb.op("vector", lambda e, hf=hf: e.tensor_scalar(
            msq()[0:1, hf * 512:hf * 512 + 512], pool[2 + hf][0:1, 0:512],
            1.0 / D, None, op0=OP.mult), "dv")
    kb.op("vector", lambda e: e.tensor_tensor(tmp(), mean(), mean(), op=OP.mult), "dv")
    dvv = kb.op("vector", lambda e: e.tensor_tensor(var(), msq(), tmp(), op=OP.subtract), "dv")
    kb.wait("scalar", "dv", dvv)
    acv = kb.op("scalar", lambda e: e.activation(std(), var(), AF.Sqrt,
                                                 bias=cb[0:1, 1:2], scale=1.0), "ac")
    kb.wait("vector", "ac", acv)
    kb.op("vector", lambda e: e.reciprocal(arow(), std()), "dv")
    kb.op("vector", lambda e: e.tensor_tensor(ln_b[0:1, :], mean(), arow(), op=OP.mult), "dv")
    kb.op("vector", lambda e: e.tensor_scalar(ln_b[0:1, :], ln_b[0:1, :], -1.0, None,
                                              op0=OP.mult), "dv")
    kb.op("vector", lambda e: e.tensor_copy(ln_a_bf[:], arow()), "dv")
    dvv = kb.op("vector", lambda e: e.tensor_copy(ln_b_bf[:], ln_b[0:1, :]), "dv")
    kb.wait("tensor", "dv", dvv)
    pev = None
    for hf in range(2):
        kb.op("tensor", lambda e, hf=hf: e.matmul(
            pool[hf][:, 0:512], onesrow[0:1, 0:128],
            ln_a_bf[0:1, hf * 512:hf * 512 + 512], start=True, stop=True), "pe")
        pev = kb.op("tensor", lambda e, hf=hf: e.matmul(
            pool[2 + hf][:, 0:512], onesrow[0:1, 0:128],
            ln_b_bf[0:1, hf * 512:hf * 512 + 512], start=True, stop=True), "pe")
    kb.wait("vector", "pe", pev)
    zld = [0, 0]
    xov = [0, 0]
    for ch in range(16):
        bi = ch % 2
        kb.wait("sync", "dv", zld[bi])
        kb.wait("sync", f"zst{0}", zst_free[0])
        kb.wait("sync", f"zst{1}", zst_free[1])
        zv = kb.dma("sync", zc[bi][:], z_dram[ch * 128:(ch + 1) * 128, :], sem=f"zld{bi}")
        kb.wait("vector", f"zld{bi}", zv)
        kb.wait("vector", f"xo{bi}", xov[bi])
        dvv = None
        for hf in range(2):
            kb.op("vector", lambda e, bi=bi, hf=hf: e.tensor_tensor(
                xo[bi][:, hf * 512:hf * 512 + 512], zc[bi][:, hf * 512:hf * 512 + 512],
                pool[hf][:, 0:512], op=OP.mult), "dv")
            dvv = kb.op("vector", lambda e, bi=bi, hf=hf: e.tensor_tensor(
                xo[bi][:, hf * 512:hf * 512 + 512], xo[bi][:, hf * 512:hf * 512 + 512],
                pool[2 + hf][:, 0:512], op=OP.add), "dv")
        zld[bi] = dvv
        kb.wait("sync", "dv", dvv)
        xov[bi] = kb.dma("sync", x3[ch * 128:(ch + 1) * 128, :], xo[bi][:], sem=f"xo{bi}")
    kb.wait("sync", f"xo{0}", xov[0])
    kb.wait("sync", f"xo{1}", xov[1])
    if standalone:
        return kb.finalize()



def build_l1_ffn(kb=None):
    standalone = kb is None
    if standalone:
        kb = KB()
    nc = kb.nc
    H = 8192
    x = kb.input("x", [D, TC], F32)
    upq = kb.input("upq", [D, H], I32)
    ups = kb.input("ups", [D, H], F32)
    b_up = kb.input("b_up", [1, H], F32)
    upA = kb.input("upA", [D, 16], F32)
    upB = kb.input("upB", [16, H], F32)
    dnq = kb.input("dnq", [H, D], I32)
    dns = kb.input("dns", [H, D], F32)
    b_dn = kb.input("b_dn", [1, D], F32)
    dnA = kb.input("dnA", [H, 16], F32)
    dnB = kb.input("dnB", [16, D], F32)
    x4 = kb.output("x4", [D, TC], F32)
    z_dram = kb.dram("zf_dram", [D, TC], F32)
    wupc = kb.dram("wupc_dq", [D, H], BF16)
    wdnc = kb.dram("wdnc_dq", [H, D], BF16)

    x_bf = kb.sb("x_bf", [128, 16 * 512], BF16)
    x2_bf = kb.sb("x2_bf", [128, 64 * 512], BF16)
    upA_sb = kb.sb("upA_sb", [128, 16 * 16], BF16)
    dnA_sb = kb.sb("dnA_sb", [128, 64 * 16], BF16)
    upB_sb = kb.sb("upB_sb", [16, H], BF16)
    dnB_sb = kb.sb("dnB_sb", [16, D], BF16)
    bup_bf = kb.sb("bup_bf", [1, H], BF16)
    bdn_bf = kb.sb("bdn_bf", [1, D], BF16)
    u1_bf = kb.sb("u1_bf", [16, 512], BF16)
    u2_bf = kb.sb("u2_bf", [16, 512], BF16)
    onesrow = kb.sb("onesrow", [1, 512], BF16)
    xr = [kb.sb(f"xr{i}", [128, 512], F32) for i in range(2)]
    zs = [kb.sb(f"zs{i}", [128, 512], F32) for i in range(2)]
    zbf = kb.sb("zbf", [128, 512], BF16)
    zsq = kb.sb("zsq", [128, 512], BF16)
    onescol = kb.sb("onescol", [128, 1], BF16)
    cb = kb.sb("cb", [128, 2], F32)
    ln_rows = kb.sb("ln_rows", [1, 2 * TC], F32)
    ln_b = kb.sb("ln_b", [1, TC], F32)
    ln_a_bf = kb.sb("ln_a_bf", [1, TC], BF16)
    ln_b_bf = kb.sb("ln_b_bf", [1, TC], BF16)
    arW = kb.arena()
    wq = [arW.sbuf_tensor(f"wq{i}", [128, 4096], BF16) for i in range(2)]
    ws = [arW.sbuf_tensor(f"ws{i}", [128, 4096], BF16) for i in range(2)]
    arW.allocate()

    pool = [kb.ps(f"pool{i}", [128, 512], F32) for i in range(6)]

    kb.dma("gpsimd", upA_sb.ap().rearrange("p (c r) -> p c r", c=16),
           upA.rearrange("(c p) r -> p c r", p=128), sem="wgx")
    kb.dma("gpsimd", dnA_sb.ap().rearrange("p (c r) -> p c r", c=64),
           dnA.rearrange("(c p) r -> p c r", p=128), sem="wgx")
    kb.dma("gpsimd", upB_sb[:], upB, sem="wgx")
    kb.dma("gpsimd", dnB_sb[:], dnB, sem="wgx")
    kb.dma("gpsimd", bup_bf[:], b_up, sem="wgx")
    xgv = kb.dma("gpsimd", bdn_bf[:], b_dn, sem="wgx")
    kb.op("vector", lambda e: e.memset(onesrow[:], 1.0))
    kb.op("vector", lambda e: e.memset(onescol[:], 1.0))
    dvv = kb.op("vector", lambda e: e.memset(cb[:, 0:1], LN_EPS), "dv")
    kb.wait("tensor", "dv", dvv)
    kb.wait("tensor", "wgx", xgv)
    kb.wait("scalar", "wgx", xgv)

    wfree = {0: 0, 1: 0}
    wback = {0: 0, 1: 0}
    psfree = {4: 0, 5: 0}
    statv = None
    zsfree = [0, 0]
    xrfree = [0, 0]
    zstv = [0, 0]
    lnx_cons = 0
    xbf_cons = 0
    x2_cons = 0
    for th in range(2):
        kb.wait("gpsimd", "pe", xbf_cons)
        xv = kb.dma("gpsimd", x_bf.ap().rearrange("p (c t) -> p c t", c=16),
                    x[:, th * 512:(th + 1) * 512].rearrange("(c p) t -> p c t", p=128),
                    sem="xbf")
        kb.wait("tensor", "xbf", xv)
        kb.wait("tensor", "ac", psfree[5])
        pv = None
        for kc in range(16):
            pv = kb.op("tensor", lambda e, kc=kc: e.matmul(
                pool[5][0:16, 0:512], upA_sb[:, kc * 16:kc * 16 + 16],
                x_bf[:, kc * 512:(kc + 1) * 512],
                start=(kc == 0), stop=(kc == 15)), "pe")
        kb.wait("scalar", "pe", pv)
        u1v = kb.op("scalar", lambda e: e.activation(
            u1_bf[0:16, 0:512], pool[5][0:16, 0:512],
            AF.Identity, bias=0.0, scale=1.0), "ac")
        psfree[5] = u1v
        kb.wait("tensor", "ac", u1v)
        for hb in range(32):
            bi = hb % 2
            if th == 0:
                kb.wait("gpsimd", "pe", wfree[bi])
                kb.wait("gpsimd", f"wb{bi}", wback[bi])
                kb.dma("gpsimd", wq[bi].ap().rearrange("p (c d) -> p c d", c=16),
                       upq[:, hb * 256:(hb + 1) * 256].rearrange("(c p) d -> p c d", p=128),
                       sem=f"wq{bi}")
                wv1 = kb.val(f"wq{bi}")
                kb.dma("gpsimd", ws[bi].ap().rearrange("p (c d) -> p c d", c=16),
                       ups[:, hb * 256:(hb + 1) * 256].rearrange("(c p) d -> p c d", p=128),
                       sem=f"ws{bi}")
                wv2 = kb.val(f"ws{bi}")
                kb.wait("vector", f"wq{bi}", wv1)
                kb.wait("vector", f"ws{bi}", wv2)
                dqv = kb.op("vector", lambda e, bi=bi: e.scalar_tensor_tensor(
                    wq[bi][:], wq[bi][:], -7.5, ws[bi][:], op0=OP.add, op1=OP.mult), "dv")
                kb.wait("sync", "dv", dqv)
                wback[bi] = kb.dma(
                    "sync", wupc[:, hb * 256:(hb + 1) * 256].rearrange("(c p) d -> p c d", p=128),
                    wq[bi].ap().rearrange("p (c d) -> p c d", c=16), sem=f"wb{bi}")
                kb.wait("tensor", "dv", dqv)
            else:
                kb.wait("sync", "pe", wfree[bi])
                kb.wait("sync", f"wb{bi}", wback[bi])
                wv1 = kb.dma("sync", wq[bi].ap().rearrange("p (c d) -> p c d", c=16),
                             wupc[:, hb * 256:(hb + 1) * 256]
                             .rearrange("(c p) d -> p c d", p=128), sem=f"wc{bi}")
                kb.wait("tensor", f"wc{bi}", wv1)
            for sub in range(2):
                hchunk = hb * 2 + sub
                pi = 4 + (hchunk % 2)
                kb.wait("tensor", "ac", psfree[pi])
                for kc in range(16):
                    kb.op("tensor", lambda e, bi=bi, sub=sub, kc=kc, pi=pi: e.matmul(
                        pool[pi][:, 0:512],
                        wq[bi][:, kc * 256 + sub * 128: kc * 256 + sub * 128 + 128],
                        x_bf[:, kc * 512:(kc + 1) * 512],
                        start=(kc == 0), stop=False), "pe")
                kb.op("tensor", lambda e, hchunk=hchunk, pi=pi: e.matmul(
                    pool[pi][:, 0:512],
                    upB_sb[0:16, hchunk * 128:(hchunk + 1) * 128],
                    u1_bf[0:16, 0:512], start=False, stop=False), "pe")
                pv = kb.op("tensor", lambda e, hchunk=hchunk, pi=pi: e.matmul(
                    pool[pi][:, 0:512],
                    bup_bf[0:1, hchunk * 128:(hchunk + 1) * 128],
                    onesrow[0:1, 0:512], start=False, stop=True), "pe")
                if sub == 1:
                    wfree[bi] = pv
                kb.wait("scalar", "pe", pv)
                psfree[pi] = kb.op("scalar", lambda e, hchunk=hchunk, pi=pi: e.activation(
                    x2_bf[:, hchunk * 512:(hchunk + 1) * 512],
                    pool[pi][:, 0:512], AF.Relu, bias=0.0, scale=1.0), "ac")
        x2v = kb.val("ac")
        xbf_cons = pv
        kb.wait("tensor", "ac", x2v)
        for hc in range(64):
            pv = kb.op("tensor", lambda e, hc=hc: e.matmul(
                pool[5][0:16, 0:512], dnA_sb[:, hc * 16:hc * 16 + 16],
                x2_bf[:, hc * 512:(hc + 1) * 512],
                start=(hc == 0), stop=(hc == 63)), "pe")
        kb.wait("scalar", "pe", pv)
        u2v = kb.op("scalar", lambda e: e.activation(
            u2_bf[0:16, 0:512], pool[5][0:16, 0:512],
            AF.Identity, bias=0.0, scale=1.0), "ac")
        psfree[5] = u2v
        kb.wait("tensor", "ac", u2v)
        for ch in range(16):
            pi = 4 + (ch % 2)
            kb.wait("tensor", "dv", zsfree[pi - 4])
            for hh in range(2):
                bi = (ch * 2 + hh) % 2
                if th == 0:
                    kb.wait("gpsimd", "pe", wfree[bi])
                    kb.wait("gpsimd", f"wb{bi}", wback[bi])
                    kb.dma("gpsimd", wq[bi].ap().rearrange("p (c d) -> p c d", c=32),
                           dnq[hh * 4096:(hh + 1) * 4096, ch * 128:(ch + 1) * 128]
                           .rearrange("(c p) d -> p c d", p=128), sem=f"wq{bi}")
                    wv1 = kb.val(f"wq{bi}")
                    kb.dma("gpsimd", ws[bi].ap().rearrange("p (c d) -> p c d", c=32),
                           dns[hh * 4096:(hh + 1) * 4096, ch * 128:(ch + 1) * 128]
                           .rearrange("(c p) d -> p c d", p=128), sem=f"ws{bi}")
                    wv2 = kb.val(f"ws{bi}")
                    kb.wait("vector", f"wq{bi}", wv1)
                    kb.wait("vector", f"ws{bi}", wv2)
                    dqv = kb.op("vector", lambda e, bi=bi: e.scalar_tensor_tensor(
                        wq[bi][:], wq[bi][:], -7.5, ws[bi][:], op0=OP.add, op1=OP.mult), "dv")
                    kb.wait("sync", "dv", dqv)
                    wback[bi] = kb.dma(
                        "sync", wdnc[hh * 4096:(hh + 1) * 4096, ch * 128:(ch + 1) * 128]
                        .rearrange("(c p) d -> p c d", p=128),
                        wq[bi].ap().rearrange("p (c d) -> p c d", c=32), sem=f"wb{bi}")
                    kb.wait("tensor", "dv", dqv)
                else:
                    kb.wait("sync", "pe", wfree[bi])
                    kb.wait("sync", f"wb{bi}", wback[bi])
                    wv1 = kb.dma("sync", wq[bi].ap().rearrange("p (c d) -> p c d", c=32),
                                 wdnc[hh * 4096:(hh + 1) * 4096, ch * 128:(ch + 1) * 128]
                                 .rearrange("(c p) d -> p c d", p=128), sem=f"wc{bi}")
                    kb.wait("tensor", f"wc{bi}", wv1)
                for hc in range(32):
                    pv = kb.op("tensor", lambda e, bi=bi, hh=hh, hc=hc, pi=pi: e.matmul(
                        pool[pi][:, 0:512],
                        wq[bi][:, hc * 128:(hc + 1) * 128],
                        x2_bf[:, (hh * 32 + hc) * 512:(hh * 32 + hc) * 512 + 512],
                        start=(hh == 0 and hc == 0), stop=False), "pe")
                wfree[bi] = pv
            kb.op("tensor", lambda e, ch=ch, pi=pi: e.matmul(
                pool[pi][:, 0:512], dnB_sb[0:16, ch * 128:(ch + 1) * 128],
                u2_bf[0:16, 0:512], start=False, stop=False), "pe")
            pv = kb.op("tensor", lambda e, ch=ch, pi=pi: e.matmul(
                pool[pi][:, 0:512], bdn_bf[0:1, ch * 128:(ch + 1) * 128],
                onesrow[0:1, 0:512], start=False, stop=True), "pe")
            xbi = ch % 2
            kb.wait("sync", "dv", xrfree[xbi])
            xrv = kb.dma("sync", xr[xbi][:],
                         x[ch * 128:(ch + 1) * 128, th * 512:(th + 1) * 512], sem=f"xr{xbi}")
            kb.wait("vector", "pe", pv)
            kb.wait("vector", f"xr{xbi}", xrv)
            zi = ch % 2
            kb.wait("vector", f"zst{zi}", zstv[zi])
            kb.wait("vector", "pe", lnx_cons)
            zv = kb.op("vector", lambda e, zi=zi, xbi=xbi, pi=pi: e.tensor_tensor(
                zs[zi][:], pool[pi][:, 0:512], xr[xbi][:], op=OP.add), "dv")
            zsfree[pi - 4] = zv
            xrfree[xbi] = zv
            zbv = kb.op("vector", lambda e, zi=zi: e.tensor_copy(zbf[:], zs[zi][:]), "dv")
            kb.wait("scalar", "dv", zbv)
            sqv = kb.op("scalar", lambda e: e.square(zsq[:], zbf[:]), "ac")
            kb.wait("tensor", "dv", zbv)
            kb.wait("tensor", "ac", sqv)
            pse_s = pool[0] if th == 0 else pool[1]
            pse_q = pool[2] if th == 0 else pool[3]
            kb.op("tensor", lambda e, ch=ch, pse_s=pse_s: e.matmul(
                pse_s[0:1, 0:512], onescol[:], zbf[:],
                start=(ch == 0), stop=(ch == 15)), "pe")
            statv = kb.op("tensor", lambda e, ch=ch, pse_q=pse_q: e.matmul(
                pse_q[0:1, 0:512], onescol[:], zsq[:],
                start=(ch == 0), stop=(ch == 15)), "pe")
            lnx_cons = statv
            kb.wait("sync", "dv", zv)
            zstv[zi] = kb.dma("sync",
                              z_dram[ch * 128:(ch + 1) * 128, th * 512:(th + 1) * 512],
                              zs[zi][:], sem=f"zst{zi}")

    kb.barrier()
    arW.free()
    arZ = kb.arena()
    zc = [arZ.sbuf_tensor(f"zc{i}", [128, TC], F32) for i in range(2)]
    xo = [arZ.sbuf_tensor(f"xo{i}", [128, TC], F32) for i in range(2)]
    arZ.allocate()
    mean = lambda: ln_rows[0:1, 0:TC]
    s2 = lambda: ln_rows[0:1, TC:2 * TC]
    kb.wait("vector", "pe", statv)
    for th in range(2):
        kb.op("vector", lambda e, th=th: e.tensor_scalar(
            mean()[0:1, th * 512:th * 512 + 512], pool[th][0:1, 0:512],
            1.0 / D, None, op0=OP.mult), "dv")
        kb.op("vector", lambda e, th=th: e.tensor_scalar(
            s2()[0:1, th * 512:th * 512 + 512], pool[2 + th][0:1, 0:512],
            1.0 / D, None, op0=OP.mult), "dv")
    kb.op("vector", lambda e: e.tensor_tensor(ln_b[0:1, :], mean(), mean(), op=OP.mult), "dv")
    dvv = kb.op("vector", lambda e: e.tensor_tensor(s2(), s2(), ln_b[0:1, :],
                                                    op=OP.subtract), "dv")
    kb.wait("scalar", "dv", dvv)
    acv = kb.op("scalar", lambda e: e.activation(s2(), s2(), AF.Sqrt,
                                                 bias=cb[0:1, 0:1], scale=1.0), "ac")
    kb.wait("vector", "ac", acv)
    kb.op("vector", lambda e: e.reciprocal(s2(), s2()), "dv")
    kb.op("vector", lambda e: e.tensor_tensor(ln_b[0:1, :], mean(), s2(), op=OP.mult), "dv")
    kb.op("vector", lambda e: e.tensor_scalar(ln_b[0:1, :], ln_b[0:1, :], -1.0, None,
                                              op0=OP.mult), "dv")
    kb.op("vector", lambda e: e.tensor_copy(ln_a_bf[:], s2()), "dv")
    dvv = kb.op("vector", lambda e: e.tensor_copy(ln_b_bf[:], ln_b[0:1, :]), "dv")
    kb.wait("tensor", "dv", dvv)
    pev = None
    for hf in range(2):
        kb.op("tensor", lambda e, hf=hf: e.matmul(
            pool[hf][:, 0:512], onesrow[0:1, 0:128],
            ln_a_bf[0:1, hf * 512:hf * 512 + 512], start=True, stop=True), "pe")
        pev = kb.op("tensor", lambda e, hf=hf: e.matmul(
            pool[2 + hf][:, 0:512], onesrow[0:1, 0:128],
            ln_b_bf[0:1, hf * 512:hf * 512 + 512], start=True, stop=True), "pe")
    kb.wait("vector", "pe", pev)
    zld = [0, 0]
    xov = [0, 0]
    kb.wait("sync", "zst0", kb.val("zst0"))
    kb.wait("sync", "zst1", kb.val("zst1"))
    for ch in range(16):
        bi = ch % 2
        kb.wait("sync", "dv", zld[bi])
        zv = kb.dma("sync", zc[bi][:], z_dram[ch * 128:(ch + 1) * 128, :], sem=f"zld{bi}")
        kb.wait("vector", f"zld{bi}", zv)
        kb.wait("vector", f"xo{bi}", xov[bi])
        dvv = None
        for hf in range(2):
            kb.op("vector", lambda e, bi=bi, hf=hf: e.tensor_tensor(
                xo[bi][:, hf * 512:hf * 512 + 512], zc[bi][:, hf * 512:hf * 512 + 512],
                pool[hf][:, 0:512], op=OP.mult), "dv")
            dvv = kb.op("vector", lambda e, bi=bi, hf=hf: e.tensor_tensor(
                xo[bi][:, hf * 512:hf * 512 + 512], xo[bi][:, hf * 512:hf * 512 + 512],
                pool[2 + hf][:, 0:512], op=OP.add), "dv")
        zld[bi] = dvv
        kb.wait("sync", "dv", dvv)
        xov[bi] = kb.dma("sync", x4[ch * 128:(ch + 1) * 128, :], xo[bi][:], sem=f"xo{bi}")
    kb.wait("sync", "xo0", xov[0])
    kb.wait("sync", "xo1", xov[1])
    if standalone:
        return kb.finalize()



F16 = mybir.dt.float16

def build_l0_attn(kb=None, dbg=False):
    standalone = kb is None
    if standalone:
        kb = KB()
    nc = kb.nc
    x = kb.input("x", [D, S], F32)
    xr = kb.input("xr", [D, TC], F32)
    wqT = kb.input("wqT", [D, 256], F32)
    wkT = kb.input("wkT", [D, 256], F32)
    wvT = kb.input("wvT", [D, 256], F32)
    woT = kb.input("woT", [256, D], F32)
    hb2 = kb.input("hb2", [2, 1], F32)
    x1 = kb.output("x1", [D, TC], F32)
    rs_in = kb.dram("rs_in", [S, D], BF16)
    rs_out = kb.dram("rs_out", [TC, D], BF16)
    z_dram = kb.dram("za_dram", [D, TC], F32)
    sl_dram = kb.dram("sl_dram", [1, 2], F32)
    attn_dbg = nc.dram_tensor("attn_dbg", [S, 256], BF16,
                              kind="ExternalOutput").ap() if dbg else None

    LN2 = float(np.log(2.0))

    q_sb = kb.sb("q_sb", [128, 2 * S], BF16)
    k_sb = kb.sb("k_sb", [128, 2 * S], BF16)
    vp_sb = kb.sb("vp_sb", [128, 64 * 258], BF16)
    wo_sb = kb.sb("wo_sb", [128, 2 * D], BF16)
    ptil = kb.sb("ptil", [128, 512], BF16)
    ptil_b2 = kb.sb("ptil_b", [128, 512], BF16)
    ptmask = kb.sb("ptmask", [128, 128], BF16)
    abf = kb.sb("abf", [128, 256], BF16)
    afm_t = kb.sb("afm_t", [128, 256], BF16)
    rs_stage = [kb.sb(f"rs_stage{i}", [128, D], BF16) for i in range(2)]
    m_diag = kb.sb("m_diag", [128, 128], BF16)
    onesq = kb.sb("onesq", [128, 128], BF16)
    idn = kb.sb("idn", [128, 128], BF16)
    onesrow = kb.sb("onesrow", [1, 512], BF16)
    onescol = kb.sb("onescol", [128, 1], BF16)
    rcol = kb.sb("rcol", [128, 1], F32)
    cb = kb.sb("cb", [128, 2], F32)
    slopes = kb.sb("slopes", [2, 1], F32)
    hcol = kb.sb("hcol", [2, 1], F32)
    slope0 = [kb.sb("slope0a", [1, 1], F32), kb.sb("slope0b", [1, 1], F32)]
    ln_xbf = [kb.sb("ln_xbf0", [128, 128], BF16), kb.sb("ln_xbf1", [128, 128], BF16)]
    ln_xsq = [kb.sb("ln_xsq0", [128, 128], BF16), kb.sb("ln_xsq1", [128, 128], BF16)]
    ln_rows = kb.sb("ln_rows", [1, 2 * TC], F32)
    ln_b = kb.sb("ln_b", [1, TC], F32)
    ln_a_bf = kb.sb("ln_a_bf", [1, TC], BF16)
    ln_b_bf = kb.sb("ln_b_bf", [1, TC], BF16)

    arT = kb.arena()
    crhs = arT.sbuf_tensor("crhs", [3, 2 * S], F16)
    clhs = arT.sbuf_tensor("clhs", [3, 256], F16)
    arT.allocate()
    arS = kb.arena()
    scr = arS.sbuf_tensor("scr", [1, S], F32)
    arS.allocate()

    pool = [kb.ps(f"pool{i}", [128, 512], F32) for i in range(6)]
    pt = kb.ps("pt", [128, 1024], BF16)

    ldv = kb.dma("sync", slopes[:], hb2, sem="ld")
    kb.op("vector", lambda e: e.memset(onesrow[:], 1.0))
    kb.op("vector", lambda e: e.memset(onescol[:], 1.0))
    kb.op("vector", lambda e: e.memset(onesq[:], 1.0))
    kb.op("vector", lambda e: e.memset(cb[:, 0:1], LN_EPS))
    kb.op("vector", lambda e: e.memset(
        vp_sb.ap().rearrange("p (t d) -> p t d", t=64)[:, :, 128:129], 1.0))
    dvv = kb.op("vector", lambda e: e.memset(
        vp_sb.ap().rearrange("p (t d) -> p t d", t=64)[:, :, 257:258], 1.0), "dv")
    kb.wait("gpsimd", "dv", dvv)
    idv = kb.op("gpsimd", lambda e: e.affine_select(idn[:], onesq[:], pattern=[[1, 128]],
                                                    compare_op=OP.is_equal, fill=0.0,
                                                    base=0, channel_multiplier=-1), "po")
    idv = kb.op("gpsimd", lambda e: e.affine_select(m_diag[:], onesq[:], pattern=[[1, 128]],
                                                    compare_op=OP.is_ge, fill=0.0,
                                                    base=0, channel_multiplier=-1), "po")
    kb.wait("scalar", "ld", ldv)
    acv = kb.op("scalar", lambda e: e.activation(slopes[:], slopes[:], AF.Exp,
                                                 bias=0.0, scale=-LN2 * 0.5), "ac")
    kb.wait("sync", "ac", acv)
    s1 = kb.dma("sync", sl_dram.rearrange("a b -> b a"), slopes[:], sem="ld")
    kb.wait("sync", "ld", s1)
    kb.dma("sync", slope0[0][:], sl_dram[0:1, 0:1], sem="ld")
    s2v = kb.dma("sync", slope0[1][:], sl_dram[0:1, 1:2], sem="ld")
    kb.wait("vector", "ld", s2v)
    kb.wait("gpsimd", "ld", s2v)

    for h in range(2):
        kb.wait("gpsimd", "tb", kb.val("tb"))
        kb.wait("vector", "tb", kb.val("tb"))
        pov = kb.op("gpsimd", lambda e: e.iota(scr[:], pattern=[[1, 64], [0, 128]],
                                               base=0, channel_multiplier=0,
                                               allow_small_or_imprecise_dtypes=True), "po")
        kb.wait("vector", "po", pov)
        dvv = kb.op("vector", lambda e, h=h: e.tensor_scalar(
            scr[:], scr[:], slope0[h][0:1, 0:1], -128.0, op0=OP.mult, op1=OP.mult), "dv")
        kb.wait("gpsimd", "dv", dvv)
        g1 = kb.dma("gpsimd", crhs[2:3, h * S:(h + 1) * S], scr[:], sem="tb")
        kb.wait("gpsimd", "tb", g1)
        kb.wait("gpsimd", "tb", kb.val("tb"))
        pov = kb.op("gpsimd", lambda e: e.iota(scr[:], pattern=[[0, 64], [1, 128]],
                                               base=0, channel_multiplier=0,
                                               allow_small_or_imprecise_dtypes=True), "po")
        kb.wait("vector", "po", pov)
        kb.wait("vector", "tb", g1)
        dvv = kb.op("vector", lambda e, h=h: e.tensor_scalar(
            scr[:], scr[:], slope0[h][0:1, 0:1], -1.0, op0=OP.mult, op1=OP.mult), "dv")
        dvv = kb.op("vector", lambda e: e.tensor_scalar(
            scr[:], scr[:], B0, None, op0=OP.subtract), "dv")
        kb.wait("gpsimd", "dv", dvv)
        g2 = kb.dma("gpsimd", crhs[1:2, h * S:(h + 1) * S], scr[:], sem="tb")
        kb.wait("gpsimd", "tb", g2)
        kb.wait("vector", "tb", kb.val("tb"))
        dvv = kb.op("vector", lambda e: e.memset(scr[:], 1.0), "dv")
        kb.wait("gpsimd", "dv", dvv)
        g3 = kb.dma("gpsimd", crhs[0:1, h * S:(h + 1) * S], scr[:], sem="tb")
        kb.wait("gpsimd", "tb", g3)
        kb.wait("gpsimd", "tb", kb.val("tb"))
        pov = kb.op("gpsimd", lambda e: e.iota(scr[0:1, 0:128], pattern=[[1, 128]],
                                               base=0, channel_multiplier=0,
                                               allow_small_or_imprecise_dtypes=True), "po")
        kb.wait("vector", "po", pov)
        dvv = kb.op("vector", lambda e, h=h: e.tensor_scalar(
            scr[0:1, 0:128], scr[0:1, 0:128], slope0[h][0:1, 0:1], None,
            op0=OP.mult), "dv")
        kb.wait("gpsimd", "dv", dvv)
        g4 = kb.dma("gpsimd", clhs[0:1, h * 128:(h + 1) * 128], scr[0:1, 0:128], sem="tb")
        kb.wait("gpsimd", "tb", g4)
        kb.wait("vector", "tb", kb.val("tb"))
        dvv = kb.op("vector", lambda e: e.memset(scr[0:1, 0:256], 1.0), "dv")
        kb.wait("gpsimd", "dv", dvv)
        g5 = kb.dma("gpsimd", clhs[1:2, h * 128:(h + 1) * 128], scr[0:1, 0:128], sem="tb")
        g6 = kb.dma("gpsimd", clhs[2:3, h * 128:(h + 1) * 128], scr[0:1, 128:256], sem="tb")
        kb.wait("gpsimd", "tb", g6)
    tbv = kb.val("tb")
    kb.wait("tensor", "tb", tbv)

    kb.barrier()
    arS.free()
    arP = kb.arena()
    xb = [arP.sbuf_tensor(f"xb{i}", [128, 16 * 256], BF16) for i in range(2)]
    wq_sb = arP.sbuf_tensor("wq_sb", [128, 16 * 256], BF16)
    wk_sb = arP.sbuf_tensor("wk_sb", [128, 16 * 256], BF16)
    wv_sb = arP.sbuf_tensor("wv_sb", [128, 16 * 256], BF16)
    arP.allocate()

    wv1 = kb.dma("gpsimd", wq_sb.ap().rearrange("p (c d) -> p c d", c=16),
                 wqT.rearrange("(c p) d -> p c d", p=128), sem="wg0")
    wv2 = kb.dma("gpsimd", wk_sb.ap().rearrange("p (c d) -> p c d", c=16),
                 wkT.rearrange("(c p) d -> p c d", p=128), sem="wg1")
    wv3 = kb.dma("gpsimd", wv_sb.ap().rearrange("p (c d) -> p c d", c=16),
                 wvT.rearrange("(c p) d -> p c d", p=128), sem="wg2")
    wv4 = kb.dma("gpsimd", wo_sb.ap().rearrange("p (c d) -> p c d", c=2),
                 woT.rearrange("(c p) d -> p c d", p=128), sem="wg3")
    for i, s in enumerate(["wg0", "wg1", "wg2", "wg3"]):
        kb.wait("tensor", s, [wv1, wv2, wv3, wv4][i])

    xbv = [0, 0]
    psfree = [0, 0, 0, 0]
    for tb in range(32):
        bi = tb % 2
        kb.wait("gpsimd", "pe", xbv[bi])
        xv = kb.dma("gpsimd", xb[bi].ap().rearrange("p (c t) -> p c t", c=16),
                    x[:, tb * 256:(tb + 1) * 256].rearrange("(c p) t -> p c t", p=128),
                    sem=f"xb{bi}")
        kb.wait("tensor", f"xb{bi}", xv)
        pv = None
        for hc in range(2):
            for src, dst, scale in ((wq_sb, q_sb, L0_SCALE), (wk_sb, k_sb, 1.0)):
                pi = hc % 2
                kb.wait("tensor", "ac", psfree[pi])
                for kc in range(16):
                    pv = kb.op("tensor", lambda e, bi=bi, hc=hc, kc=kc, src=src, pi=pi: e.matmul(
                        pool[pi][:, 0:256],
                        src[:, kc * 256 + hc * 128: kc * 256 + hc * 128 + 128],
                        xb[bi][:, kc * 256:(kc + 1) * 256],
                        start=(kc == 0), stop=(kc == 15)), "pe")
                kb.wait("scalar", "pe", pv)
                psfree[pi] = kb.op("scalar", lambda e, tb=tb, hc=hc, dst=dst, scale=scale,
                                   pi=pi: e.activation(
                    dst[:, hc * S + tb * 256: hc * S + tb * 256 + 256],
                    pool[pi][:, 0:256], AF.Identity, bias=0.0, scale=scale), "ac")
        for ts in range(2):
            tt = tb * 2 + ts
            pi = 2 + ts % 2
            kb.wait("tensor", "ac", psfree[pi])
            for kc in range(16):
                pv = kb.op("tensor", lambda e, bi=bi, ts=ts, kc=kc, pi=pi: e.matmul(
                    pool[pi][:, 0:256],
                    xb[bi][:, kc * 256 + ts * 128: kc * 256 + ts * 128 + 128],
                    wv_sb[:, kc * 256:(kc + 1) * 256],
                    start=(kc == 0), stop=(kc == 15)), "pe")
            kb.wait("scalar", "pe", pv)
            kb.op("scalar", lambda e, tt=tt, pi=pi: e.activation(
                vp_sb[:, tt * 258: tt * 258 + 128], pool[pi][:, 0:128],
                AF.Identity, bias=0.0, scale=1.0), "ac")
            psfree[pi] = kb.op("scalar", lambda e, tt=tt, pi=pi: e.activation(
                vp_sb[:, tt * 258 + 129: tt * 258 + 257], pool[pi][:, 128:256],
                AF.Identity, bias=0.0, scale=1.0), "ac")
        xbv[bi] = pv

    kb.barrier()
    arP.free()
    projv = kb.val("ac")
    psS_free = [0, 0]
    pfree = [0, 0]
    psO_free = [("ac", projv), ("ac", projv)]
    ptil_free = 0
    abf_cons = 0
    pt_free = 0
    rs_free = [0, 0]
    opfree = [0, 0]
    ptil2b = [ptil, ptil_b2]
    for qt in range(64):
        for h in range(2):
            psO = pool[2 + h]
            kb.wait("tensor", psO_free[h][0], psO_free[h][1])
            ngr = qt // 4 + 1

            def emit_S(g, h=h, qt=qt, ngr=ngr):
                si = g % 2
                psS = pool[si]
                nsl = min(4, qt + 1 - g * 4)
                kb.wait("tensor", "ac", psS_free[si])
                spev = None
                for j in range(nsl):
                    kc = g * 4 + j
                    dd = qt - kc
                    kb.op("tensor", lambda e, h=h, kc=kc, qt=qt, j=j, psS=psS: e.matmul(
                        psS[:, j * 128:(j + 1) * 128],
                        k_sb[:, h * S + kc * 128: h * S + kc * 128 + 128],
                        q_sb[:, h * S + qt * 128: h * S + qt * 128 + 128],
                        start=True, stop=False), "pe")
                    spev = kb.op("tensor", lambda e, h=h, dd=dd, j=j, psS=psS: e.matmul(
                        psS[:, j * 128:(j + 1) * 128],
                        clhs[0:3, h * 128:(h + 1) * 128],
                        crhs[0:3, h * S + dd * 128: h * S + dd * 128 + 128],
                        start=False, stop=True), "pe")
                return spev

            def emit_P(g, spev, h=h, qt=qt, ngr=ngr):
                nonlocal psO_free, ptil_free, pfree
                si = g % 2
                pi = g % 2
                psS = pool[si]
                pt_b = ptil2b[pi]
                nsl = min(4, qt + 1 - g * 4)
                kb.wait("scalar", "pe", spev)
                kb.wait("scalar", "pe", pfree[pi])
                av = kb.op("scalar", lambda e, nsl=nsl, psS=psS, pt_b=pt_b: e.activation(
                    pt_b[:, 0:nsl * 128], psS[:, 0:nsl * 128], AF.Exp,
                    bias=0.0, scale=1.0), "ac")
                psS_free[si] = av
                mav = av
                if g == ngr - 1:
                    kb.wait("vector", "ac", av)
                    kb.wait("vector", "pe", ptil_free)
                    mav = kb.op("vector", lambda e, nsl=nsl, pt_b=pt_b: e.tensor_tensor(
                        ptmask[:], pt_b[:, (nsl - 1) * 128: nsl * 128], m_diag[:],
                        op=OP.mult), "dv")
                    kb.wait("tensor", "dv", mav)
                kb.wait("tensor", "ac", av)
                last = None
                for j in range(nsl):
                    kc = g * 4 + j
                    lhs = (lambda e, nsl=nsl: ptmask[:]) if (g == ngr - 1 and j == nsl - 1) \
                        else (lambda e, j=j, pt_b=pt_b: pt_b[:, j * 128:(j + 1) * 128])
                    last = kb.op("tensor", lambda e, h=h, kc=kc, g=g, j=j,
                                 lhs=lhs, psO=psO, ngr=ngr, nsl=nsl: e.matmul(
                        psO[:, 0:129], lhs(e),
                        vp_sb[:, kc * 258 + 129 * h: kc * 258 + 129 * h + 129],
                        start=(g == 0 and j == 0), stop=(g == ngr - 1 and j == nsl - 1)), "pe")
                pfree[pi] = last
                ptil_free = last

            pend = emit_S(0)
            for g in range(1, ngr):
                pend2 = emit_S(g)
                emit_P(g - 1, pend)
                pend = pend2
            emit_P(ngr - 1, pend)
            kb.wait("vector", "pe", ptil_free)
            kb.wait("vector", "pe", abf_cons)
            kb.op("vector", lambda e, psO=psO: e.reciprocal(rcol[:], psO[:, 128:129]), "dv")
            nv = kb.op("vector", lambda e, h=h, psO=psO: e.tensor_scalar(
                abf[:, h * 128:(h + 1) * 128], psO[:, 0:128], rcol[:], None,
                op0=OP.mult), "dv")
            psO_free[h] = ("dv", nv)
        if dbg:
            kb.wait("sync", "dv", nv)
            kb.dma("sync", nc.__dict__.setdefault("_attn_dbg", None) or attn_dbg[qt * 128:(qt + 1) * 128, :], abf[:], sem="dbg2")
        kb.wait("tensor", "dv", nv)
        kb.wait("tensor", "ac", pt_free)
        pev = None
        for hc in range(2):
            pev = kb.op("tensor", lambda e, hc=hc: e.transpose(
                pt[:, hc * 128:(hc + 1) * 128], abf[:, hc * 128:(hc + 1) * 128],
                idn[:]), "pe")
        kb.wait("scalar", "pe", pev)
        abf_cons = pev
        tev = kb.op("scalar", lambda e: e.activation(
            afm_t[:, 0:256], pt[:, 0:256], AF.Identity, bias=0.0, scale=1.0), "ac")
        pt_free = tev
        kb.wait("tensor", "ac", tev)
        ri = qt % 2
        kb.wait("tensor", f"st{ri}", rs_free[ri])
        opv = None
        for nb in range(4):
            pi = 4 + nb % 2
            kb.wait("tensor", "ac", opfree[pi - 4])
            for hc in range(2):
                opv = kb.op("tensor", lambda e, hc=hc, nb=nb, pi=pi: e.matmul(
                    pool[pi][:, 0:512],
                    afm_t[:, hc * 128:(hc + 1) * 128],
                    wo_sb[:, hc * D + nb * 512: hc * D + nb * 512 + 512],
                    start=(hc == 0), stop=(hc == 1)), "pe")
            kb.wait("scalar", "pe", opv)
            opfree[pi - 4] = kb.op("scalar", lambda e, nb=nb, pi=pi, ri=ri: e.activation(
                rs_stage[ri][:, nb * 512:(nb + 1) * 512], pool[pi][:, 0:512],
                AF.Identity, bias=0.0, scale=1.0), "ac")
        sv = kb.val("ac")
        kb.wait("sync", "ac", sv)
        rs_free[ri] = kb.dma("sync", rs_in[qt * 128:(qt + 1) * 128, :],
                             rs_stage[ri][:], sem=f"st{ri}")
    kb.wait("gpsimd", "st0", kb.val("st0"))
    kb.wait("gpsimd", "st1", kb.val("st1"))

    if dbg:
        kb.wait("sync", "dbg2", kb.val("dbg2"))
        cdbg = kb.output("cdbg", [3, 2 * S], F32)
        ldbg = kb.output("ldbg", [3, 256], F32)
        kb.dma("gpsimd", cdbg, crhs[0:3, :], sem="dbg3")
        db3 = kb.dma("gpsimd", ldbg, clhs[0:3, :], sem="dbg3")
        kb.wait("sync", "dbg3", db3)
        qdbg = kb.output("qdbg", [128, 2 * S], BF16)
        kdbg = kb.output("kdbg", [128, 2 * S], BF16)
        vdbg = kb.output("vdbg", [128, 64 * 258], BF16)
        rdbg = kb.output("rdbg", [S, D], BF16)
        kb.dma("sync", qdbg, q_sb[:], sem="dbg")
        kb.dma("sync", kdbg, k_sb[:], sem="dbg")
        kb.dma("sync", vdbg, vp_sb[:], sem="dbg")
        dbgv = kb.dma("sync", rdbg, rs_in, sem="dbg")
        kb.wait("sync", "dbg", dbgv)

    ccv = kb.op("gpsimd", lambda e: e.collective_compute(
        "ReduceScatter", mybir.AluOpType.add,
        replica_groups=[list(range(NC))],
        ins=[rs_in.opt()], outs=[rs_out.opt()]), "cc", amt=1)
    kb.wait("sync", "cc", ccv)
    kb.wait("vector", "cc", ccv)
    kb.wait("tensor", "cc", ccv)
    kb.wait("scalar", "cc", ccv)

    kb.barrier()
    arT.free()
    arZ = kb.arena()
    rst = [arZ.sbuf_tensor(f"rst{i}", [128, D], BF16) for i in range(2)]
    xq_sb = [arZ.sbuf_tensor(f"xq_sb{i}", [128, 16 * 128], F32) for i in range(2)]
    z_stage = [arZ.sbuf_tensor(f"z_stage{i}", [128, 16 * 128], F32) for i in range(2)]
    zc = [arZ.sbuf_tensor(f"zc{i}", [128, TC], F32) for i in range(2)]
    xo = [arZ.sbuf_tensor(f"xo{i}", [128, TC], F32) for i in range(2)]
    arZ.allocate()
    pt_free = 0
    zst_free = [0, 0]
    statv = None
    lnx_cons = 0
    lnx_cons2 = [0, 0]
    rst_cons = [0, 0]
    xq_cons = [0, 0]
    for qt in range(8):
        bi = qt % 2
        kb.wait("sync", "pe", rst_cons[bi])
        rv = kb.dma("sync", rst[bi][:], rs_out[qt * 128:(qt + 1) * 128, :], sem=f"rst{bi}")
        kb.wait("sync", "dv", xq_cons[bi])
        xv = kb.dma("sync", xq_sb[bi].ap().rearrange("p (c q) -> p c q", c=16),
                    xr[:, qt * 128:(qt + 1) * 128].rearrange("(c p) q -> p c q", p=128),
                    sem=f"xq{bi}")
        kb.wait("tensor", f"rst{bi}", rv)
        kb.wait("vector", f"xq{bi}", xv)
        for rd in range(2):
            kb.wait("tensor", "dv", pt_free)
            pev = None
            for c8 in range(8):
                ch = rd * 8 + c8
                pev = kb.op("tensor", lambda e, ch=ch, c8=c8, bi=bi: e.transpose(
                    pt[:, c8 * 128: c8 * 128 + 128],
                    rst[bi][:, ch * 128:ch * 128 + 128], idn[:]), "pe")
            kb.wait("vector", "pe", pev)
            kb.wait("vector", f"zst{bi}", zst_free[bi])
            pt_free = kb.op("vector", lambda e, rd=rd, bi=bi: e.tensor_tensor(
                z_stage[bi][:, rd * 1024:(rd + 1) * 1024], pt[:, 0:1024],
                xq_sb[bi][:, rd * 1024:(rd + 1) * 1024], op=OP.add), "dv")
            if rd == 1:
                xq_cons[bi] = pt_free
                rst_cons[bi] = pev
        kb.wait("sync", "dv", pt_free)
        zv = kb.dma("sync", z_dram[:, qt * 128:(qt + 1) * 128]
                    .rearrange("(c p) q -> p c q", p=128),
                    z_stage[bi].ap().rearrange("p (c q) -> p c q", c=16), sem=f"zst{bi}")
        zst_free[bi] = zv
        pse_s = pool[0] if qt < 4 else pool[1]
        pse_q = pool[2] if qt < 4 else pool[3]
        col = (qt % 4) * 128
        for ch in range(16):
            li = ch % 2
            kb.wait("vector", "pe", lnx_cons2[li])
            dvv = kb.op("vector", lambda e, ch=ch, bi=bi, li=li: e.tensor_copy(
                ln_xbf[li][:], z_stage[bi][:, ch * 128:ch * 128 + 128]), "dv")
            kb.wait("scalar", "dv", dvv)
            acv = kb.op("scalar", lambda e, li=li: e.square(ln_xsq[li][:], ln_xbf[li][:]), "ac")
            kb.wait("tensor", "dv", dvv)
            kb.wait("tensor", "ac", acv)
            kb.op("tensor", lambda e, ch=ch, col=col, pse_s=pse_s, li=li: e.matmul(
                pse_s[0:1, col:col + 128], onescol[:], ln_xbf[li][:],
                start=(ch == 0), stop=(ch == 15)), "pe")
            statv = kb.op("tensor", lambda e, ch=ch, col=col, pse_q=pse_q, li=li: e.matmul(
                pse_q[0:1, col:col + 128], onescol[:], ln_xsq[li][:],
                start=(ch == 0), stop=(ch == 15)), "pe")
            lnx_cons2[li] = statv

    mean = lambda: ln_rows[0:1, 0:TC]
    s2r = lambda: ln_rows[0:1, TC:2 * TC]
    kb.wait("vector", "pe", statv)
    for hf2 in range(2):
        kb.op("vector", lambda e, hf2=hf2: e.tensor_scalar(
            mean()[0:1, hf2 * 512:hf2 * 512 + 512], pool[hf2][0:1, 0:512],
            1.0 / D, None, op0=OP.mult), "dv")
        kb.op("vector", lambda e, hf2=hf2: e.tensor_scalar(
            s2r()[0:1, hf2 * 512:hf2 * 512 + 512], pool[2 + hf2][0:1, 0:512],
            1.0 / D, None, op0=OP.mult), "dv")
    kb.op("vector", lambda e: e.tensor_tensor(ln_b[0:1, :], mean(), mean(), op=OP.mult), "dv")
    dvv = kb.op("vector", lambda e: e.tensor_tensor(s2r(), s2r(), ln_b[0:1, :],
                                                    op=OP.subtract), "dv")
    kb.wait("scalar", "dv", dvv)
    acv = kb.op("scalar", lambda e: e.activation(s2r(), s2r(), AF.Sqrt,
                                                 bias=cb[0:1, 0:1], scale=1.0), "ac")
    kb.wait("vector", "ac", acv)
    kb.op("vector", lambda e: e.reciprocal(s2r(), s2r()), "dv")
    kb.op("vector", lambda e: e.tensor_tensor(ln_b[0:1, :], mean(), s2r(), op=OP.mult), "dv")
    kb.op("vector", lambda e: e.tensor_scalar(ln_b[0:1, :], ln_b[0:1, :], -1.0, None,
                                              op0=OP.mult), "dv")
    kb.op("vector", lambda e: e.tensor_copy(ln_a_bf[:], s2r()), "dv")
    dvv = kb.op("vector", lambda e: e.tensor_copy(ln_b_bf[:], ln_b[0:1, :]), "dv")
    kb.wait("tensor", "dv", dvv)
    pev = None
    for hf2 in range(2):
        kb.op("tensor", lambda e, hf2=hf2: e.matmul(
            pool[hf2][:, 0:512], onesrow[0:1, 0:128],
            ln_a_bf[0:1, hf2 * 512:hf2 * 512 + 512], start=True, stop=True), "pe")
        pev = kb.op("tensor", lambda e, hf2=hf2: e.matmul(
            pool[2 + hf2][:, 0:512], onesrow[0:1, 0:128],
            ln_b_bf[0:1, hf2 * 512:hf2 * 512 + 512], start=True, stop=True), "pe")
    kb.wait("vector", "pe", pev)
    zld = [0, 0]
    xov = [0, 0]
    kb.wait("sync", "zst0", kb.val("zst0"))
    kb.wait("sync", "zst1", kb.val("zst1"))
    for ch in range(16):
        bi = ch % 2
        kb.wait("sync", "dv", zld[bi])
        zv = kb.dma("sync", zc[bi][:], z_dram[ch * 128:(ch + 1) * 128, :], sem=f"zld{bi}")
        kb.wait("vector", f"zld{bi}", zv)
        kb.wait("vector", f"xo{bi}", xov[bi])
        dvv = None
        for hf2 in range(2):
            kb.op("vector", lambda e, bi=bi, hf2=hf2: e.tensor_tensor(
                xo[bi][:, hf2 * 512:hf2 * 512 + 512], zc[bi][:, hf2 * 512:hf2 * 512 + 512],
                pool[hf2][:, 0:512], op=OP.mult), "dv")
            dvv = kb.op("vector", lambda e, bi=bi, hf2=hf2: e.tensor_tensor(
                xo[bi][:, hf2 * 512:hf2 * 512 + 512], xo[bi][:, hf2 * 512:hf2 * 512 + 512],
                pool[2 + hf2][:, 0:512], op=OP.add), "dv")
        zld[bi] = dvv
        kb.wait("sync", "dv", dvv)
        xov[bi] = kb.dma("sync", x1[ch * 128:(ch + 1) * 128, :], xo[bi][:], sem=f"xo{bi}")
    kb.wait("sync", "xo0", xov[0])
    kb.wait("sync", "xo1", xov[1])
    if standalone:
        return kb.finalize()



def build_l0_ffn(kb=None, dbg=False):
    standalone = kb is None
    if standalone:
        kb = KB()
    nc = kb.nc
    E, G, LAT, INTER, SHI = 32, 4, 1024, 1024, 4096
    x = kb.input("x", [D, TC], F32)
    gateT = kb.input("gateT", [D, E], F32)
    gbias = kb.input("gbias", [1, E], F32)
    fc1T = kb.input("fc1T", [D, LAT], F32)
    w1T = kb.input("w1T", [E, LAT, INTER], F32)
    w2T = kb.input("w2T", [E, INTER, LAT], F32)
    fc2T = kb.input("fc2T", [LAT, D], F32)
    shupT = kb.input("shupT", [D, SHI], F32)
    shdnT = kb.input("shdnT", [SHI, D], F32)
    x2 = kb.output("x2", [D, TC], F32)
    z_dram = kb.dram("zm_dram", [D, TC], F32)
    y0d = kb.output("y0d", [128, 8 * TC], F32) if dbg else None
    h0d = kb.output("h0d", [128, 8 * TC], BF16) if dbg else None
    cw0d = kb.output("cw0d", [128, 8 * E], F32) if dbg else None
    w2d = kb.output("w2d", [128, 8 * 1024], BF16) if dbg else None
    psd = kb.output("psd", [128, 512], F32) if dbg else None
    ysd1 = kb.output("ysd1", [128, 512], F32) if dbg else None
    yse0 = kb.output("yse0", [128, 8 * TC], F32) if dbg else None
    s1d = kb.output("s1d", [128, 32 * 512], BF16) if dbg else None
    zd0 = kb.output("zd0", [128, 512], F32) if dbg else None
    pad = kb.output("pad", [128, 512], F32) if dbg else None
    yse1 = kb.output("yse1", [128, 8 * TC], F32) if dbg else None

    x_bf = kb.sb("x_bf", [128, 16 * TC], BF16)
    x_lat = kb.sb("x_lat", [128, 8 * TC], BF16)
    y_sb = kb.sb("y_sb", [128, 8 * TC], F32)
    hsb = kb.sb("hsb", [128, 8 * TC], BF16)
    cw_all = kb.sb("cw_all", [128, 8 * E], F32)
    gate_sb = kb.sb("gate_sb", [128, 16 * E], BF16)
    glog = kb.sb("glog", [128, 8 * E], F32)
    glog_bf = kb.sb("glog_bf", [32, TC], F32)
    idn32 = kb.sb("idn32", [32, 32], F32)
    onesq32 = kb.sb("onesq32", [32, 32], F32)
    gb_bc = kb.sb("gb_bc", [128, E], F32)
    gb_bf = kb.sb("gb_bf", [1, E], BF16)
    onesrow = kb.sb("onesrow", [1, 512], BF16)
    onescol = kb.sb("onescol", [128, 1], BF16)
    idn = kb.sb("idn", [128, 128], BF16)
    onesq = kb.sb("onesq", [128, 128], BF16)
    cb = kb.sb("cb", [128, 2], F32)
    scr_s = kb.sb("scr_s", [128, E], F32)
    sbv = kb.sb("sbv", [128, E], F32)
    gs = kb.sb("gs", [128, G], F32)
    gwk = kb.sb("gwk", [128, 8], F32)
    gme = kb.sb("gme", [128, E], F32)
    mskd = kb.sb("mskd", [128, E], F32)
    sel = kb.sb("sel", [128, E], F32)
    eqt = kb.sb("eqt", [128, E], F32)
    mcol = kb.sb("mcol", [128, 1], F32)
    m2col = kb.sb("m2col", [128, 1], F32)
    rcol = kb.sb("rcol", [128, 1], F32)
    xr = [kb.sb(f"xr{i}", [128, 512], F32) for i in range(2)]
    zs = [kb.sb(f"zs{i}", [128, 512], F32) for i in range(2)]
    zbf = kb.sb("zbf", [128, 512], BF16)
    zsq = kb.sb("zsq", [128, 512], BF16)
    ln_rows = kb.sb("ln_rows", [1, 2 * TC], F32)
    ln_b = kb.sb("ln_b", [1, TC], F32)
    ln_a_bf = kb.sb("ln_a_bf", [1, TC], BF16)
    ln_b_bf = kb.sb("ln_b_bf", [1, TC], BF16)
    gfl = kb.sb("gfl", [128, 16 * 32], F32)
    glo = kb.sb("glo", [128, 16 * 32], BF16)
    xfc = [kb.sb("xfc0", [128, 512], F32)] * 2
    xlo = [kb.sb("xlo0", [128, 512], BF16)] * 2
    arW = kb.arena()
    wstrip = [arW.sbuf_tensor(f"wstrip{i}", [128, 8 * 1024], BF16) for i in range(2)]
    arW.allocate()

    pool = [kb.ps(f"pool{i}", [128, 512], F32) for i in range(6)]
    pt = kb.ps("pt", [128, 1024], BF16)

    xg = kb.dma("gpsimd", x_bf.ap().rearrange("p (c t) -> p c t", c=16),
                x.rearrange("(c p) t -> p c t", p=128), sem="xbf")
    gldv = kb.dma("gpsimd", gate_sb.ap().rearrange("p (c e) -> p c e", c=16),
                  gateT.rearrange("(c p) e -> p c e", p=128), sem="gld")
    gbv = kb.dma("gpsimd", gb_bf[:], gbias, sem="gbl")
    kb.op("vector", lambda e: e.memset(onesrow[:], 1.0))
    kb.op("vector", lambda e: e.memset(onescol[:], 1.0))
    kb.op("vector", lambda e: e.memset(onesq[:], 1.0))
    dvv = kb.op("vector", lambda e: e.memset(cb[:, 0:1], LN_EPS), "dv")
    for tt in range(8):
        dvv = kb.op("vector", lambda e, tt=tt: e.memset(y_sb[:, tt * LAT:(tt + 1) * LAT], 0.0), "dv")
    kb.wait("gpsimd", "dv", dvv)
    if dbg:
        kb.wait("sync", "dv", dvv)
        ymv = kb.dma("sync", y0d, y_sb[:], sem="dbgm")
        kb.wait("vector", "dbgm", ymv)
    idv = kb.op("gpsimd", lambda e: e.affine_select(idn[:], onesq[:], pattern=[[1, 128]],
                                                    compare_op=OP.is_equal, fill=0.0,
                                                    base=0, channel_multiplier=-1), "po")
    kb.op("vector", lambda e: e.memset(onesq32[:], 1.0), "dv")
    kb.wait("gpsimd", "dv", kb.val("dv"))
    idv = kb.op("gpsimd", lambda e: e.affine_select(idn32[:], onesq32[:], pattern=[[1, 32]],
                                                    compare_op=OP.is_equal, fill=0.0,
                                                    base=0, channel_multiplier=-1), "po")
    kb.wait("tensor", "po", idv)
    kb.wait("tensor", "xbf", xg)
    kb.wait("tensor", "gld", gldv)
    kb.wait("tensor", "gbl", gbv)

    gfv = kb.dma("sync", gfl.ap().rearrange("p (c e) -> p c e", c=16),
                 gateT.rearrange("(c p) e -> p c e", p=128), sem="gf")
    kb.wait("vector", "gf", gfv)
    kb.wait("vector", "gld", gldv)
    kb.wait("vector", "xbf", xg)
    dvv = kb.op("vector", lambda e: e.tensor_tensor(glo[:], gfl[:], gate_sb[:],
                                                    op=OP.subtract), "dv")
    kb.wait("tensor", "dv", dvv)
    pv = None
    xlov = [0, 0]
    xlo_pe = [0, 0]
    for th in range(2):
        pi = th % 2
        for kc in range(16):
            xbi = 0
            kb.wait("sync", "dv", xlov[xbi])
            kb.wait("sync", "pe", xlo_pe[xbi])
            xcv = kb.dma("sync", xfc[xbi][:],
                         x[kc * 128:(kc + 1) * 128, th * 512:(th + 1) * 512], sem=f"xf{xbi}")
            kb.wait("vector", f"xf{xbi}", xcv)
            kb.wait("vector", "pe", xlo_pe[xbi])
            xlov[xbi] = kb.op("vector", lambda e, kc=kc, th=th, xbi=xbi: e.tensor_tensor(
                xlo[xbi][:], xfc[xbi][:],
                x_bf[:, kc * TC + th * 512: kc * TC + th * 512 + 512], op=OP.subtract), "dv")
            kb.wait("tensor", "dv", xlov[xbi])
            kb.op("tensor", lambda e, th=th, kc=kc, pi=pi: e.matmul(
                pool[pi][0:32, 0:512], gate_sb[:, kc * E:(kc + 1) * E],
                x_bf[:, kc * TC + th * 512: kc * TC + th * 512 + 512],
                start=(kc == 0), stop=False), "pe")
            xlo_pe[xbi] = kb.op("tensor", lambda e, th=th, kc=kc, pi=pi, xbi=xbi: e.matmul(
                pool[pi][0:32, 0:512], gate_sb[:, kc * E:(kc + 1) * E],
                xlo[xbi][:], start=False, stop=False), "pe")
            pv = kb.op("tensor", lambda e, th=th, kc=kc, pi=pi: e.matmul(
                pool[pi][0:32, 0:512], glo[:, kc * E:(kc + 1) * E],
                x_bf[:, kc * TC + th * 512: kc * TC + th * 512 + 512],
                start=False, stop=(kc == 15)), "pe")
        kb.wait("scalar", "pe", pv)
        kb.op("scalar", lambda e, th=th, pi=pi: e.activation(
            glog_bf[0:32, th * 512:(th + 1) * 512], pool[pi][0:32, 0:512],
            AF.Identity, bias=0.0, scale=1.0), "ac")
    acv = kb.val("ac")
    kb.wait("tensor", "ac", acv)
    pv = kb.op("tensor", lambda e: e.matmul(
        pool[2][:, 0:E], onesrow[0:1, 0:128], gb_bf[0:1, 0:E], start=True, stop=True), "pe")
    kb.wait("vector", "pe", pv)
    dvv = kb.op("vector", lambda e: e.tensor_copy(gb_bc[:], pool[2][:, 0:E]), "dv")
    ptfree = 0
    for tt in range(8):
        kb.wait("tensor", "dv", ptfree)
        pv = kb.op("tensor", lambda e, tt=tt: e.transpose(
            pool[3][:, 0:32], glog_bf[0:32, tt * 128:(tt + 1) * 128], idn32[:]), "pe")
        kb.wait("vector", "pe", pv)
        ptfree = kb.op("vector", lambda e, tt=tt: e.tensor_copy(
            glog[:, tt * E:(tt + 1) * E], pool[3][:, 0:32]), "dv")

    NEG = -1.0e30
    kb.wait("scalar", "dv", kb.val("dv"))
    prev_dv = 0
    for tt in range(8):
        kb.wait("scalar", "dv", prev_dv)
        acv = kb.op("scalar", lambda e, tt=tt: e.activation(
            scr_s[:], glog[:, tt * E:(tt + 1) * E], AF.Sigmoid, bias=0.0, scale=1.0), "ac")
        kb.wait("vector", "ac", acv)
        kb.op("vector", lambda e: e.tensor_tensor(sbv[:], scr_s[:], gb_bc[:], op=OP.add), "dv")
        for g in range(G):
            kb.op("vector", lambda e, g=g: e.reduce_max(mcol[:], sbv[:, g * 8:(g + 1) * 8],
                                                        axis=AX.X), "dv")
            kb.op("vector", lambda e, g=g: e.tensor_scalar(gwk[:, 0:8], sbv[:, g * 8:(g + 1) * 8],
                                                           mcol[:], None, op0=OP.is_equal), "dv")
            kb.op("vector", lambda e, g=g: e.scalar_tensor_tensor(
                gwk[:, 0:8], gwk[:, 0:8], NEG, sbv[:, g * 8:(g + 1) * 8],
                op0=OP.mult, op1=OP.add), "dv")
            kb.op("vector", lambda e, g=g: e.reduce_max(m2col[:], gwk[:, 0:8], axis=AX.X), "dv")
            kb.op("vector", lambda e, g=g: e.tensor_tensor(gs[:, g:g + 1], mcol[:], m2col[:],
                                                           op=OP.add), "dv")
        kb.op("vector", lambda e: e.reduce_max(mcol[:], gs[:], axis=AX.X), "dv")
        kb.op("vector", lambda e: e.tensor_scalar(gwk[:, 0:G], gs[:], mcol[:], None,
                                                  op0=OP.is_equal), "dv")
        kb.op("vector", lambda e: e.scalar_tensor_tensor(gwk[:, 0:G], gwk[:, 0:G], NEG, gs[:],
                                                         op0=OP.mult, op1=OP.add), "dv")
        kb.op("vector", lambda e: e.reduce_max(m2col[:], gwk[:, 0:G], axis=AX.X), "dv")
        for g in range(G):
            kb.op("vector", lambda e, g=g: e.tensor_scalar(
                mcol[:], gs[:, g:g + 1], m2col[:], None, op0=OP.is_ge), "dv")
            kb.op("vector", lambda e, g=g: e.tensor_scalar(
                gme[:, g * 8:(g + 1) * 8], onesq[:, 0:8].bitcast(F32) if False else gme[:, g * 8:(g + 1) * 8],
                mcol[:], None, op0=OP.mult), "dv") if False else None
            kb.op("vector", lambda e, g=g: e.memset(gme[:, g * 8:(g + 1) * 8], 1.0), "dv")
            kb.op("vector", lambda e, g=g: e.tensor_scalar(
                gme[:, g * 8:(g + 1) * 8], gme[:, g * 8:(g + 1) * 8], mcol[:], None,
                op0=OP.mult), "dv")
        kb.op("vector", lambda e: e.tensor_scalar(gme[:], gme[:], 1.0, NEG,
                                                  op0=OP.subtract, op1=OP.mult), "dv")
        kb.op("vector", lambda e: e.tensor_tensor(mskd[:], sbv[:], gme[:], op=OP.subtract), "dv")
        kb.op("vector", lambda e: e.memset(sel[:], 0.0), "dv")
        for it in range(8):
            kb.op("vector", lambda e: e.reduce_max(mcol[:], mskd[:], axis=AX.X), "dv")
            kb.op("vector", lambda e: e.tensor_scalar(eqt[:], mskd[:], mcol[:], None,
                                                      op0=OP.is_equal), "dv")
            kb.op("vector", lambda e: e.tensor_tensor(sel[:], sel[:], eqt[:], op=OP.max), "dv")
            kb.op("vector", lambda e: e.scalar_tensor_tensor(mskd[:], eqt[:], NEG, mskd[:],
                                                             op0=OP.mult, op1=OP.add), "dv")
        kb.op("vector", lambda e: e.tensor_tensor(eqt[:], scr_s[:], sel[:], op=OP.mult), "dv")
        kb.op("vector", lambda e: e.reduce_sum(mcol[:], eqt[:], axis=AX.X), "dv")
        kb.op("vector", lambda e: e.reciprocal(rcol[:], mcol[:]), "dv")
        prev_dv = kb.op("vector", lambda e, tt=tt: e.tensor_scalar(
            cw_all[:, tt * E:(tt + 1) * E], eqt[:], rcol[:], 2.5,
            op0=OP.mult, op1=OP.mult), "dv")
    cw_done = prev_dv

    kb.wait("tensor", "dv", ptfree)
    wfree = [0, 0]
    psfree = [0, 0, 0, 0]
    for blk in range(2):
        bi = blk % 2
        kb.wait("gpsimd", "pe", wfree[bi])
        wld = kb.dma("gpsimd", wstrip[bi].ap().rearrange("p (c d) -> p c d", c=16)[:, :, 0:512],
                     fc1T[:, blk * 512:(blk + 1) * 512].rearrange("(c p) d -> p c d", p=128),
                     sem=f"w{bi}")
        kb.wait("tensor", f"w{bi}", wld)
        for sub in range(4):
            lc = blk * 4 + sub
            for th in range(2):
                pi = (sub * 2 + th) % 4
                kb.wait("tensor", "ac", psfree[pi])
                pv = None
                for kc in range(16):
                    pv = kb.op("tensor", lambda e, bi=bi, sub=sub, th=th, kc=kc, pi=pi: e.matmul(
                        pool[pi][:, 0:512],
                        wstrip[bi][:, kc * 512 + sub * 128: kc * 512 + sub * 128 + 128],
                        x_bf[:, kc * TC + th * 512: kc * TC + th * 512 + 512],
                        start=(kc == 0), stop=(kc == 15)), "pe")
                kb.wait("scalar", "pe", pv)
                psfree[pi] = kb.op("scalar", lambda e, lc=lc, th=th, pi=pi: e.activation(
                    x_lat[:, lc * TC + th * 512: lc * TC + th * 512 + 512],
                    pool[pi][:, 0:512], AF.Identity, bias=0.0, scale=1.0), "ac")
        wfree[bi] = pv
    xlat_done = kb.val("ac")

    kb.wait("tensor", "ac", xlat_done)
    kb.wait("vector", "dv", cw_done)
    hfree = 0
    yfree = [0, 0]
    for ee in range(E):
        bi = ee % 2
        kb.wait("gpsimd", "pe", wfree[bi])
        wld = kb.dma("gpsimd", wstrip[bi].ap().rearrange("p (c d) -> p c d", c=8),
                     w1T[ee].rearrange("(c p) d -> p c d", p=128), sem=f"w{bi}")
        kb.wait("tensor", f"w{bi}", wld)
        sqv = None
        sq_guard = 0
        for ic in range(8):
            for th in range(2):
                pi = th % 2
                kb.wait("tensor", "ac", psfree[pi])
                kb.wait("tensor", "dv", sq_guard)
                pv = None
                for lc in range(8):
                    pv = kb.op("tensor", lambda e, bi=bi, ic=ic, th=th, lc=lc, pi=pi: e.matmul(
                        pool[pi][:, 0:512],
                        wstrip[bi][:, lc * 1024 + ic * 128: lc * 1024 + ic * 128 + 128],
                        x_lat[:, lc * TC + th * 512: lc * TC + th * 512 + 512],
                        start=(lc == 0), stop=(lc == 7)), "pe")
                kb.wait("scalar", "pe", pv)
                psfree[pi] = kb.op("scalar", lambda e, ic=ic, th=th, pi=pi: e.activation(
                    hsb[:, ic * TC + th * 512: ic * TC + th * 512 + 512],
                    pool[pi][:, 0:512], AF.Relu, bias=0.0, scale=1.0), "ac")
            kb.wait("vector", "ac", psfree[1])
            sqv = kb.op("vector", lambda e, ic=ic: e.tensor_tensor(
                hsb[:, ic * TC:(ic + 1) * TC], hsb[:, ic * TC:(ic + 1) * TC],
                hsb[:, ic * TC:(ic + 1) * TC], op=OP.mult), "dv")
        bi2 = (ee + 1) % 2
        kb.wait("gpsimd", "pe", wfree[bi2])
        wld2 = kb.dma("gpsimd", wstrip[bi2].ap().rearrange("p (c d) -> p c d", c=8),
                      w2T[ee].rearrange("(c p) d -> p c d", p=128), sem=f"w{bi2}")
        kb.wait("tensor", f"w{bi2}", wld2)
        if dbg and ee == 31:
            kb.wait("sync", f"w{bi2}", wld2)
            w2v = kb.dma("sync", w2d, wstrip[bi2][:], sem="dbgw")
            kb.wait("gpsimd", "dbgw", w2v)
        kb.wait("tensor", "dv", sqv)
        pv = None
        for ts2 in range(8):
            for lh in range(2):
                pi = 4 + (ts2 * 2 + lh) % 2
                kb.wait("tensor", "dv", yfree[pi - 4])
                for ic in range(8):
                    pv = kb.op("tensor", lambda e, bi2=bi2, ts2=ts2, lh=lh, ic=ic, pi=pi: e.matmul(
                        pool[pi][:, 0:512],
                        hsb[:, ic * TC + ts2 * 128: ic * TC + ts2 * 128 + 128],
                        wstrip[bi2][:, ic * 1024 + lh * 512: ic * 1024 + lh * 512 + 512],
                        start=(ic == 0), stop=(ic == 7)), "pe")
                kb.wait("vector", "pe", pv)
                if dbg and ee == 0 and ts2 == 0 and lh == 0:
                    cpv = kb.op("vector", lambda e, pi=pi: e.tensor_copy(zs[0][:], pool[pi][:, 0:512]), "dv")
                    kb.wait("sync", "dv", cpv)
                    pdv = kb.dma("sync", psd, zs[0][:], sem="dbgp")
                    kb.wait("sync", "dbgp", pdv)
                    kb.wait("vector", "dbgp", pdv)
                yfree[pi - 4] = kb.op("vector", lambda e, ee=ee, ts2=ts2, lh=lh, pi=pi: e.scalar_tensor_tensor(
                    y_sb[:, ts2 * LAT + lh * 512: ts2 * LAT + lh * 512 + 512],
                    pool[pi][:, 0:512], cw_all[:, ts2 * E + ee: ts2 * E + ee + 1],
                    y_sb[:, ts2 * LAT + lh * 512: ts2 * LAT + lh * 512 + 512],
                    op0=OP.mult, op1=OP.add), "dv")
                if dbg and ee == 0 and ts2 == 0 and lh == 0:
                    kb.wait("sync", "dv", yfree[pi - 4])
                    y1v = kb.dma("sync", ysd1, y_sb[:, 0:512], sem="dbgy")
                    kb.wait("sync", "dbgy", y1v)
                    kb.wait("vector", "dbgy", y1v)
        wfree[bi2] = pv
        hfree = pv

    if dbg:
        cwd = kb.output("cwd", [128, 8 * E], F32)
        xld = kb.output("xld", [128, 8 * TC], BF16)
        ysd = kb.output("ysd", [128, 8 * TC], F32)
        gld2 = kb.output("gld2", [128, 8 * E], F32)
        kb.wait("sync", "dv", kb.val("dv"))
        kb.wait("sync", "ac", kb.val("ac"))
        hd = kb.output("hd", [128, 8 * TC], BF16)
        kb.dma("sync", hd, hsb[:], sem="dbg")
        kb.dma("sync", cwd, cw_all[:], sem="dbg")
        kb.dma("sync", xld, x_lat[:], sem="dbg")
        kb.dma("sync", ysd, y_sb[:], sem="dbg")
        dgv = kb.dma("sync", gld2, glog[:], sem="dbg")
        kb.wait("sync", "dbg", dgv)
        kb.wait("vector", "dbg", dgv)

    ydone = kb.val("dv")
    kb.wait("vector", "dv", ydone)
    kb.wait("vector", "pe", hfree)
    ptfree = 0
    ycast = 0
    for ts2 in range(8):
        dvv = kb.op("vector", lambda e, ts2=ts2: e.tensor_copy(
            x_lat[:, ts2 * 1024: ts2 * 1024 + 1024],
            y_sb[:, ts2 * LAT:(ts2 + 1) * LAT]), "dv")
        kb.wait("tensor", "dv", dvv)
        kb.wait("tensor", "ac", ptfree)
        pv = None
        for lc8 in range(8):
            pv = kb.op("tensor", lambda e, ts2=ts2, lc8=lc8: e.transpose(
                pt[:, lc8 * 128:(lc8 + 1) * 128],
                x_lat[:, ts2 * 1024 + lc8 * 128: ts2 * 1024 + lc8 * 128 + 128], idn[:]), "pe")
        kb.wait("scalar", "pe", pv)
        ptfree = kb.op("scalar", lambda e, ts2=ts2: e.activation(
            hsb.ap().rearrange("p (c t) -> p c t", c=8)[:, :, ts2 * 128:(ts2 + 1) * 128],
            pt.ap().rearrange("p (c t) -> p c t", c=8), AF.Identity,
            bias=0.0, scale=1.0), "ac")
    yfm_done = ptfree

    arS1 = kb.arena()
    s1 = arS1.sbuf_tensor("s1", [128, 32 * 512], BF16)
    fcs = [arS1.sbuf_tensor(f"fcs{i}", [128, 1024], BF16) for i in range(2)]
    arS1.allocate()
    kb.wait("tensor", "ac", yfm_done)
    statv = None
    zstv = [0, 0]
    xrfree = [0, 0]
    zsfree = [0, 0]
    lnx_cons = 0
    s1_cons = 0
    for th in range(2):
        kb.wait("vector", "ac", s1_cons)
        sq_last = None
        for blk8 in range(8):
            bi = blk8 % 2
            kb.wait("gpsimd", "pe", wfree[bi])
            wld = kb.dma("gpsimd", wstrip[bi].ap().rearrange("p (c d) -> p c d", c=16)[:, :, 0:512],
                         shupT[:, blk8 * 512:(blk8 + 1) * 512].rearrange("(c p) d -> p c d", p=128),
                         sem=f"w{bi}")
            kb.wait("tensor", f"w{bi}", wld)
            for sub in range(4):
                sc = blk8 * 4 + sub
                pi = sub % 2
                kb.wait("tensor", "ac", psfree[pi])
                pv = None
                for kc in range(16):
                    pv = kb.op("tensor", lambda e, bi=bi, sub=sub, th=th, kc=kc, pi=pi: e.matmul(
                        pool[pi][:, 0:512],
                        wstrip[bi][:, kc * 512 + sub * 128: kc * 512 + sub * 128 + 128],
                        x_bf[:, kc * TC + th * 512: kc * TC + th * 512 + 512],
                        start=(kc == 0), stop=(kc == 15)), "pe")
                kb.wait("scalar", "pe", pv)
                psfree[pi] = kb.op("scalar", lambda e, sc=sc, pi=pi: e.activation(
                    s1[:, sc * 512:(sc + 1) * 512], pool[pi][:, 0:512],
                    AF.Relu, bias=0.0, scale=1.0), "ac")
            wfree[bi] = pv
        rlv = kb.val("ac")
        kb.wait("vector", "ac", rlv)
        for sc in range(32):
            sq_last = kb.op("vector", lambda e, sc=sc: e.tensor_tensor(
                s1[:, sc * 512:(sc + 1) * 512], s1[:, sc * 512:(sc + 1) * 512],
                s1[:, sc * 512:(sc + 1) * 512], op=OP.mult), "dv")
        kb.wait("tensor", "dv", sq_last)
        if dbg and th == 0:
            kb.wait("sync", "dv", kb.val("dv"))
            sdv = kb.dma("sync", s1d, s1[:], sem="dbgs")
            kb.wait("sync", "dbgs", sdv)
            kb.wait("vector", "dbgs", sdv)
            kb.wait("scalar", "dbgs", sdv)
        for dc in range(16):
            bi = dc % 2
            kb.wait("gpsimd", "pe", wfree[bi])
            wld = kb.dma("gpsimd",
                         wstrip[bi][:, 0:4096].rearrange("p (c d) -> p c d", c=32),
                         shdnT[:, dc * 128:(dc + 1) * 128].rearrange("(c p) d -> p c d", p=128),
                         sem=f"w{bi}")
            kb.wait("gpsimd", "ac", kb.val("ac") if th == 0 and dc < 2 else 0)
            fv = kb.dma("gpsimd", fcs[bi].ap().rearrange("p (c d) -> p c d", c=8),
                        fc2T[:, dc * 128:(dc + 1) * 128].rearrange("(c p) d -> p c d", p=128),
                        sem=f"fc{bi}")
            kb.wait("tensor", f"w{bi}", wld)
            kb.wait("tensor", f"fc{bi}", fv)
            kb.wait("tensor", "dv", zsfree[0 if dc % 2 == 0 else 1])
            pvA = None
            for lc in range(8):
                pvA = kb.op("tensor", lambda e, bi=bi, lc=lc, th=th: e.matmul(
                    pool[4][:, 0:512], fcs[bi][:, lc * 128:(lc + 1) * 128],
                    hsb[:, lc * TC + th * 512: lc * TC + th * 512 + 512],
                    start=(lc == 0), stop=(lc == 7)), "pe")
            pvB = None
            for sc in range(32):
                pvB = kb.op("tensor", lambda e, bi=bi, sc=sc, th=th: e.matmul(
                    pool[5][:, 0:512], wstrip[bi][:, sc * 128:(sc + 1) * 128],
                    s1[:, sc * 512:(sc + 1) * 512],
                    start=(sc == 0), stop=(sc == 31)), "pe")
            wfree[bi] = pvB
            if th == 1 and dc >= 14:
                s1_cons = 0
            xbi = dc % 2
            kb.wait("sync", "dv", xrfree[xbi])
            xrv = kb.dma("sync", xr[xbi][:],
                         x[dc * 128:(dc + 1) * 128, th * 512:(th + 1) * 512], sem=f"xr{xbi}")
            kb.wait("vector", "pe", pvB)
            kb.wait("vector", f"xr{xbi}", xrv)
            zi = dc % 2
            kb.wait("vector", f"zst{zi}", zstv[zi])
            kb.wait("vector", "pe", lnx_cons)
            kb.op("vector", lambda e, zi=zi, xbi=xbi: e.scalar_tensor_tensor(
                zs[zi][:], pool[4][:, 0:512], 1.0, xr[xbi][:], op0=OP.mult, op1=OP.add), "dv")
            zv = kb.op("vector", lambda e, zi=zi: e.tensor_tensor(
                zs[zi][:], zs[zi][:], pool[5][:, 0:512], op=OP.add), "dv")
            zsfree[zi] = zv
            xrfree[xbi] = zv
            if dbg and th == 0 and dc == 0:
                cpv = kb.op("vector", lambda e: e.tensor_copy(zbf[:], pool[4][:, 0:512]), "dv")
                kb.wait("sync", "dv", cpv)
                kb.dma("sync", pad, zbf.ap().bitcast(F32) if False else zs[1][:], sem="dbgt") if False else None
                cpv2 = kb.op("vector", lambda e: e.tensor_copy(zs[1][:], pool[4][:, 0:512]), "dv")
                kb.wait("sync", "dv", cpv2)
                pdv2 = kb.dma("sync", pad, zs[1][:], sem="dbgt")
                kb.wait("sync", "dbgt", pdv2)
                kb.wait("vector", "dbgt", pdv2)
                kb.wait("sync", "dv", zv)
                zdv = kb.dma("sync", zd0, zs[zi][:], sem="dbgt")
                kb.wait("sync", "dbgt", zdv)
                kb.wait("vector", "dbgt", zdv)
            zbv = kb.op("vector", lambda e, zi=zi: e.tensor_copy(zbf[:], zs[zi][:]), "dv")
            kb.wait("scalar", "dv", zbv)
            sqv = kb.op("scalar", lambda e: e.square(zsq[:], zbf[:]), "ac")
            kb.wait("tensor", "dv", zbv)
            kb.wait("tensor", "ac", sqv)
            pse_s = pool[0] if th == 0 else pool[1]
            pse_q = pool[2] if th == 0 else pool[3]
            kb.op("tensor", lambda e, dc=dc, pse_s=pse_s: e.matmul(
                pse_s[0:1, 0:512], onescol[:], zbf[:],
                start=(dc == 0), stop=(dc == 15)), "pe")
            statv = kb.op("tensor", lambda e, dc=dc, pse_q=pse_q: e.matmul(
                pse_q[0:1, 0:512], onescol[:], zsq[:],
                start=(dc == 0), stop=(dc == 15)), "pe")
            lnx_cons = statv
            kb.wait("sync", "dv", zv)
            zstv[zi] = kb.dma("sync",
                              z_dram[dc * 128:(dc + 1) * 128, th * 512:(th + 1) * 512],
                              zs[zi][:], sem=f"zst{zi}")

    kb.barrier()
    arS1.free()
    arZ = kb.arena()
    zc = [arZ.sbuf_tensor("zc0", [128, TC], F32)] * 2
    xo = [arZ.sbuf_tensor("xo0", [128, TC], F32)] * 2
    arZ.allocate()
    mean = lambda: ln_rows[0:1, 0:TC]
    s2 = lambda: ln_rows[0:1, TC:2 * TC]
    kb.wait("vector", "pe", statv)
    for th in range(2):
        kb.op("vector", lambda e, th=th: e.tensor_scalar(
            mean()[0:1, th * 512:th * 512 + 512], pool[th][0:1, 0:512],
            1.0 / D, None, op0=OP.mult), "dv")
        kb.op("vector", lambda e, th=th: e.tensor_scalar(
            s2()[0:1, th * 512:th * 512 + 512], pool[2 + th][0:1, 0:512],
            1.0 / D, None, op0=OP.mult), "dv")
    kb.op("vector", lambda e: e.tensor_tensor(ln_b[0:1, :], mean(), mean(), op=OP.mult), "dv")
    dvv = kb.op("vector", lambda e: e.tensor_tensor(s2(), s2(), ln_b[0:1, :],
                                                    op=OP.subtract), "dv")
    kb.wait("scalar", "dv", dvv)
    acv = kb.op("scalar", lambda e: e.activation(s2(), s2(), AF.Sqrt,
                                                 bias=cb[0:1, 0:1], scale=1.0), "ac")
    kb.wait("vector", "ac", acv)
    kb.op("vector", lambda e: e.reciprocal(s2(), s2()), "dv")
    kb.op("vector", lambda e: e.tensor_tensor(ln_b[0:1, :], mean(), s2(), op=OP.mult), "dv")
    kb.op("vector", lambda e: e.tensor_scalar(ln_b[0:1, :], ln_b[0:1, :], -1.0, None,
                                              op0=OP.mult), "dv")
    kb.op("vector", lambda e: e.tensor_copy(ln_a_bf[:], s2()), "dv")
    dvv = kb.op("vector", lambda e: e.tensor_copy(ln_b_bf[:], ln_b[0:1, :]), "dv")
    kb.wait("tensor", "dv", dvv)
    pev = None
    for hf in range(2):
        kb.op("tensor", lambda e, hf=hf: e.matmul(
            pool[hf][:, 0:512], onesrow[0:1, 0:128],
            ln_a_bf[0:1, hf * 512:hf * 512 + 512], start=True, stop=True), "pe")
        pev = kb.op("tensor", lambda e, hf=hf: e.matmul(
            pool[2 + hf][:, 0:512], onesrow[0:1, 0:128],
            ln_b_bf[0:1, hf * 512:hf * 512 + 512], start=True, stop=True), "pe")
    kb.wait("vector", "pe", pev)
    zld = [0, 0]
    xov = [0, 0]
    kb.wait("sync", "zst0", kb.val("zst0"))
    kb.wait("sync", "zst1", kb.val("zst1"))
    for ch in range(16):
        bi = 0
        kb.wait("sync", "dv", zld[bi])
        zv = kb.dma("sync", zc[bi][:], z_dram[ch * 128:(ch + 1) * 128, :], sem=f"zld{bi}")
        kb.wait("vector", f"zld{bi}", zv)
        kb.wait("vector", f"xo{bi}", xov[bi])
        dvv = None
        for hf in range(2):
            kb.op("vector", lambda e, bi=bi, hf=hf: e.tensor_tensor(
                xo[bi][:, hf * 512:hf * 512 + 512], zc[bi][:, hf * 512:hf * 512 + 512],
                pool[hf][:, 0:512], op=OP.mult), "dv")
            dvv = kb.op("vector", lambda e, bi=bi, hf=hf: e.tensor_tensor(
                xo[bi][:, hf * 512:hf * 512 + 512], xo[bi][:, hf * 512:hf * 512 + 512],
                pool[2 + hf][:, 0:512], op=OP.add), "dv")
        zld[bi] = dvv
        kb.wait("sync", "dv", dvv)
        xov[bi] = kb.dma("sync", x2[ch * 128:(ch + 1) * 128, :], xo[bi][:], sem=f"xo{bi}")
    kb.wait("sync", "xo0", xov[0])
    if standalone:
        return kb.finalize()



def _bf(a):
    a = np.asarray(a)
    return a.view(BF) if a.dtype == np.uint16 else a


def kernel_unfused(L0_attn_x, L0_attn_w_qkv, L0_attn_w_out, L0_ffn_gate_w, L0_ffn_gate_bias,
           L0_ffn_fc1_w, L0_ffn_fc2_w, L0_ffn_w1, L0_ffn_w2, L0_ffn_shared_up_w,
           L0_ffn_shared_down_w, L1_attn_Wq, L1_attn_bq, L1_attn_Wk, L1_attn_bk,
           L1_attn_Wv, L1_attn_bv, L1_ffn_w_up_q, L1_ffn_w_up_scale, L1_ffn_b_up,
           L1_ffn_w_up_lora_a, L1_ffn_w_up_lora_b, L1_ffn_w_down_q, L1_ffn_w_down_scale,
           L1_ffn_b_down, L1_ffn_w_down_lora_a, L1_ffn_w_down_lora_b):
    f32 = np.float32
    X = np.asarray(L0_attn_x, f32).reshape(S, D)
    Xfm = np.ascontiguousarray(X.T)
    w_qkv = np.asarray(L0_attn_w_qkv, f32)
    w_out = np.asarray(L0_attn_w_out, f32)
    cores = list(range(NC))

    maps = []
    for c in cores:
        maps.append({
            "x": Xfm,
            "xr": np.ascontiguousarray(Xfm[:, c * TC:(c + 1) * TC]),
            "wqT": np.ascontiguousarray(w_qkv[256 * c:256 * c + 256, :].T),
            "wkT": np.ascontiguousarray(w_qkv[D + 256 * c:D + 256 * c + 256, :].T),
            "wvT": np.ascontiguousarray(w_qkv[2 * D + 256 * c:2 * D + 256 * c + 256, :].T),
            "woT": np.ascontiguousarray(w_out[:, 256 * c:256 * c + 256].T),
            "hb2": np.array([[2.0 * c + 1], [2.0 * c + 2]], f32),
        })
    r = run_bass_kernel_spmd(build_l0_attn(), maps, core_ids=cores)
    x1 = [np.asarray(r.results[c]["x1"], f32) for c in cores]

    common = {
        "gateT": np.ascontiguousarray(np.asarray(L0_ffn_gate_w, f32).T),
        "gbias": np.asarray(L0_ffn_gate_bias, f32).reshape(1, 32),
        "fc1T": np.ascontiguousarray(np.asarray(L0_ffn_fc1_w, f32).T),
        "w1T": np.ascontiguousarray(np.asarray(L0_ffn_w1, f32).transpose(0, 2, 1)),
        "w2T": np.ascontiguousarray(np.asarray(L0_ffn_w2, f32).transpose(0, 2, 1)),
        "fc2T": np.ascontiguousarray(np.asarray(L0_ffn_fc2_w, f32).T),
        "shupT": np.ascontiguousarray(np.asarray(L0_ffn_shared_up_w, f32).T),
        "shdnT": np.ascontiguousarray(np.asarray(L0_ffn_shared_down_w, f32).T),
    }
    maps = [dict(common, x=x1[c]) for c in cores]
    r = run_bass_kernel_spmd(build_l0_ffn(), maps, core_ids=cores)
    x2 = [np.asarray(r.results[c]["x2"], f32) for c in cores]

    common = {
        "Wq": np.asarray(L1_attn_Wq, f32), "Wk": np.asarray(L1_attn_Wk, f32),
        "Wv": np.asarray(L1_attn_Wv, f32),
        "bqT": np.ascontiguousarray(np.asarray(L1_attn_bq, f32).reshape(16, 128).T),
        "bkT": np.ascontiguousarray(np.asarray(L1_attn_bk, f32).reshape(16, 128).T),
        "bv": np.asarray(L1_attn_bv, f32).reshape(1, D),
    }
    maps = [dict(common, x=x2[c]) for c in cores]
    r = run_bass_kernel_spmd(build_l1_qkv(), maps, core_ids=cores)
    q1 = [_bf(r.results[c]["q1"]) for c in cores]
    Kg = np.concatenate([_bf(r.results[c]["k1"]) for c in cores], 0)
    Vg = np.concatenate([_bf(r.results[c]["v1p"]) for c in cores], 0)

    maps = []
    for c in cores:
        maps.append({
            "x": x2[c], "q1": np.ascontiguousarray(q1[c]), "Kg": Kg, "Vg": Vg,
            "qbase": np.array([[TC * c]], f32),
        })
    r = run_bass_kernel_spmd(build_l1_attn(), maps, core_ids=cores)
    x3 = [np.asarray(r.results[c]["x3"], f32) for c in cores]

    common = {
        "upq": np.ascontiguousarray(np.asarray(L1_ffn_w_up_q, np.int32).T),
        "ups": np.ascontiguousarray(np.repeat(np.asarray(L1_ffn_w_up_scale, f32), 64, 1).T),
        "b_up": np.asarray(L1_ffn_b_up, f32).reshape(1, 8192),
        "upA": np.asarray(L1_ffn_w_up_lora_a, f32),
        "upB": np.asarray(L1_ffn_w_up_lora_b, f32),
        "dnq": np.ascontiguousarray(np.asarray(L1_ffn_w_down_q, np.int32).T),
        "dns": np.ascontiguousarray(np.repeat(np.asarray(L1_ffn_w_down_scale, f32), 64, 1).T),
        "b_dn": np.asarray(L1_ffn_b_down, f32).reshape(1, D),
        "dnA": np.asarray(L1_ffn_w_down_lora_a, f32),
        "dnB": np.asarray(L1_ffn_w_down_lora_b, f32),
    }
    maps = [dict(common, x=x3[c]) for c in cores]
    r = run_bass_kernel_spmd(build_l1_ffn(), maps, core_ids=cores)
    x4 = [np.asarray(r.results[c]["x4"], f32) for c in cores]

    out = np.concatenate([x4[c].T for c in cores], 0).reshape(1, S, D)
    return out



def build_fused():
    kb = KB()
    nc = kb.nc
    E, H = 32, 8192
    EI = lambda n, s, d: nc.dram_tensor(n, s, d, kind="ExternalInput").ap()
    xfm = EI("xfm", [D, S], F32)
    xr = EI("xr", [D, TC], F32)
    wqT = EI("wqT", [D, 256], F32)
    wkT = EI("wkT", [D, 256], F32)
    wvT = EI("wvT", [D, 256], F32)
    woT = EI("woT", [256, D], F32)
    hb2 = EI("hb2", [2, 1], F32)
    gateT = EI("gateT", [D, E], F32)
    gbias = EI("gbias", [1, E], F32)
    fc1T = EI("fc1T", [D, 1024], F32)
    w1T = EI("w1T", [E, 1024, 1024], F32)
    w2T = EI("w2T", [E, 1024, 1024], F32)
    fc2T = EI("fc2T", [1024, D], F32)
    shupT = EI("shupT", [D, 4096], F32)
    shdnT = EI("shdnT", [4096, D], F32)
    Wq = EI("Wq", [D, D], F32)
    Wk = EI("Wk", [D, D], F32)
    Wv = EI("Wv", [D, D], F32)
    bqT = EI("bqT", [128, 16], F32)
    bkT = EI("bkT", [128, 16], F32)
    bv = EI("bv", [1, D], F32)
    qbase = EI("qbase", [1, 1], F32)
    upq = EI("upq", [D, H], I32)
    ups = EI("ups", [D, H], F32)
    b_up = EI("b_up", [1, H], F32)
    upA = EI("upA", [D, 16], F32)
    upB = EI("upB", [16, H], F32)
    dnq = EI("dnq", [H, D], I32)
    dns = EI("dns", [H, D], F32)
    b_dn = EI("b_dn", [1, D], F32)
    dnA = EI("dnA", [H, 16], F32)
    dnB = EI("dnB", [16, D], F32)
    y = nc.dram_tensor("y", [D, TC], F32, kind="ExternalOutput").ap()

    x1_i = kb.dram("x1_i", [D, TC], F32)
    x2_i = kb.dram("x2_i", [D, TC], F32)
    x3_i = kb.dram("x3_i", [D, TC], F32)
    q1_i = kb.dram("q1_i", [D, TC], BF16)
    k1_i = kb.dram("k1_i", [D, TC], BF16)
    v1p_i = kb.dram("v1p_i", [TC, D + 1], BF16)
    Kg_i = kb.dram("Kg_i", [NC * D, TC], BF16)
    Vg_i = kb.dram("Vg_i", [S, D + 1], BF16)

    kb.begin_phase("pa_", {"x": xfm, "xr": xr, "wqT": wqT, "wkT": wkT, "wvT": wvT,
                           "woT": woT, "hb2": hb2, "x1": x1_i})
    build_l0_attn(kb)
    kb.end_phase()

    kb.begin_phase("pb_", {"x": x1_i, "gateT": gateT, "gbias": gbias, "fc1T": fc1T,
                           "w1T": w1T, "w2T": w2T, "fc2T": fc2T, "shupT": shupT,
                           "shdnT": shdnT, "x2": x2_i})
    build_l0_ffn(kb)
    kb.end_phase()

    kb.begin_phase("pc_", {"x": x2_i, "Wq": Wq, "Wk": Wk, "Wv": Wv, "bqT": bqT,
                           "bkT": bkT, "bv": bv, "q1": q1_i, "k1": k1_i, "v1p": v1p_i})
    build_l1_qkv(kb)
    kb.wait("gpsimd", "stk", kb.val("stk"))
    kb.op("gpsimd", lambda e: e.collective_compute(
        "AllGather", mybir.AluOpType.bypass, replica_groups=[list(range(NC))],
        ins=[k1_i.opt()], outs=[Kg_i.opt()]), "cc", amt=1)
    kb.wait("gpsimd", "stv", kb.val("stv"))
    kb.op("gpsimd", lambda e: e.collective_compute(
        "AllGather", mybir.AluOpType.bypass, replica_groups=[list(range(NC))],
        ins=[v1p_i.opt()], outs=[Vg_i.opt()]), "cc", amt=1)
    kb.end_phase()

    kb.begin_phase("pd_", {"x": x2_i, "q1": q1_i, "Kg": Kg_i, "Vg": Vg_i,
                           "qbase": qbase, "x3": x3_i})
    build_l1_attn(kb)
    kb.end_phase()

    kb.begin_phase("pe_", {"x": x3_i, "upq": upq, "ups": ups, "b_up": b_up,
                           "upA": upA, "upB": upB, "dnq": dnq, "dns": dns,
                           "b_dn": b_dn, "dnA": dnA, "dnB": dnB, "x4": y})
    build_l1_ffn(kb)
    kb.end_phase()
    return kb.finalize()


def kernel_fused(L0_attn_x, L0_attn_w_qkv, L0_attn_w_out, L0_ffn_gate_w,
                 L0_ffn_gate_bias, L0_ffn_fc1_w, L0_ffn_fc2_w, L0_ffn_w1, L0_ffn_w2,
                 L0_ffn_shared_up_w, L0_ffn_shared_down_w, L1_attn_Wq, L1_attn_bq,
                 L1_attn_Wk, L1_attn_bk, L1_attn_Wv, L1_attn_bv, L1_ffn_w_up_q,
                 L1_ffn_w_up_scale, L1_ffn_b_up, L1_ffn_w_up_lora_a, L1_ffn_w_up_lora_b,
                 L1_ffn_w_down_q, L1_ffn_w_down_scale, L1_ffn_b_down,
                 L1_ffn_w_down_lora_a, L1_ffn_w_down_lora_b):
    f32 = np.float32
    X = np.asarray(L0_attn_x, f32).reshape(S, D)
    Xfm = np.ascontiguousarray(X.T)
    w_qkv = np.asarray(L0_attn_w_qkv, f32)
    w_out = np.asarray(L0_attn_w_out, f32)
    cores = list(range(NC))
    common = {
        "xfm": Xfm,
        "gateT": np.ascontiguousarray(np.asarray(L0_ffn_gate_w, f32).T),
        "gbias": np.asarray(L0_ffn_gate_bias, f32).reshape(1, 32),
        "fc1T": np.ascontiguousarray(np.asarray(L0_ffn_fc1_w, f32).T),
        "w1T": np.ascontiguousarray(np.asarray(L0_ffn_w1, f32).transpose(0, 2, 1)),
        "w2T": np.ascontiguousarray(np.asarray(L0_ffn_w2, f32).transpose(0, 2, 1)),
        "fc2T": np.ascontiguousarray(np.asarray(L0_ffn_fc2_w, f32).T),
        "shupT": np.ascontiguousarray(np.asarray(L0_ffn_shared_up_w, f32).T),
        "shdnT": np.ascontiguousarray(np.asarray(L0_ffn_shared_down_w, f32).T),
        "Wq": np.asarray(L1_attn_Wq, f32), "Wk": np.asarray(L1_attn_Wk, f32),
        "Wv": np.asarray(L1_attn_Wv, f32),
        "bqT": np.ascontiguousarray(np.asarray(L1_attn_bq, f32).reshape(16, 128).T),
        "bkT": np.ascontiguousarray(np.asarray(L1_attn_bk, f32).reshape(16, 128).T),
        "bv": np.asarray(L1_attn_bv, f32).reshape(1, D),
        "upq": np.ascontiguousarray(np.asarray(L1_ffn_w_up_q, np.int32).T),
        "ups": np.ascontiguousarray(np.repeat(np.asarray(L1_ffn_w_up_scale, f32), 64, 1).T),
        "b_up": np.asarray(L1_ffn_b_up, f32).reshape(1, 8192),
        "upA": np.asarray(L1_ffn_w_up_lora_a, f32),
        "upB": np.asarray(L1_ffn_w_up_lora_b, f32),
        "dnq": np.ascontiguousarray(np.asarray(L1_ffn_w_down_q, np.int32).T),
        "dns": np.ascontiguousarray(np.repeat(np.asarray(L1_ffn_w_down_scale, f32), 64, 1).T),
        "b_dn": np.asarray(L1_ffn_b_down, f32).reshape(1, D),
        "dnA": np.asarray(L1_ffn_w_down_lora_a, f32),
        "dnB": np.asarray(L1_ffn_w_down_lora_b, f32),
    }
    maps = []
    for c in cores:
        maps.append(dict(
            common,
            xr=np.ascontiguousarray(Xfm[:, c * TC:(c + 1) * TC]),
            wqT=np.ascontiguousarray(w_qkv[256 * c:256 * c + 256, :].T),
            wkT=np.ascontiguousarray(w_qkv[D + 256 * c:D + 256 * c + 256, :].T),
            wvT=np.ascontiguousarray(w_qkv[2 * D + 256 * c:2 * D + 256 * c + 256, :].T),
            woT=np.ascontiguousarray(w_out[:, 256 * c:256 * c + 256].T),
            hb2=np.array([[2.0 * c + 1], [2.0 * c + 2]], f32),
            qbase=np.array([[TC * c]], f32),
        ))
    r = run_bass_kernel_spmd(build_fused(), maps, core_ids=cores)
    out = np.concatenate([np.asarray(r.results[c]["y"], f32).T for c in cores], 0)
    return out.reshape(1, S, D)


kernel = kernel_fused
